# version 16
# baseline (speedup 1.0000x reference)
import numpy as np
import ml_dtypes
import concourse.bass as bass
import concourse.mybir as mybir
from concourse.bass_utils import run_bass_kernel_spmd
from contextlib import ExitStack

F32 = mybir.dt.float32
BF16 = mybir.dt.bfloat16
AF = mybir.ActivationFunctionType
ALU = mybir.AluOpType
AX = mybir.AxisListType.X
BF = ml_dtypes.bfloat16

B_, SQ, D = 4, 4096, 1024
OWN, HALO = 2048, 3
EXT = OWN + 2 * HALO
NTT = 17
WINDOWS = [(0, 512), (512, 512), (1024, 512), (1536, 512), (2048, 6)]
OWNW = [(3, 512), (515, 512), (1027, 512), (1539, 512)]
ENGINES = ("sync", "tensor", "vector", "scalar", "gpsimd")


def twidth(t):
    return 6 if t == 16 else 128


QPOOL = {"sync": 6, "gpsimd": 6, "scalar": 4, "vector": 2, "tensor": 2}


class Prog:
    def __init__(self, nc):
        self.nc = nc
        self.recs = {e: [] for e in ENGINES}
        self.cnt = {}
        self.sem_names = []
        self.qpool = {}
        self.qrr = {e: 0 for e in ENGINES}
        for e in ENGINES:
            self.cnt[e] = 0
            self.sem_names.append(e)
            self.qpool[e] = []
            for i in range(QPOOL[e]):
                n = f"q_{e}{i}"
                self.cnt[n] = 0
                self.sem_names.append(n)
                self.qpool[e].append(n)

    def op(self, eng, fn, tok=False):
        self.recs[eng].append(["op", fn, []])
        if tok:
            return self.inc(eng)
        return None

    def inc(self, eng, sem=None, n=1):
        name = sem or eng
        assert self.recs[eng] and self.recs[eng][-1][0] == "op"
        self.cnt[name] += n
        self.recs[eng][-1][2].append((name, n))
        return (name, self.cnt[name])

    def tok(self, eng):
        return (eng, self.cnt[eng])

    def wait(self, eng, tok):
        if tok is None:
            return
        if isinstance(tok, list):
            for t in tok:
                self.wait(eng, t)
            return
        if tok[1] <= 0:
            return
        self.recs[eng].append(["wait", tok[0], tok[1]])

    def dma(self, eng, out, in_, **kw):
        def fn(E, out=out, in_=in_, kw=kw):
            return E.dma_start(out=out, in_=in_, **kw)
        pool = self.qpool[eng]
        name = pool[self.qrr[eng] % len(pool)]
        self.qrr[eng] += 1
        if self.cnt[name] > 0:
            self.recs[eng].append(["wait", name, self.cnt[name]])
        self.recs[eng].append(["op", fn, []])
        return self.inc(eng, name, 16)

    def qtok(self, eng):
        return [(n, self.cnt[n]) for n in self.qpool[eng] if self.cnt[n] > 0]

    def wait_q_all(self, eng, queue_eng):
        for t in self.qtok(queue_eng):
            self.wait(eng, t)

    def emit(self, es):
        nc = self.nc
        sems = {}
        for name in self.sem_names:
            if self.cnt[name] > 0:
                sems[name] = es.enter_context(nc.semaphore(name))
        block = es.enter_context(nc.Block())
        for eng in ENGINES:
            recs = self.recs[eng]
            if not recs:
                continue

            def section(E, recs=recs, sems=sems):
                seen = {}
                for r in recs:
                    if r[0] == "wait":
                        if seen.get(r[1], -1) >= r[2]:
                            continue
                        seen[r[1]] = r[2]
                        E.wait_ge(sems[r[1]], r[2])
                    else:
                        ins = r[1](E)
                        for (name, n) in r[2]:
                            ins.then_inc(sems[name], n)
            getattr(block, eng)(section)



def host_constants():
    c = {}
    c["eye_f"] = np.eye(128, dtype=np.float32)
    c["eye_b"] = np.eye(128, dtype=BF)
    c["ones_lhs"] = np.ones((1, 128), dtype=BF)
    c["ones_rhs"] = np.ones((128, 1), dtype=BF)
    c["ones_row"] = np.ones((1, 512), dtype=BF)
    p1a = np.zeros((16, 128), dtype=np.float32)
    p1b = np.zeros((16, 128), dtype=np.float32)
    p2 = np.zeros((16, 128), dtype=np.float32)
    s = 2.0 ** -0.5
    for m in range(128):
        p1a[m // 16, m] = s
        p1b[(m + 128) // 16, m] = s
        p2[m % 16, m] = 1.0
    c["P1a"] = p1a.astype(BF)
    c["P1b"] = p1b.astype(BF)
    c["P2"] = p2.astype(BF)
    bA = np.zeros((128, 128), dtype=np.float32)
    for j in range(128):
        bA[j, max(0, j - 6):j + 1] = 1.0
    bB = np.zeros((6, 128), dtype=np.float32)
    for j in range(6):
        bB[j, 122 + j:128] = 1.0
    c["bandA"] = bA.astype(BF)
    c["bandB"] = bB.astype(BF)
    sel = np.zeros((4, 4 * 128), dtype=np.float32)
    for r in range(4):
        sel[r, r * 128:(r + 1) * 128] = 1.0
    c["sel4"] = sel.astype(BF)
    c["cvec"] = np.tile(np.array([[1e-5, 1e-8, 1.0, 0.0]], dtype=np.float32), (128, 1))
    sel8 = np.zeros((8, 8 * 128), dtype=np.float32)
    for r in range(8):
        sel8[r, r * 128:(r + 1) * 128] = 1.0
    c["sel8"] = sel8.astype(BF)
    return c


def host_weights(inp):
    w = {}
    f32 = np.float32
    A = np.ascontiguousarray
    w["wq_r"] = A(inp["L0_attn_w_q"].reshape(8, 128, 128).transpose(1, 0, 2))
    w["wkv_r"] = A(inp["L0_attn_w_kv"].reshape(8, 128, 256).transpose(1, 0, 2))
    w["wout0"] = A(inp["L0_attn_w_out"]).astype(f32)
    w["gamma_r"] = A(np.asarray(inp["L0_attn_gamma"]).reshape(8, 128).T)
    w["win_r"] = A(inp["L0_ffn_Win"].reshape(8, 128, 64).transpose(1, 0, 2))
    W1 = np.asarray(inp["L0_ffn_W1"])
    Bm = np.zeros((64, 1024), dtype=f32)
    for m in range(64):
        Bm[m, m * 16:(m + 1) * 16] = W1[m]
    w["Bmat"] = Bm
    W2 = np.asarray(inp["L0_ffn_W2"])
    Cm = np.zeros((1024, 64), dtype=f32)
    for m in range(64):
        Cm[m * 16:(m + 1) * 16, m] = W2[m]
    w["Cmat_r"] = A(Cm.reshape(8, 128, 64).transpose(1, 0, 2))
    w["woutf"] = A(inp["L0_ffn_Wout"]).astype(f32)
    w["bin_c"] = A(inp["L0_ffn_bin_"].reshape(64, 1))
    w["b1flat_r"] = A(inp["L0_ffn_b1"].reshape(1024).reshape(8, 128).T)
    w["b2f_c"] = A(inp["L0_ffn_b2"].reshape(64, 1))
    w["bout_r"] = A(inp["L0_ffn_bout"].reshape(8, 128).T)
    wqkv = np.asarray(inp["L1_attn_w_qkv"])
    w["wqk_r"] = A(wqkv[:, 0:2048].reshape(8, 128, 2048).transpose(1, 0, 2))
    w["wv_r"] = A(wqkv[:, 2048:3072].reshape(8, 128, 1024).transpose(1, 0, 2))
    w["proj"] = A(inp["L1_attn_proj"]).astype(f32)
    w["wout1_r"] = A(inp["L1_attn_w_out"].reshape(8, 128, 1024).transpose(1, 0, 2))
    w["bout1_r"] = A(inp["L1_attn_b_out"].reshape(8, 128).T)
    wc = np.concatenate([np.asarray(inp["L1_ffn_router_w"]).T,
                         np.asarray(inp["L1_ffn_lang_w"]).T], axis=1)
    w["wcomb_r"] = A(wc.reshape(8, 128, 12).transpose(1, 0, 2))
    w["rbias_c"] = A(np.concatenate([np.asarray(inp["L1_ffn_router_b"]),
                                     np.asarray(inp["L1_ffn_lang_b"])]).reshape(12, 1))
    w["w1"] = np.asarray(inp["L1_ffn_w1"])
    w["w2"] = np.asarray(inp["L1_ffn_w2"])
    w["b1_r"] = A(np.asarray(inp["L1_ffn_b1"]).reshape(8, 32, 128).transpose(0, 2, 1))
    w["b2m"] = np.asarray(inp["L1_ffn_b2"])
    return w


WDEFS = [
    ("wq_r", [128, 8, 128]), ("wkv_r", [128, 8, 256]), ("wout0", [128, 1024]),
    ("gamma_r", [128, 8]), ("win_r", [128, 8, 64]), ("Bmat", [64, 1024]),
    ("Cmat_r", [128, 8, 64]), ("woutf", [64, 1024]), ("bin_c", [64, 1]),
    ("b1flat_r", [128, 8]), ("b2f_c", [64, 1]), ("bout_r", [128, 8]),
    ("wqk_r", [128, 8, 2048]), ("wv_r", [128, 8, 1024]), ("proj", [128, 64]),
    ("wout1_r", [128, 8, 1024]), ("bout1_r", [128, 8]), ("wcomb_r", [128, 8, 12]),
    ("rbias_c", [12, 1]), ("w1", [8, 1024, 4096]), ("w2", [8, 4096, 1024]),
    ("b1_r", [8, 128, 32]), ("b2m", [8, 1024]),
]


def host_core_slices(x, core):
    b, s = core // 2, core % 2
    base = s * OWN
    idx = np.clip(np.arange(base - HALO, base + OWN + HALO), 0, SQ - 1)
    x_ext = np.ascontiguousarray(np.asarray(x[b])[idx])
    x_part = np.ascontiguousarray(np.asarray(x[b])[(1 - s) * OWN:(2 - s) * OWN])
    left_edge = 0.0 if s == 0 else 1.0
    right_edge = 0.0 if s == 1 else 1.0
    flags = np.tile(np.array([[left_edge, right_edge]], dtype=np.float32), (128, 1))
    return x_ext, x_part, flags



class Ctx:
    pass


def make_ctx(phases=("b0", "b1", "moe"), debug=()):
    nc = bass.Bass("TRN2", target_bir_lowering=False)
    C = Ctx()
    C.nc = nc
    C.P = Prog(nc)
    C.es = ExitStack()
    C.phases = phases
    C.debug = list(debug)

    C.t_conv = [None] * 8
    dram = lambda n, s, d, k: nc.dram_tensor(n, s, d, kind=k).ap()
    C.in_x = dram("x_ext", [EXT, D], F32, "ExternalInput")
    C.in_xp = dram("x_part", [OWN, D], F32, "ExternalInput")
    C.in_flags = dram("d_flags", [128, 2], F32, "ExternalInput")
    C.dc = {}
    for n, v in host_constants().items():
        dt = F32 if v.dtype == np.float32 else BF16
        C.dc[n] = dram("d_" + n, list(v.shape), dt, "ExternalInput")
    C.wd = {n: dram("d_" + n, s, F32, "ExternalInput") for n, s in WDEFS}
    C.out_y = dram("y", [D, OWN], F32, "ExternalOutput")
    C.stream_dram = dram("sdram", [128, 8 * EXT], F32, "Internal")
    C.w2bf = nc.dram_tensor("w2bf", [8, 8, 128, 4096], BF16, kind="Internal").ap()
    C.w1bf = nc.dram_tensor("w1bf", [8, 8, 128, 4096], BF16, kind="Internal").ap()
    C.dbg = {}
    for name, shape in debug:
        C.dbg[name] = dram("dbg_" + name, list(shape), F32, "ExternalOutput")

    sb = lambda n, s, d: C.es.enter_context(nc.sbuf_tensor(n, s, d))
    psa = lambda n, s, d: C.es.enter_context(nc.psum_tensor(n, s, d))

    C.pA = psa("pA", [128, 1024], F32)
    C.pB = psa("pB", [128, 1024], F32)
    C.pC = psa("pC", [128, 512], F32)
    C.pD = psa("pD", [128, 512], F32)
    C.pE = psa("pE", [128, 512], F32)
    C.pF = psa("pF", [128, 1024], BF16)

    C.R1 = sb("R1", [128, 16432], F32)
    C.R1b = C.R1.ap().bitcast(BF16)
    C.R2 = sb("R2", [128, 16432], BF16)
    C.R2f = C.R2.ap().bitcast(F32)
    C.R3 = sb("R3", [128, 16432], BF16)
    C.R3f = C.R3.ap().bitcast(F32)
    C.U3 = sb("U3", [128, 6144], BF16)
    C.U3f = C.U3.ap().bitcast(F32)
    C.U5 = sb("U5", [128, 7168], BF16)
    C.U5f = C.U5.ap().bitcast(F32)

    C.eye_f = sb("eye_f", [128, 128], F32)
    C.eye_b = sb("eye_b", [128, 128], BF16)
    C.ones_lhs = sb("ones_lhs", [1, 128], BF16)
    C.ones_rhs = sb("ones_rhs", [128, 1], BF16)
    C.ones_row = sb("ones_row", [1, 512], BF16)
    C.P1a = sb("P1a", [16, 128], BF16)
    C.P1b = sb("P1b", [16, 128], BF16)
    C.P2 = sb("P2", [16, 128], BF16)
    C.bandA = sb("bandA", [128, 128], BF16)
    C.bandB = sb("bandB", [6, 128], BF16)
    C.sel4 = sb("sel4", [4, 4 * 128], BF16)
    C.cbias = sb("cbias", [128, 4], F32)
    C.sel8 = sb("sel8", [8, 8 * 128], BF16)
    C.Bmat = sb("Bmat", [64, 1024], BF16)
    C.Cmat = sb("Cmat", [128, 8 * 64], BF16)
    C.woutf = sb("woutf", [64, 1024], BF16)
    C.gamma = sb("gamma", [128, 8], F32)
    C.bin_c = sb("bin_c", [64, 1], F32)
    C.b1flat = sb("b1flat", [128, 8], F32)
    C.b2f = sb("b2f", [64, 1], F32)
    C.bout = sb("bout", [128, 8], F32)
    C.proj = sb("proj", [128, 64], BF16)
    C.bout1 = sb("bout1", [128, 8], F32)
    C.wcomb = sb("wcomb", [128, 8 * 12], BF16)
    C.rbias = sb("rbias", [12, 1], F32)
    C.b1sb = sb("b1sb", [128, 8 * 32], F32)
    C.b2sb = sb("b2sb", [8, 1024], BF16)
    C.flags = sb("flags", [128, 2], F32)
    C.stat6 = sb("stat6", [128, 6 * NTT], F32)
    C.bn12 = sb("bn12", [128, 12], F32)
    C.stat6fm = sb("stat6fm", [4, EXT], BF16)
    C.lnA = sb("lnA", [1, EXT], BF16)
    C.lnC = sb("lnC", [1, EXT], BF16)
    C.kvA = sb("kvA", [128, 8 * 17], BF16)
    C.kvB = sb("kvB", [128, 8 * 17], BF16)
    C.kvC = sb("kvC", [17, 8 * 17], BF16)
    C.Wfmb = sb("Wfmb", [8, OWN], BF16)
    C.sc1 = sb("sc1", [128, 8], F32)
    C.sc2 = sb("sc2", [128, 16], F32)
    C.lp = sb("lp", [128, 5], F32)
    C.rt = sb("rt", [128, 12], F32)
    C.sr = sb("sr", [1, 3 * 512], F32)
    C.rl = sb("rl", [12, 128], F32)
    C.qinv = sb("qinv", [128, 8], F32)

    C.stream = lambda ci, lo, w: C.R1[:, ci * EXT + lo: ci * EXT + lo + w]
    C.xn1 = lambda ci, lo, w: C.R2[:, ci * EXT + lo: ci * EXT + lo + w]
    C.v2 = lambda ci, lo, w: C.R3[:, ci * EXT + lo: ci * EXT + lo + w]
    C.xtok = lambda i: C.R2f[:, i * 1024:(i + 1) * 1024]
    C.u_part = lambda i: C.R2[:, 4096 + i * 1024: 4096 + (i + 1) * 1024]
    C.up_fm = lambda i: C.R2[:, 6144 + i * 1024: 6144 + (i + 1) * 1024]
    C.kv_tok = lambda i: C.R2[:, 8192 + i * 256: 8192 + (i + 1) * 256]
    C.kexp_s = [C.R2[:, 8704:10888], C.R2[:, 4096:6280]]
    C.v_ext_s = [C.R2[:, 10888:11024], C.R2[:, 6280:6416]]
    C.kexp = C.kexp_s[0]
    C.v_ext = C.v_ext_s[0]
    C.wkv = C.R2[:, 11024:13072]
    C.wq = C.R2[:, 13072:14096]
    C.wout0 = C.R2[:, 14096:15120]
    C.win = C.R2[:, 15120:15632]
    C.v1 = lambda kc, w: C.U3[:, kc * 512: kc * 512 + w]
    C.s_sb = C.U3[0:64, 4096:4608]
    C.xbf = lambda kc, w: C.U3[:, kc * 512: kc * 512 + w]
    C.w2buf = [C.U3[:, 0:4096], C.U5[:, 2048:6144]]
    C.kv_acc = C.U5f[0:17, 1024:3208]
    C.qhead = lambda h: C.U5[0:17, h * 512:(h + 1) * 512]
    C.qf_sb = C.U5[:, 4096:4608]
    C.rsb = C.U5[:, 4608:5120]
    C.qeA = C.U5[:, 5120:5632]
    C.qeB = C.U5[:, 5632:6144]
    C.atok0 = C.U5f[:, 3072:3200]
    C.afm0 = C.U5[:, 6400:6912]
    C.stA = lambda i: C.U5[:, i * 128:(i + 1) * 128]
    C.stB = lambda i: C.U5[0:6, 512 + i * 128: 640 + i * 128]
    C.atok1 = C.U5[:, 1024:2048]
    C.afm1 = lambda kc, w: C.U5[:, 2048 + kc * 512: 2048 + kc * 512 + w]
    C.qkf = lambda i: C.U5[:, 6144 + i * 512: 6144 + (i + 1) * 512]
    C.t_sb1 = lambda i: C.U3f[:, 1024 + i * 512: 1536 + i * 512]
    C.t_sb = lambda i: C.U5f[:, i * 512:(i + 1) * 512]
    C.kphi = lambda kp, lo, w: C.R1b[:, kp * EXT + lo: kp * EXT + lo + w]
    C.qphi = lambda kp, lo, w: C.R1b[:, 8216 + kp * EXT + lo: 8216 + kp * EXT + lo + w]
    C.Vsb = lambda i: C.R1b[:, 16432 + i * 1032: 16432 + (i + 1) * 1032]
    C.wv = lambda kc: C.R1b[:, 19528 + kc * 1024: 19528 + (kc + 1) * 1024]
    C.wqk_rot = lambda i: C.R1b[:, 27720 + i * 1024: 27720 + (i + 1) * 1024]
    C.wout1_rot = lambda i: C.R1b[:, 27720 + i * 1024: 27720 + (i + 1) * 1024]
    C.w1sb = lambda kc: C.R1b[:, kc * 4096:(kc + 1) * 4096]
    C.h_sb = lambda hch, w: C.R3[:, hch * 512: hch * 512 + w]
    C.w_rep = lambda i: C.U3f[:, 2048 + i * 512: 2048 + (i + 1) * 512]
    return C


def load_feeds(C, inp, core):
    feeds = {}
    x_ext, x_part, flags = host_core_slices(inp["L0_attn_x"], core)
    feeds["x_ext"] = x_ext
    feeds["x_part"] = x_part
    feeds["d_flags"] = flags
    for n, v in host_constants().items():
        feeds["d_" + n] = v
    hw = host_weights(inp)
    for n, _ in WDEFS:
        feeds["d_" + n] = np.ascontiguousarray(hw[n], dtype=np.float32)
    return feeds



def emit_setup(C):
    P = C.P
    for name, dst in [("eye_f", C.eye_f), ("eye_b", C.eye_b), ("ones_lhs", C.ones_lhs),
                      ("ones_rhs", C.ones_rhs), ("ones_row", C.ones_row),
                      ("P1a", C.P1a), ("P1b", C.P1b), ("P2", C.P2),
                      ("bandA", C.bandA), ("bandB", C.bandB), ("sel4", C.sel4),
                      ("cvec", C.cbias), ("sel8", C.sel8)]:
        P.dma("sync", dst[:], C.dc[name])
    for dst, src in [
        (C.wq, C.wd["wq_r"].rearrange("p c m -> p (c m)")),
        (C.wkv, C.wd["wkv_r"].rearrange("p c m -> p (c m)")),
        (C.wout0, C.wd["wout0"]),
        (C.win, C.wd["win_r"].rearrange("p c m -> p (c m)")),
        (C.Bmat[:], C.wd["Bmat"]),
        (C.Cmat[:], C.wd["Cmat_r"].rearrange("p c m -> p (c m)")),
        (C.woutf[:], C.wd["woutf"]),
        (C.proj[:], C.wd["proj"]),
        (C.wcomb[:], C.wd["wcomb_r"].rearrange("p c m -> p (c m)")),
        (C.b2sb[:], C.wd["b2m"]),
    ]:
        P.dma("gpsimd", dst, src)
    for dst, src in [
        (C.gamma[:], C.wd["gamma_r"]), (C.bin_c[:], C.wd["bin_c"]),
        (C.b1flat[:], C.wd["b1flat_r"]), (C.b2f[:], C.wd["b2f_c"]),
        (C.bout[:], C.wd["bout_r"]), (C.bout1[:], C.wd["bout1_r"]),
        (C.rbias[:], C.wd["rbias_c"]), (C.flags[:], C.in_flags),
        (C.b1sb[:], C.wd["b1_r"].rearrange("e p c -> p e c")),
    ]:
        P.dma("sync", dst, src)
    C.t_su_sync = P.qtok("sync")
    C.t_su_gp = P.qtok("gpsimd")
    P.wait("vector", C.t_su_sync)
    P.wait("vector", C.t_su_gp)
    for kc in range(8):
        P.op("vector", lambda v, kc=kc: v.tensor_scalar(
            C.wq[:, kc * 128:(kc + 1) * 128], C.wq[:, kc * 128:(kc + 1) * 128],
            C.gamma[:, kc:kc + 1], None, ALU.mult))
        P.op("vector", lambda v, kc=kc: v.tensor_scalar(
            C.wkv[:, kc * 256:(kc + 1) * 256], C.wkv[:, kc * 256:(kc + 1) * 256],
            C.gamma[:, kc:kc + 1], None, ALU.mult))
    C.t_gamma = P.inc("vector")


def emit_stats_tile(C, x, t, col0, w, gate_pool, rows4=True):
    P = C.P
    st = C.stat6[:, 6 * t: 6 * t + 6]
    P.op("vector", lambda v: v.bn_stats(C.bn12[:, 0:6], x[:, 0:512]))
    P.op("vector", lambda v: v.bn_stats(C.bn12[:, 6:12], x[:, 512:1024]))
    t_sq = P.inc("vector")
    P.wait("vector", t_sq)
    P.op("vector", lambda v: v.bn_aggr(st[:, 4:6], C.bn12[:]))
    tv = P.inc("vector")
    P.wait("vector", tv)
    P.op("vector", lambda v: v.tensor_scalar(st[:, 5:6], st[:, 5:6], 1e-30, None, ALU.max))
    tvar = P.inc("vector")
    P.wait("scalar", tvar)
    P.op("scalar", lambda s: s.activation(C.sc2[:, 1:2], st[:, 5:6], AF.Ln, bias=C.cbias[:, 0:1]))
    P.op("scalar", lambda s: s.activation(C.sc2[:, 2:3], st[:, 5:6], AF.Ln))
    ts = P.inc("scalar")
    P.wait("scalar", ts)
    P.op("scalar", lambda s: s.activation(st[:, 0:1], C.sc2[:, 1:2], AF.Exp, scale=-0.5))
    P.op("scalar", lambda s: s.activation(st[:, 2:3], C.sc2[:, 2:3], AF.Exp, scale=-0.5))
    ts = P.inc("scalar")
    P.wait("vector", ts)
    P.op("vector", lambda v: v.tensor_scalar(C.sc2[:, 3:4], st[:, 4:5], -1.0, None, ALU.mult))
    tv = P.inc("vector")
    P.wait("vector", tv)
    P.op("vector", lambda v: v.tensor_scalar(st[:, 1:2], C.sc2[:, 3:4], st[:, 0:1], None, ALU.mult))
    P.op("vector", lambda v: v.tensor_scalar(st[:, 3:4], C.sc2[:, 3:4], st[:, 2:3], None, ALU.mult))
    tv = P.inc("vector")
    if not rows4:
        return t_sq, tv
    P.wait("tensor", tv)
    P.wait("tensor", getattr(C, "t_stats_ev", None))
    P.op("tensor", lambda T: T.transpose(C.pE[0:4, 0:128], st[:, 0:4], C.eye_f[:]))
    tt = P.inc("tensor")
    P.wait("vector", tt)
    P.op("vector", lambda v: v.tensor_copy(C.stat6fm[:, col0:col0 + w], C.pE[0:4, 0:w]))
    te = P.inc("vector")
    C.t_stats_ev = te
    return t_sq, te


def emit_load_x(C):
    P = C.P
    P.wait("vector", C.t_su_sync)
    P.wait("tensor", C.t_su_sync)
    buf_free = [None, None]
    gate_pool = None
    t_pe_prev = [None, None]
    for t in range(NTT):
        buf = t % 2
        w = twidth(t)
        xt = C.xtok(buf)
        if t == 16:
            P.wait("vector", buf_free[buf])
            P.op("vector", lambda v, xt=xt: v.memset(xt[:], 0.0))
            tz = P.inc("vector")
            buf_free[buf] = tz
        P.wait("sync", buf_free[buf])
        tl = P.dma("sync", xt[0:w, :], C.in_x[t * 128: t * 128 + w, :])
        P.wait("vector", tl)
        P.wait("gpsimd", tl)
        pz = (C.pA, C.pB)[t % 2]
        P.wait("tensor", tl)
        P.wait("tensor", t_pe_prev[t % 2])
        for kc in range(8):
            P.op("tensor", lambda T, kc=kc, xt=xt, pz=pz: T.transpose(
                pz[:, kc * 128:(kc + 1) * 128], xt[:, kc * 128:(kc + 1) * 128],
                C.eye_f[:]))
        tt = P.inc("tensor")
        gate_pool, t_stats = emit_stats_tile(C, xt, t, t * 128, w, gate_pool)
        for z in range(2):
            if 2 * t + z < 34:
                emit_w2_convert_step(C, 2 * t + z)
        P.wait("scalar", tt)
        evtok = None
        for kc in range(8):
            P.op("scalar", lambda sc, kc=kc, t=t, w=w, pz=pz: sc.copy(
                C.stream(kc, t * 128, w), pz[:, kc * 128: kc * 128 + w]))
            evtok = P.inc("scalar")
        buf_free[buf] = evtok
        t_pe_prev[t % 2] = evtok
    C.t_xload = [P.tok("scalar"), P.tok("vector")]
    C.t_stats_done = C.t_xload


def emit_vphase(C):
    P = C.P


def emit_v_window(C, lo, w, which, gate_mm=None, gate_rep=None):
    P = C.P
    r0 = 2 * which
    selA = C.sel4[:, r0 * 128:(r0 + 1) * 128]
    selC = C.sel4[:, (r0 + 1) * 128:(r0 + 2) * 128]
    P.wait("tensor", gate_mm)
    P.op("tensor", lambda T: T.matmul(C.pB[:, 0:w], selA, C.stat6fm[:, lo:lo + w],
                                      start=True, stop=True))
    P.op("tensor", lambda T: T.matmul(C.pB[:, 512:512 + w], selC, C.stat6fm[:, lo:lo + w],
                                      start=True, stop=True))
    tmm = P.inc("tensor")
    P.wait("vector", tmm)
    P.wait("vector", gate_rep)
    P.op("vector", lambda v: v.tensor_copy(C.rsb[:, 0:w], C.pB[:, 0:w]))
    P.op("vector", lambda v: v.tensor_copy(C.qf_sb[:, 0:w], C.pB[:, 512:512 + w]))
    trep = P.inc("vector")
    toks = []
    for kc in range(8):
        dst = C.v1(kc, w) if which == 0 else C.v2(kc, lo, w)
        P.wait("vector", trep)
        P.op("vector", lambda v, kc=kc, dst=dst, lo=lo, w=w: v.tensor_tensor(
            dst, C.stream(kc, lo, w), C.rsb[:, 0:w], ALU.mult))
        tv = P.inc("vector")
        P.wait("gpsimd", tv)
        P.op("gpsimd", lambda g, kc=kc, dst=dst, w=w: g.tensor_tensor(
            dst, dst, C.qf_sb[:, 0:w], ALU.add))
        toks.append(P.inc("gpsimd"))
    return tmm, trep, toks[-1]


def emit_ffn0_window(C, wi, lo, w, t_v1):
    P = C.P
    h0 = C.U5[:, 0:512]
    h1 = C.U5[:, 512:1024]
    g_sb = C.U5[0:64, 1024:1536]
    P.wait("tensor", t_v1)
    P.wait("tensor", getattr(C, "t_ffn_out_prev", None))
    for kc in range(8):
        P.op("tensor", lambda T, kc=kc: T.matmul(
            C.pA[0:64, 0:w], C.win[0:128, kc * 64:(kc + 1) * 64], C.v1(kc, w),
            start=(kc == 0), stop=(kc == 7)))
    tsm = P.inc("tensor")
    P.wait("scalar", tsm)
    P.op("scalar", lambda s: s.activation(C.s_sb[0:64, 0:w], C.pA[0:64, 0:w],
                                          AF.Identity, bias=C.bin_c[0:64, 0:1]))
    t_s = P.inc("scalar")
    t_h_ev = [None, None]
    P.wait("tensor", t_s)
    for mch in range(8):
        hb = (h0, h1)[mch % 2]
        pb = C.pB[:, (mch % 2) * 512:(mch % 2) * 512 + w]
        P.wait("tensor", t_h_ev[mch % 2])
        P.op("tensor", lambda T, mch=mch, pb=pb: T.matmul(
            pb, C.Bmat[0:64, mch * 128:(mch + 1) * 128], C.s_sb[0:64, 0:w],
            start=True, stop=True))
        tmm = P.inc("tensor")
        P.wait("scalar", tmm)
        P.op("scalar", lambda s, mch=mch, hb=hb, pb=pb: s.activation(
            hb[:, 0:w], pb, AF.Relu, bias=C.b1flat[:, mch:mch + 1]))
        the = P.inc("scalar")
        t_h_ev[mch % 2] = the
        P.wait("tensor", the)
        P.op("tensor", lambda T, mch=mch, hb=hb: T.matmul(
            C.pE[0:64, 0:w], C.Cmat[:, mch * 64:(mch + 1) * 64], hb[:, 0:w],
            start=(mch == 0), stop=(mch == 7)))
        tg = P.inc("tensor")
        t_h_ev[mch % 2] = tg
    P.wait("scalar", tg)
    P.op("scalar", lambda s: s.activation(g_sb[0:64, 0:w], C.pE[0:64, 0:w],
                                          AF.Identity, bias=C.b2f[0:64, 0:1]))
    t_gs = P.inc("scalar")
    P.wait("tensor", t_gs)
    t_o_ev = [None, None]
    last = None
    for dch in range(8):
        po = C.pA[:, (dch % 2) * 512:(dch % 2) * 512 + w]
        P.wait("tensor", t_o_ev[dch % 2])
        P.op("tensor", lambda T, dch=dch, po=po: T.matmul(
            po, C.woutf[0:64, dch * 128:(dch + 1) * 128], g_sb[0:64, 0:w],
            start=True, stop=True))
        tmm = P.inc("tensor")
        P.wait("vector", tmm)
        P.op("vector", lambda v, dch=dch, po=po, lo=lo, w=w: v.scalar_tensor_tensor(
            C.stream(dch, lo, w), po, C.bout[:, dch:dch + 1], C.stream(dch, lo, w),
            ALU.add, ALU.add))
        tev = P.inc("vector")
        t_o_ev[dch % 2] = tev
        last = tev
    C.t_ffn_out_prev = last
    return last


def emit_kside_tile(C, fm_lhs, i, n_tiles_done, first, last, gate, slot=0):
    P = C.P
    kvt = C.kv_tok(i % 2)
    P.wait("tensor", getattr(C, "t_kvtok_ev", None))
    for kc in range(8):
        P.op("tensor", lambda T, kc=kc: T.matmul(
            C.pE[:, 0:256], fm_lhs(kc), C.wkv[:, kc * 256:(kc + 1) * 256],
            start=(kc == 0), stop=(kc == 7)))
    tkv = P.inc("tensor")
    P.wait("scalar", tkv)
    P.op("scalar", lambda sc, kvt=kvt: sc.copy(kvt[:], C.pE[:, 0:256]))
    tev = P.inc("scalar")
    C.t_kvtok_ev = tev
    kx = C.kexp_s[slot]
    vx = C.v_ext_s[slot]
    P.wait("gpsimd", gate)
    P.wait("gpsimd", tev)
    for h in range(8):
        ko = 273 * h
        P.op("gpsimd", lambda g, h=h, ko=ko, kvt=kvt, kx=kx: g.tensor_tensor(
            kx[:, ko:ko + 256].rearrange("p (i j) -> p i j", j=16),
            kvt[:, 16 * h:16 * h + 16].unsqueeze(2).broadcast_to([128, 16, 16]),
            kvt[:, 16 * h:16 * h + 16].unsqueeze(1).broadcast_to([128, 16, 16]),
            ALU.mult))
    P.op("gpsimd", lambda g, kvt=kvt, kx=kx: g.tensor_copy(
        kx[:].rearrange("p (h j) -> p h j", j=273)[:, :, 256:272],
        kvt[:, 0:128].rearrange("p (h j) -> p h j", j=16)))
    P.op("gpsimd", lambda g, kvt=kvt, vx=vx: g.tensor_copy(
        vx[:].rearrange("p (h j) -> p h j", j=17)[:, :, 0:16],
        kvt[:, 128:256].rearrange("p (h j) -> p h j", j=16)))
    tb = P.inc("gpsimd")
    P.wait("tensor", tb)
    ttoks = []
    for h in range(8):
        pk = (C.pC, C.pD)[h % 2]
        P.wait("tensor", C.kv_ev[h % 2])
        P.op("tensor", lambda T, h=h, pk=pk, kx=kx, vx=vx: T.matmul(
            pk[0:17, 0:273], vx[:, 17 * h:17 * h + 17],
            kx[:, 273 * h:273 * h + 273], start=True, stop=True))
        tm = P.inc("tensor")
        P.wait("vector", tm)
        P.op("vector", lambda v, h=h, pk=pk: v.scalar_tensor_tensor(
            C.kv_acc[0:17, 273 * h:273 * h + 273], pk[0:17, 0:273], 0.0,
            C.kv_acc[0:17, 273 * h:273 * h + 273], ALU.add, ALU.add))
        C.kv_ev[h % 2] = P.inc("vector")
        ttoks.append(C.kv_ev[h % 2])
    return ttoks[-1]


def emit_partner(C):
    P = C.P
    C.kv_ev = [None, None]
    P.wait("vector", C.t_v2_done)
    P.op("vector", lambda v: v.memset(C.kv_acc[:], 0.0))
    P.op("vector", lambda v: v.memset(
        C.kexp_s[0][:].rearrange("p (h j) -> p h j", j=273)[:, :, 272:273], 1.0))
    P.op("vector", lambda v: v.memset(
        C.v_ext_s[0][:].rearrange("p (h j) -> p h j", j=17)[:, :, 16:17], 1.0))
    t0 = P.inc("vector")
    C.t_kv_zero = t0
    buf_free = [C.t_xload, C.t_xload]
    C.t_up_r = [None, None]
    C.t_uf_r = [None, None]
    C.t_upfm_ev = None
    gate_pool = ("tensor", P.cnt["tensor"])
    gate_kexp = t0
    for i in range(16):
        buf = i % 2
        xp = C.xtok(buf)
        P.wait("sync", buf_free[buf])
        tl = P.dma("sync", xp[:], C.in_xp[i * 128:(i + 1) * 128, :])
        P.wait("vector", tl)
        P.wait("gpsimd", tl)
        gate_pool, trc = emit_stats_tile(C, xp, 0, 0, 128, gate_pool, rows4=False)
        if 34 + i < 48:
            emit_w2_convert_step(C, 34 + i)
        up = C.u_part(buf)
        P.wait("scalar", trc)
        P.wait("scalar", C.t_up_r[buf])
        P.op("scalar", lambda s, xp=xp, up=up: s.activation(
            up[:], xp[:], AF.Identity, bias=C.stat6[:, 3:4], scale=C.stat6[:, 2:3]))
        tup = P.inc("scalar")
        P.wait("tensor", tup)
        P.wait("tensor", C.t_upfm_ev)
        for kc in range(8):
            P.op("tensor", lambda T, kc=kc, up=up: T.transpose(
                C.pF[:, kc * 128:(kc + 1) * 128], up[:, kc * 128:(kc + 1) * 128],
                C.eye_b[:]))
        ttr = P.inc("tensor")
        C.t_up_r[buf] = ttr
        P.wait("vector", ttr)
        P.wait("vector", C.t_uf_r[buf])
        uf = C.up_fm(buf)
        P.op("vector", lambda v, uf=uf: v.tensor_copy(uf[:], C.pF[:]))
        tuf = P.inc("vector")
        C.t_upfm_ev = tuf
        buf_free[buf] = tuf
        P.wait("tensor", tuf)
        gate_kexp = emit_kside_tile(
            C, lambda kc, uf=uf: uf[:, kc * 128:(kc + 1) * 128],
            i, i, i == 0, i == 15, gate_kexp)
        C.t_uf_r[buf] = ("tensor", P.cnt["tensor"])
    C.gate_kexp = gate_kexp


def emit_own_kside(C):
    P = C.P
    P.wait("gpsimd", C.gate_kexp)
    P.op("gpsimd", lambda g: g.memset(
        C.kexp_s[1][:].rearrange("p (h j) -> p h j", j=273)[:, :, 272:273], 1.0))
    P.op("gpsimd", lambda g: g.memset(
        C.v_ext_s[1][:].rearrange("p (h j) -> p h j", j=17)[:, :, 16:17], 1.0))
    gates = [C.gate_kexp, C.gate_kexp]
    for i in range(16):
        lo = 3 + i * 128
        gates[i % 2] = emit_kside_tile(
            C, lambda kc, lo=lo: C.v2(kc, lo, 128), i, 16 + i,
            False, i == 15, gates[i % 2], slot=i % 2)
    C.gate_kexp = gates[15 % 2]
    C.P.wait("tensor", gates[0])
    C.P.wait("tensor", gates[1])


def emit_kv_finalize(C):
    P = C.P
    P.wait("tensor", C.gate_kexp)
    for h in range(8):
        base = 273 * h
        for j, dst in ((0, C.kvA), (1, C.kvB)):
            P.op("tensor", lambda T, h=h, base=base, j=j: T.transpose(
                C.pE[:, 0:17] if j == 0 else C.pE[:, 17:34],
                C.kv_acc[0:17, base + j * 128: base + (j + 1) * 128],
                C.eye_f[0:17, 0:17]))
        P.op("tensor", lambda T, h=h, base=base: T.transpose(
            C.pE[0:17, 34:51], C.kv_acc[0:17, base + 256: base + 273],
            C.eye_f[0:17, 0:17]))
        tt = P.inc("tensor")
        P.wait("vector", tt)
        P.op("vector", lambda v, h=h: v.tensor_copy(
            C.kvA[:, 17 * h:17 * h + 17], C.pE[:, 0:17]))
        P.op("vector", lambda v, h=h: v.tensor_copy(
            C.kvB[:, 17 * h:17 * h + 17], C.pE[:, 17:34]))
        P.op("vector", lambda v, h=h: v.tensor_copy(
            C.kvC[0:17, 17 * h:17 * h + 17], C.pE[0:17, 34:51]))
        tev = P.inc("vector")
        P.wait("tensor", tev)
    C.t_kv_ready = P.tok("vector")


def emit_qphase(C):
    P = C.P
    P.wait("sync", C.t_kv_ready)
    for h in range(8):
        P.dma("sync", C.qhead(h)[16:17, 0:512], C.dc["ones_row"])
    t_ones = P.qtok("sync")
    t_wout_ev = [None, None]
    t_qf_dma = None
    t_pcpd_free = None
    for wi, (lo, w) in enumerate(WINDOWS):
        ntc = (w + 127) // 128
        P.wait("tensor", t_wout_ev[0])
        P.wait("tensor", t_wout_ev[1])
        for kc in range(8):
            P.op("tensor", lambda T, kc=kc, lo=lo, w=w: T.matmul(
                C.pA[:, 0:w], C.wq[:, kc * 128:(kc + 1) * 128], C.v2(kc, lo, w),
                start=(kc == 0), stop=(kc == 7)))
        tqm = P.inc("tensor")
        P.wait("scalar", tqm)
        P.wait("scalar", t_qf_dma)
        P.op("scalar", lambda s, w=w: s.activation(C.qf_sb[:, 0:w], C.pA[:, 0:w],
                                                   AF.Identity, scale=0.25))
        tqf = P.inc("scalar")
        P.wait("scalar", t_ones)
        P.wait("scalar", tqf)
        P.wait("scalar", getattr(C, "t_qhead_free", None))
        for h in range(8):
            P.dma("scalar", C.qhead(h)[0:16, 0:w], C.qf_sb[16 * h:16 * h + 16, 0:w])
        t_qf_dma = P.qtok("scalar")
        t_evA = None
        t_evB = None
        t_out_done = []
        for h in range(8):
            P.wait("tensor", t_qf_dma)
            P.wait("tensor", t_evB)
            P.wait("tensor", t_evA)
            P.op("tensor", lambda T, h=h, w=w: T.matmul(
                C.pB[:, 0:w], C.P1a[:], C.qhead(h)[0:16, 0:w], start=True, stop=True))
            P.op("tensor", lambda T, h=h, w=w: T.matmul(
                C.pB[:, 512:512 + w], C.P2[:], C.qhead(h)[0:16, 0:w], start=True, stop=True))
            tra = P.inc("tensor")
            P.wait("vector", tra)
            P.op("vector", lambda v, w=w: v.tensor_copy(C.rsb[:, 0:w], C.pB[:, 0:w]))
            tcp = P.inc("vector")
            P.wait("vector", tcp)
            P.op("vector", lambda v, w=w: v.tensor_tensor(
                C.qeA[:, 0:w], C.rsb[:, 0:w], C.pB[:, 512:512 + w], ALU.mult))
            t_evA = P.inc("vector")
            P.wait("tensor", t_evA)
            P.wait("tensor", t_evB)
            P.op("tensor", lambda T, h=h, w=w: T.matmul(
                C.pB[:, 0:w], C.P1b[:], C.qhead(h)[0:16, 0:w], start=True, stop=True))
            P.op("tensor", lambda T, h=h, w=w: T.matmul(
                C.pB[:, 512:512 + w], C.P2[:], C.qhead(h)[0:16, 0:w], start=True, stop=True))
            trb = P.inc("tensor")
            P.wait("vector", trb)
            P.op("vector", lambda v, w=w: v.tensor_copy(C.rsb[:, 0:w], C.pB[:, 0:w]))
            tcp = P.inc("vector")
            P.wait("vector", tcp)
            P.op("vector", lambda v, w=w: v.tensor_tensor(
                C.qeB[:, 0:w], C.rsb[:, 0:w], C.pB[:, 512:512 + w], ALU.mult))
            t_evB = P.inc("vector")
            P.wait("tensor", t_evB)
            if h == 0:
                P.wait("tensor", t_pcpd_free)
            for tc in range(ntc):
                tw = min(128, w - tc * 128)
                pk = (C.pC, C.pC, C.pD, C.pD)[tc]
                off = (0, 136, 0, 136)[tc]
                P.op("tensor", lambda T, h=h, tc=tc, tw=tw, pk=pk, off=off: T.matmul(
                    pk[0:tw, off + 17 * h: off + 17 * h + 17],
                    C.qeA[:, tc * 128: tc * 128 + tw], C.kvA[:, 17 * h:17 * h + 17],
                    start=True, stop=False))
                P.op("tensor", lambda T, h=h, tc=tc, tw=tw, pk=pk, off=off: T.matmul(
                    pk[0:tw, off + 17 * h: off + 17 * h + 17],
                    C.qeB[:, tc * 128: tc * 128 + tw], C.kvB[:, 17 * h:17 * h + 17],
                    start=False, stop=False))
                P.op("tensor", lambda T, h=h, tc=tc, tw=tw, pk=pk, off=off, lo=lo:
                     T.matmul(pk[0:tw, off + 17 * h: off + 17 * h + 17],
                              C.qhead(h)[0:17, tc * 128: tc * 128 + tw],
                              C.kvC[0:17, 17 * h:17 * h + 17],
                              start=False, stop=True))
            tmo = P.inc("tensor")
            t_out_done.append(tmo)
            C.t_qhead_free = tmo
        t_tr = []
        t_at_free = None
        t_af_ev = [tqf, tqf]
        for tc in range(ntc):
            tw = min(128, w - tc * 128)
            pk = (C.pC, C.pC, C.pD, C.pD)[tc]
            off = (0, 136, 0, 136)[tc]
            pv = pk[0:tw, off:off + 136].rearrange("p (h e) -> p h e", e=17)
            P.wait("vector", t_out_done[-1])
            P.wait("vector", t_at_free)
            P.op("vector", lambda v, pv=pv, tw=tw: v.tensor_scalar(
                C.sc2[0:tw, 0:8].unsqueeze(2), pv[:, :, 16:17], 1e-5, None, ALU.max))
            tv = P.inc("vector")
            P.wait("vector", tv)
            P.op("vector", lambda v, tw=tw: v.reciprocal(
                C.qinv[0:tw, 0:8], C.sc2[0:tw, 0:8]))
            tv = P.inc("vector")
            P.wait("vector", tv)
            P.op("vector", lambda v, pv=pv, tw=tw: v.tensor_tensor(
                C.atok0[0:tw, 0:128].rearrange("p (h e) -> p h e", e=16),
                pv[:, :, 0:16],
                C.qinv[0:tw, 0:8].unsqueeze(2).broadcast_to([tw, 8, 16]), ALU.mult))
            tat = P.inc("vector")
            P.wait("tensor", tat)
            P.wait("tensor", t_af_ev[tc % 2])
            P.op("tensor", lambda T, tc=tc, tw=tw: T.transpose(
                C.pA[:, (tc % 2) * 512:(tc % 2) * 512 + tw], C.atok0[0:tw, 0:128],
                C.eye_f[0:tw, 0:tw]))
            tt = P.inc("tensor")
            t_at_free = tt
            P.wait("vector", tt)
            P.op("vector", lambda v, tc=tc, tw=tw: v.tensor_copy(
                C.afm0[:, tc * 128: tc * 128 + tw],
                C.pA[:, (tc % 2) * 512:(tc % 2) * 512 + tw]))
            t_tr.append(P.inc("vector"))
            t_af_ev[tc % 2] = t_tr[-1]
        t_pcpd_free = ("vector", C.P.cnt["vector"])
        P.wait("tensor", t_tr[-1])
        for dch in range(8):
            po = C.pA[:, (dch % 2) * 512:(dch % 2) * 512 + w]
            P.wait("tensor", t_wout_ev[dch % 2])
            P.op("tensor", lambda T, dch=dch, po=po, w=w: T.matmul(
                po, C.wout0[:, dch * 128:(dch + 1) * 128], C.afm0[:, 0:w],
                start=True, stop=True))
            tmm = P.inc("tensor")
            P.wait("vector", tmm)
            P.op("vector", lambda v, dch=dch, po=po, lo=lo, w=w: v.scalar_tensor_tensor(
                C.stream(dch, lo, w), po, 0.0, C.stream(dch, lo, w),
                ALU.add, ALU.add))
            t_wout_ev[dch % 2] = P.inc("vector")
    C.t_block0 = P.tok("vector")


def emit_final_store(C, src="dram"):
    P = C.P
    if C.dbg:
        P.wait_q_all("vector", "gpsimd")
        P.wait_q_all("vector", "sync")
    if src == "dram":
        P.wait("sync", C.t_moe_done)
    else:
        P.wait("sync", C.t_block0)
        C.stream_dram = C.R1
    for kc in range(8):
        P.dma("sync", C.out_y[kc * 128:(kc + 1) * 128, :],
              (C.stream_dram[:, kc * EXT + 3: kc * EXT + 3 + OWN] if src == "dram"
               else C.R1[:, kc * EXT + 3: kc * EXT + 3 + OWN]))
    P.wait_q_all("sync", "sync")
    P.wait_q_all("sync", "gpsimd")
    P.wait_q_all("sync", "scalar")


def emit_debug_stores(C):
    P = C.P
    if "stream" in C.dbg:
        P.wait("sync", C.t_block0)
        P.dma("sync", C.dbg["stream"], C.R1[:, 0:8 * EXT])
    if "statfm" in C.dbg:
        P.wait("gpsimd", C.t_stats_done)
        P.dma("gpsimd", C.dbg["statfm"], C.stat6fm[:])
    if "v2" in C.dbg:
        P.wait("gpsimd", C.t_v2_done)
        P.dma("gpsimd", C.dbg["v2"], C.R3[:, 0:8 * EXT])
    if "xn1" in C.dbg:
        P.wait("gpsimd", C.t_xn1)
        P.dma("gpsimd", C.dbg["xn1"], C.R2[:, 0:8 * EXT])
    if "phi" in C.dbg:
        P.wait("gpsimd", C.t_phi)
        P.dma("gpsimd", C.dbg["phi"], C.R1b[:, 0:8 * EXT])
    if "afm" in C.dbg:
        P.wait("gpsimd", C.t_attn1)
        P.dma("gpsimd", C.dbg["afm"], C.U5[:, 2048:6144])
    if "atok" in C.dbg:
        P.wait("gpsimd", C.t_attn1)
        P.dma("gpsimd", C.dbg["atok"], C.atok1)
    if "sdram" in C.dbg:
        P.wait("gpsimd", C.t_moe_done)
        P.dma("gpsimd", C.dbg["sdram"], C.stream_dram[:, 0:8 * EXT])
    if "wfm" in C.dbg:
        P.wait("gpsimd", C.t_router)
        P.dma("gpsimd", C.dbg["wfm"], C.Wfmb[:])


def emit_block0(C):
    emit_load_x(C)
    P = C.P
    prev_trep = C.t_xload
    prev_pool = None
    t_prev_s = None
    P.wait("tensor", C.t_xload)
    for wi, (lo, w) in enumerate(WINDOWS):
        tmm, trep, tpool = emit_v_window(C, lo, w, 0, gate_mm=prev_trep,
                                         gate_rep=prev_pool)
        tmm2, trep2, tpool2 = emit_v_window(C, lo, w, 1, gate_mm=trep,
                                            gate_rep=tpool)
        prev_trep = trep2
        prev_pool = tpool2
        P.wait("tensor", trep2)
        t_prev_s = emit_ffn0_window(C, wi, lo, w, tpool)
    C.t_v2_done = prev_pool
    C.t_ffn0 = t_prev_s
    C.P.wait("vector", C.t_ffn0)
    emit_partner(C)
    emit_own_kside(C)
    emit_kv_finalize(C)
    emit_qphase(C)


def build(phases=("b0", "b1", "moe"), debug=()):
    C = make_ctx(phases, debug)
    emit_setup(C)
    if "load" in phases:
        emit_load_x(C)
        C.t_block0 = C.t_xload
    if "loadd" in phases:
        emit_load_x(C)
        C.t_block0 = C.t_xload
        C.P.wait("sync", C.t_xload)
        C.t_offload = C.P.dma("sync", C.stream_dram[:, 0:8 * EXT], C.R1[:, 0:8 * EXT])
        C.t_moe_done = C.t_offload
    if "b0" in phases:
        emit_block0(C)
    if "b1" in phases:
        emit_block1(C, with_b2=("moe" in phases))
    if "moe" in phases:
        emit_moe(C)
    emit_debug_stores(C)
    emit_final_store(C, src="dram" if ("moe" in phases or "b1" in phases
                                       or "loadd" in phases) else "stream")
    C.P.emit(C.es)
    C.es.close()
    return C


def run_cores(C, inp, cores, trace=False):
    feeds = [load_feeds(C, inp, c) for c in cores]
    res = run_bass_kernel_spmd(C.nc, feeds, core_ids=list(range(len(cores))),
                               trace=trace)
    for r in res.results:
        r["y"] = np.ascontiguousarray(r["y"].T)
    return res


def emit_ln1(C):
    P = C.P
    sq = [C.U5[:, 0:512], C.U5[:, 512:1024]]
    arep_sb = C.U5[:, 1024:1536]
    crep_sb = C.U5[:, 1536:2048]
    t_prev_rep = None
    t_prev_xbf = C.t_block0
    t_prev_pool = None
    t_prev_mm = C.t_block0
    for wi, (lo, w) in enumerate(WINDOWS):
        t_sq_prev = [t_prev_pool, t_prev_pool]
        P.wait("vector", t_prev_xbf)
        P.wait("tensor", t_prev_mm)
        tkc = []
        for kc in range(8):
            P.op("vector", lambda v, kc=kc, lo=lo, w=w: v.tensor_copy(
                C.xbf(kc, w), C.stream(kc, lo, w)))
            txb = P.inc("vector")
            P.wait("gpsimd", txb)
            P.wait("gpsimd", t_sq_prev[kc % 2])
            P.op("gpsimd", lambda g, kc=kc, w=w: g.tensor_tensor(
                sq[kc % 2][:, 0:w], C.xbf(kc, w), C.xbf(kc, w), ALU.mult))
            tsq = P.inc("gpsimd")
            P.wait("tensor", txb)
            P.op("tensor", lambda T, kc=kc, w=w: T.matmul(
                C.pE[0:1, 0:w], C.ones_rhs[:], C.xbf(kc, w),
                start=(kc == 0), stop=(kc == 7)))
            P.wait("tensor", tsq)
            P.op("tensor", lambda T, kc=kc, w=w: T.matmul(
                C.pC[0:1, 0:w], C.ones_rhs[:], sq[kc % 2][:, 0:w],
                start=(kc == 0), stop=(kc == 7)))
            t_sq_prev[kc % 2] = P.inc("tensor")
        t_sums = P.tok("tensor")
        t_prev_pool = P.tok("gpsimd")
        P.wait("vector", t_sums)
        P.op("vector", lambda v, w=w: v.tensor_scalar(
            C.sr[0:1, 0:w], C.pE[0:1, 0:w], 1.0 / D, None, ALU.mult))
        P.op("vector", lambda v, w=w: v.tensor_scalar(
            C.sr[0:1, 512:512 + w], C.pC[0:1, 0:w], 1.0 / D, None, ALU.mult))
        tv = P.inc("vector")
        P.wait("vector", tv)
        P.op("vector", lambda v, w=w: v.tensor_tensor(
            C.sr[0:1, 1024:1024 + w], C.sr[0:1, 0:w], C.sr[0:1, 0:w], ALU.mult))
        tv = P.inc("vector")
        P.wait("vector", tv)
        P.op("vector", lambda v, w=w: v.tensor_tensor(
            C.sr[0:1, 512:512 + w], C.sr[0:1, 512:512 + w],
            C.sr[0:1, 1024:1024 + w], ALU.subtract))
        tv = P.inc("vector")
        P.wait("vector", tv)
        P.op("vector", lambda v, w=w: v.tensor_scalar(
            C.sr[0:1, 512:512 + w], C.sr[0:1, 512:512 + w], 0.0, None, ALU.max))
        tvar = P.inc("vector")
        P.wait("scalar", tvar)
        P.op("scalar", lambda s, w=w: s.activation(
            C.sr[0:1, 1024:1024 + w], C.sr[0:1, 512:512 + w], AF.Ln,
            bias=C.cbias[0:1, 0:1]))
        ts = P.inc("scalar")
        P.wait("scalar", ts)
        P.op("scalar", lambda s, w=w: s.activation(
            C.sr[0:1, 512:512 + w], C.sr[0:1, 1024:1024 + w], AF.Exp, scale=-0.5))
        ta = P.inc("scalar")
        P.wait("vector", ta)
        P.op("vector", lambda v, lo=lo, w=w: v.tensor_copy(
            C.lnA[0:1, lo:lo + w], C.sr[0:1, 512:512 + w]))
        P.op("vector", lambda v, w=w: v.tensor_scalar(
            C.sr[0:1, 0:w], C.sr[0:1, 0:w], -1.0, None, ALU.mult))
        tv = P.inc("vector")
        P.wait("vector", tv)
        P.op("vector", lambda v, lo=lo, w=w: v.tensor_tensor(
            C.sr[0:1, 0:w], C.sr[0:1, 0:w], C.sr[0:1, 512:512 + w], ALU.mult))
        tv = P.inc("vector")
        P.wait("vector", tv)
        P.op("vector", lambda v, lo=lo, w=w: v.tensor_copy(
            C.lnC[0:1, lo:lo + w], C.sr[0:1, 0:w]))
        tc_ = P.inc("vector")
        P.wait("tensor", tc_)
        P.wait("tensor", t_prev_rep)
        P.op("tensor", lambda T, lo=lo, w=w: T.matmul(
            C.pB[:, 0:w], C.ones_lhs[:], C.lnA[0:1, lo:lo + w], start=True, stop=True))
        P.op("tensor", lambda T, lo=lo, w=w: T.matmul(
            C.pB[:, 512:512 + w], C.ones_lhs[:], C.lnC[0:1, lo:lo + w],
            start=True, stop=True))
        tmm = P.inc("tensor")
        P.wait("vector", tmm)
        P.wait("vector", t_prev_pool)
        P.op("vector", lambda v, w=w: v.tensor_copy(arep_sb[:, 0:w], C.pB[:, 0:w]))
        P.op("vector", lambda v, w=w: v.tensor_copy(crep_sb[:, 0:w], C.pB[:, 512:512 + w]))
        t_prev_rep = P.inc("vector")
        P.wait("vector", t_prev_rep)
        toks = []
        for kc in range(8):
            P.op("vector", lambda v, kc=kc, lo=lo, w=w: v.tensor_tensor(
                C.xn1(kc, lo, w), C.xbf(kc, w), arep_sb[:, 0:w], ALU.mult))
            tv = P.inc("vector")
            P.wait("gpsimd", tv)
            P.op("gpsimd", lambda g, kc=kc, lo=lo, w=w: g.tensor_tensor(
                C.xn1(kc, lo, w), C.xn1(kc, lo, w), crep_sb[:, 0:w], ALU.add))
            toks.append(P.inc("gpsimd"))
        t_prev_xbf = toks[-1]
        t_prev_pool = toks[-1]
        t_prev_mm = P.tok("vector")
    C.t_xn1 = t_prev_xbf
    P.wait("sync", P.tok("vector"))
    C.t_offload = P.dma("sync", C.stream_dram[:, 0:8 * EXT], C.R1[:, 0:8 * EXT])


def emit_router_tc(C, t):
    P = C.P
    rsc = C.U3f[:, 512:536]
    E_ = rsc[:, 0:8]
    sel = rsc[:, 8:16]
    Wv = rsc[:, 16:24]
    t_rl_ev = C.t_rl_ev
    t_tr_prev = C.t_tr_prev
    if True:
        lo = 3 + t * 128
        P.wait("tensor", C.t_xn1)
        P.wait("tensor", t_rl_ev)
        P.wait("tensor", t_tr_prev)
        for kc in range(8):
            P.op("tensor", lambda T, kc=kc, lo=lo: T.matmul(
                C.pD[0:12, 0:128], C.wcomb[:, kc * 12:(kc + 1) * 12],
                C.xn1(kc, lo, 128), start=(kc == 0), stop=(kc == 7)))
        tmm = P.inc("tensor")
        P.wait("scalar", tmm)
        P.op("scalar", lambda s: s.activation(C.rl[0:12, 0:128], C.pD[0:12, 0:128],
                                              AF.Identity, bias=C.rbias[0:12, 0:1]))
        trl = P.inc("scalar")
        t_rl_ev = trl
        P.wait("tensor", trl)
        P.wait("tensor", t_tr_prev)
        P.op("tensor", lambda T: T.transpose(C.pD[0:128, 128:140], C.rl[0:12, 0:128],
                                             C.eye_f[0:12, 0:12]))
        ttr = P.inc("tensor")
        P.wait("vector", ttr)
        P.op("vector", lambda v: v.tensor_copy(C.rt[:, 0:12], C.pD[:, 128:140]))
        tv = P.inc("vector")
        t_tr_prev = tv
        P.wait("vector", tv)
        P.op("vector", lambda v: v.reduce_max(C.sc2[:, 8:9], C.rt[:, 8:12], axis=AX))
        tv = P.inc("vector")
        P.wait("vector", tv)
        P.op("vector", lambda v: v.tensor_scalar(C.sc2[:, 9:10], C.sc2[:, 8:9], -1.0, None, ALU.mult))
        tv = P.inc("vector")
        P.wait("scalar", tv)
        P.op("scalar", lambda s: s.activation(C.lp[:, 0:4], C.rt[:, 8:12], AF.Exp,
                                              bias=C.sc2[:, 9:10]))
        ts = P.inc("scalar")
        P.wait("vector", ts)
        P.op("vector", lambda v: v.reduce_sum(C.sc2[:, 10:11], C.lp[:, 0:4], axis=AX))
        tv = P.inc("vector")
        P.wait("vector", tv)
        P.op("vector", lambda v: v.reciprocal(C.sc2[:, 11:12], C.sc2[:, 10:11]))
        tv = P.inc("vector")
        P.wait("vector", tv)
        P.op("vector", lambda v: v.tensor_scalar(C.lp[:, 0:4], C.lp[:, 0:4],
                                                 C.sc2[:, 11:12], None, ALU.mult))
        tv = P.inc("vector")
        P.wait("vector", tv)
        P.op("vector", lambda v: v.tensor_tensor(
            C.sc1[:, 0:8].rearrange("p (a b) -> p a b", b=4),
            C.lp[:, 0:4].unsqueeze(1).broadcast_to([128, 2, 4]),
            C.lp[:, 1:5].unsqueeze(1).broadcast_to([128, 2, 4]), ALU.add))
        tv = P.inc("vector")
        P.wait("vector", tv)
        P.op("vector", lambda v: v.scalar_tensor_tensor(
            C.rt[:, 0:8], C.sc1[:, 0:8], 0.1, C.rt[:, 0:8], ALU.mult, ALU.add))
        tv = P.inc("vector")
        P.wait("vector", tv)
        P.op("vector", lambda v: v.reduce_max(C.sc2[:, 8:9], C.rt[:, 0:8], axis=AX))
        tv = P.inc("vector")
        P.wait("vector", tv)
        P.op("vector", lambda v: v.tensor_scalar(C.sc2[:, 9:10], C.sc2[:, 8:9], -1.0, None, ALU.mult))
        tv = P.inc("vector")
        P.wait("scalar", tv)
        P.op("scalar", lambda s: s.activation(E_, C.rt[:, 0:8], AF.Exp,
                                              bias=C.sc2[:, 9:10]))
        ts = P.inc("scalar")
        P.wait("vector", ts)
        P.op("vector", lambda v: v.tensor_scalar(sel, E_, 1.0, None, ALU.is_lt))
        tv = P.inc("vector")
        P.wait("vector", tv)
        P.op("vector", lambda v: v.tensor_tensor(Wv, E_, sel, ALU.mult))
        tv = P.inc("vector")
        P.wait("vector", tv)
        P.op("vector", lambda v: v.reduce_max(C.sc2[:, 8:9], Wv, axis=AX))
        tv = P.inc("vector")
        P.wait("vector", tv)
        P.op("vector", lambda v: v.tensor_scalar(C.sc2[:, 9:10], C.sc2[:, 8:9], 1.0, None, ALU.add))
        P.op("vector", lambda v: v.tensor_scalar(sel, E_, C.sc2[:, 8:9], None, ALU.is_ge))
        tv = P.inc("vector")
        P.wait("vector", tv)
        P.op("vector", lambda v: v.reciprocal(C.sc2[:, 10:11], C.sc2[:, 9:10]))
        tv = P.inc("vector")
        P.wait("vector", tv)
        P.op("vector", lambda v: v.tensor_tensor(Wv, E_, sel, ALU.mult))
        tv = P.inc("vector")
        P.wait("vector", tv)
        P.op("vector", lambda v: v.tensor_scalar(Wv, Wv, C.sc2[:, 10:11], None, ALU.mult))
        tv = P.inc("vector")
        P.wait("tensor", tv)
        P.op("tensor", lambda T: T.transpose(C.pD[0:8, 140:268],
                                             Wv, C.eye_f[:]))
        ttr = P.inc("tensor")
        P.wait("vector", ttr)
        P.op("vector", lambda v, t=t: v.tensor_copy(C.Wfmb[0:8, t * 128:(t + 1) * 128],
                                                    C.pD[0:8, 140:268]))
        tv = P.inc("vector")
        t_tr_prev = tv
    C.t_rl_ev = t_rl_ev
    C.t_tr_prev = t_tr_prev
    C.t_router = P.tok("vector")


def emit_qkv_phi_v(C):
    P = C.P
    php = C.U5[:, 0:512]
    tm1 = C.U5[:, 512:1024]
    tm2 = C.U5[:, 1024:1536]
    P.wait("gpsimd", C.t_offload)
    P.wait("vector", C.t_offload)
    t_w_used = [None, None]
    t_w_read = [None, None]
    t_phi_prev = None
    t_qh_used = [None, None]
    for pair in range(8):
        for half in range(2):
            mc = (pair % 4) * 2 + half + (8 if pair >= 4 else 0)
            P.wait("gpsimd", t_w_read[half])
            tl = P.dma("gpsimd", C.wqk_rot(half)[:],
                       C.wd["wqk_r"][:, :, mc * 128:(mc + 1) * 128])
            t_w_used[half] = tl
        for wi, (lo, w) in enumerate(WINDOWS):
            for half in range(2):
                pq = (C.pE, C.pC)[half]
                P.wait("tensor", t_w_used[half])
                P.wait("tensor", C.t_xn1)
                P.wait("tensor", t_qh_used[half])
                for kc in range(8):
                    P.op("tensor", lambda T, kc=kc, half=half, lo=lo, w=w, pq=pq:
                         T.matmul(pq[:, 0:w],
                                  C.wqk_rot(half)[:, kc * 128:(kc + 1) * 128],
                                  C.xn1(kc, lo, w), start=(kc == 0), stop=(kc == 7)))
                tmm = P.inc("tensor")
                P.wait("scalar", tmm)
                P.op("scalar", lambda sc, half=half, w=w, pq=pq: sc.copy(
                    C.qkf(half)[:, 0:w], pq[:, 0:w]))
                tev = P.inc("scalar")
                P.wait("tensor", tev)
                P.wait("tensor", t_phi_prev)
                ph = (C.pB[:, 0:512], C.pB[:, 512:1024])[half]
                P.op("tensor", lambda T, half=half, w=w, ph=ph: T.matmul(
                    ph[0:64, 0:w], C.proj[:], C.qkf(half)[:, 0:w],
                    start=True, stop=True))
                t_qh_used[half] = P.inc("tensor")
            tph = P.tok("tensor")
            P.wait("vector", tph)
            P.op("vector", lambda v, w=w: v.tensor_copy(php[0:64, 0:w], C.pB[0:64, 0:w]))
            P.op("vector", lambda v, w=w: v.tensor_copy(php[64:128, 0:w],
                                                        C.pB[0:64, 512:512 + w]))
            tev = P.inc("vector")
            t_phi_prev = tev
            P.wait("vector", tev)
            P.op("vector", lambda v, w=w: v.tensor_scalar(tm1[:, 0:w], php[:, 0:w],
                                                          0.0, None, ALU.min))
            tv = P.inc("vector")
            P.wait("scalar", tv)
            P.op("scalar", lambda s, w=w: s.activation(tm2[:, 0:w], tm1[:, 0:w], AF.Exp))
            ts = P.inc("scalar")
            P.wait("vector", ts)
            kp = pair % 4
            dst = (C.qphi, C.kphi)[pair // 4]
            P.op("vector", lambda v, kp=kp, dst=dst, lo=lo, w=w: v.scalar_tensor_tensor(
                dst(kp, lo, w), php[:, 0:w], 0.0, tm2[:, 0:w], ALU.max, ALU.add))
            tv = P.inc("vector")
            t_phi_prev = tv
        t_w_read[0] = ("tensor", P.cnt["tensor"])
        t_w_read[1] = t_w_read[0]
        emit_router_tc(C, 2 * pair)
        emit_router_tc(C, 2 * pair + 1)
    C.t_qkv_done = ("tensor", P.cnt["tensor"])
    if True:
        pass
    P.wait("vector", P.tok("vector"))
    for kp in range(4):
        P.op("vector", lambda v, kp=kp: v.tensor_scalar(
            C.kphi(kp, 0, 3), C.kphi(kp, 0, 3), C.flags[:, 0:1], None, ALU.mult))
        P.op("vector", lambda v, kp=kp: v.tensor_scalar(
            C.kphi(kp, 2051, 3), C.kphi(kp, 2051, 3), C.flags[:, 1:2], None, ALU.mult))
    C.t_phi = P.inc("vector")
    C.t_v_ready = {}
    C.t_vslot_used = [C.t_offload, C.t_offload, C.t_offload]
    for sl in range(3):
        P.wait("vector", C.t_offload)
        P.op("vector", lambda v, sl=sl: v.memset(
            C.Vsb(sl).rearrange("p (h e) -> p h e", e=129)[:, :, 128:129], 1.0))
    C.t_vones = P.inc("vector")


def emit_v_tile(C, t):
    P = C.P
    tw = twidth(t)
    sl = t % 3
    P.wait("tensor", C.t_xn1)
    tev = None
    for half in range(2):
        P.wait("tensor", C.t_vslot_used[sl])
        P.wait("tensor", getattr(C, "t_pd_ev", None))
        P.wait("tensor", getattr(C, "t_router", None))
        for kc in range(8):
            P.op("tensor", lambda T, kc=kc, half=half, t=t, tw=tw: T.matmul(
                C.pD[0:tw, 0:512], C.xn1(kc, t * 128, tw),
                C.wv(kc)[:, half * 512:half * 512 + 512],
                start=(kc == 0), stop=(kc == 7)))
        tmm = P.inc("tensor")
        P.wait("vector", tmm)
        P.wait("vector", C.t_vones)
        P.op("vector", lambda v, t=t, tw=tw, sl=sl, half=half: v.tensor_copy(
            C.Vsb(sl)[0:tw, half * 516:half * 516 + 516]
            .rearrange("p (h e) -> p h e", e=129)[:, :, 0:128],
            C.pD[0:tw, 0:512].rearrange("p (h e) -> p h e", e=128)))
        tev = P.inc("vector")
        C.t_pd_ev = tev
    C.t_v_ready[t] = tev
    return tev


def emit_block1_attn(C, with_b2=True):
    P = C.P
    P.wait("gpsimd", C.t_offload)
    for kc in range(8):
        P.dma("gpsimd", C.wv(kc)[:], C.wd["wv_r"][:, kc:kc + 1, :])
    C.t_wv = P.qtok("gpsimd")
    P.wait("tensor", C.t_wv)
    emit_v_tile(C, 0)
    emit_v_tile(C, 1)
    t_st_prev = [None, None]
    t_st_rd = [None, None]
    t_atok_prev = None
    t_bank_ev = None
    t_pf_prev = None
    t_afm_used = None
    t_wout_used = [None, None, None, None]
    t_wo_dma = [None, None]
    for qt in range(16):
        if qt + 1 <= 16:
            emit_v_tile(C, qt + 1)
        for hb4 in range(2):
            P.wait("tensor", C.t_phi)
            P.wait("tensor", t_st_prev[hb4 % 2])
            if hb4 == 0:
                for _b in range(4):
                    P.wait("tensor", t_wout_used[_b])
            for hi in range(4):
                h = hb4 * 4 + hi
                kp, hh = h // 2, h % 2
                P.op("tensor", lambda T, kp=kp, hh=hh, qt=qt, hi=hi: T.matmul(
                    C.pE[:, hi * 128:(hi + 1) * 128],
                    C.kphi(kp, qt * 128, 128)[64 * hh:64 * hh + 64, :],
                    C.qphi(kp, qt * 128 + 3, 128)[64 * hh:64 * hh + 64, :],
                    start=True, stop=True))
                P.op("tensor", lambda T, kp=kp, hh=hh, qt=qt, hi=hi: T.matmul(
                    C.pC[0:6, hi * 128:(hi + 1) * 128],
                    C.kphi(kp, qt * 128 + 128, 6)[64 * hh:64 * hh + 64, :],
                    C.qphi(kp, qt * 128 + 3, 128)[64 * hh:64 * hh + 64, :],
                    start=True, stop=True))
            tst = P.inc("tensor")
            P.wait("vector", tst)
            P.wait("vector", t_st_rd[hb4 % 2])
            P.op("vector", lambda v: v.tensor_tensor(
                C.U5[:, 0:512].rearrange("p (a b) -> p a b", b=128),
                C.pE[:, 0:512].rearrange("p (a b) -> p a b", b=128),
                C.bandA[:].unsqueeze(1).broadcast_to([128, 4, 128]), ALU.mult))
            P.op("vector", lambda v: v.tensor_tensor(
                C.U5[0:6, 512:1024].rearrange("p (a b) -> p a b", b=128),
                C.pC[0:6, 0:512].rearrange("p (a b) -> p a b", b=128),
                C.bandB[:].unsqueeze(1).broadcast_to([6, 4, 128]), ALU.mult))
            t_st_prev[hb4 % 2] = P.inc("vector")
            P.wait("tensor", t_st_prev[hb4 % 2])
            if hb4 == 0:
                P.wait("tensor", t_bank_ev)
            P.wait("tensor", C.t_v_ready[qt])
            P.wait("tensor", C.t_v_ready[qt + 1])
            for hi in range(4):
                h = hb4 * 4 + hi
                bank = (C.pA, C.pA, C.pB, C.pB)[h // 2]
                off = (0, 512)[(h // 2) % 2] + (h % 2) * 129
                P.op("tensor", lambda T, h=h, qt=qt, bank=bank, off=off, hi=hi:
                     T.matmul(bank[:, off:off + 129], C.stA(hi),
                              C.Vsb(qt % 3)[:, h * 129:(h + 1) * 129],
                              start=True, stop=False))
                P.op("tensor", lambda T, h=h, qt=qt, bank=bank, off=off, hi=hi:
                     T.matmul(bank[:, off:off + 129], C.stB(hi),
                              C.Vsb((qt + 1) % 3)[0:6, h * 129:(h + 1) * 129],
                              start=False, stop=True))
            tat = P.inc("tensor")
            t_st_rd[hb4 % 2] = tat
        C.t_vslot_used[qt % 3] = tat
        C.t_vslot_used[(qt + 1) % 3] = tat
        t_ev = []
        for g in range(4):
            bank = (C.pA, C.pA, C.pB, C.pB)[g]
            off = (0, 512)[g % 2]
            pv = bank[:, off:off + 258].rearrange("p (h e) -> p h e", e=129)
            P.wait("vector", tat)
            P.wait("vector", t_atok_prev)
            P.op("vector", lambda v, pv=pv, g=g: v.tensor_scalar(
                C.sc2[:, 2 * g:2 * g + 2].unsqueeze(2), pv[:, :, 128:129],
                1e-8, None, ALU.add))
            tv = P.inc("vector")
            P.wait("vector", tv)
            P.op("vector", lambda v, g=g: v.reciprocal(
                C.qinv[:, 2 * g:2 * g + 2], C.sc2[:, 2 * g:2 * g + 2]))
            tv = P.inc("vector")
            P.wait("vector", tv)
            P.op("vector", lambda v, pv=pv, g=g: v.tensor_tensor(
                C.atok1[:, g * 256:(g + 1) * 256].rearrange("p (h e) -> p h e", e=128),
                pv[:, :, 0:128],
                C.qinv[:, 2 * g:2 * g + 2].unsqueeze(2).broadcast_to([128, 2, 128]),
                ALU.mult))
            t_ev.append(P.inc("vector"))
        t_bank_ev = t_ev[-1]
        t_atok_prev = None
        P.wait("tensor", t_ev[-1])
        P.wait("tensor", t_pf_prev)
        for kc in range(8):
            P.op("tensor", lambda T, kc=kc: T.transpose(
                C.pF[:, kc * 128:(kc + 1) * 128],
                C.atok1[:, kc * 128:(kc + 1) * 128], C.eye_b[:]))
        ttr = P.inc("tensor")
        t_atok_prev = ttr
        P.wait("vector", ttr)
        P.wait("vector", t_afm_used)
        for kc in range(8):
            P.op("vector", lambda v, kc=kc, qt=qt: v.tensor_copy(
                C.afm1(kc, 512)[:, (qt % 4) * 128:(qt % 4) * 128 + 128],
                C.pF[:, kc * 128:(kc + 1) * 128]))
        t_pf_prev = P.inc("vector")
        if qt % 4 == 3:
            g4 = qt // 4
            lo = 3 + g4 * 512
            P.wait("tensor", t_pf_prev)
            for dch in range(8):
                buf = dch % 4
                P.wait("gpsimd", C.t_qkv_done)
                P.wait("gpsimd", t_wout_used[buf])
                tl = P.dma("gpsimd", C.wout1_rot(buf)[:],
                           C.wd["wout1_r"][:, :, dch * 128:(dch + 1) * 128])
                P.wait("tensor", tl)
                P.wait("tensor", t_wout_used[buf])
                po = (C.pE, C.pC)[dch % 2]
                for kc in range(8):
                    P.op("tensor", lambda T, kc=kc, buf=buf, po=po, wb=with_b2:
                         T.matmul(po[:, 0:512],
                                  C.wout1_rot(buf)[:, kc * 128:(kc + 1) * 128],
                                  C.afm1(kc, 512), start=(kc == 0),
                                  stop=(kc == 7 and not wb)))
                if with_b2:
                    P.wait("tensor", C.t_router)
                    P.op("tensor", lambda T, dch=dch, g4=g4, po=po: T.matmul(
                        po[:, 0:512], C.b2sb[0:8, dch * 128:(dch + 1) * 128],
                        C.Wfmb[0:8, g4 * 512:(g4 + 1) * 512],
                        start=False, stop=True))
                tmm = P.inc("tensor")
                P.wait("vector", tmm)
                P.wait("vector", t_wo_dma[dch % 2])
                P.op("vector", lambda v, dch=dch, po=po: v.tensor_scalar(
                    C.t_sb1(dch % 2), po[:, 0:512], C.bout1[:, dch:dch + 1],
                    None, ALU.add))
                tev = P.inc("vector")
                t_wout_used[buf] = tev
                P.wait("gpsimd", tev)
                tD = P.dma("gpsimd",
                           C.stream_dram[:, dch * EXT + lo: dch * EXT + lo + 512],
                           C.t_sb1(dch % 2), accum_op=ALU.add)
                t_wo_dma[dch % 2] = tD
            t_afm_used = P.tok("tensor")
    C.t_attn1 = P.qtok("gpsimd")


def emit_block1(C, with_b2=True):
    emit_ln1(C)
    C.t_rl_ev = None
    C.t_tr_prev = None
    C.P.op("vector", lambda v: v.memset(C.lp[:, 4:5], 0.0))
    C.P.inc("vector")
    emit_qkv_phi_v(C)
    emit_block1_attn(C, with_b2=with_b2)
    C.t_moe_done = C.t_attn1


def emit_w2_convert_step(C, i):
    P = C.P
    e, j = i // 16, i % 16
    if j < 8:
        P.dma("gpsimd", C.w2bf[e, j],
              C.wd["w2"][e, :, j * 128:(j + 1) * 128]
              .rearrange("(c p) m -> p c m", p=128))
    else:
        P.dma("gpsimd", C.w1bf[e, j - 8],
              C.wd["w1"][e, (j - 8) * 128:(j - 7) * 128, :])
    C.t_conv[e] = P.qtok("gpsimd")


def emit_moe(C):
    P = C.P
    t_blk1_pe = ("tensor", P.cnt["tensor"])
    t_blk1_ve = ("vector", P.cnt["vector"])
    P.wait("gpsimd", t_blk1_pe)
    P.wait("gpsimd", t_blk1_ve)
    P.wait("gpsimd", C.t_attn1)
    P.wait("sync", t_blk1_pe)
    P.wait("sync", t_blk1_ve)
    P.wait("sync", C.t_attn1)

    t_w1_read = None
    t_w2_used = [None, None]
    t_h_used = {}
    t_hps = [None, None, None, None]
    t_out_ev = [None, None]
    t_acc_dma = [None, None]
    t_wrep_used = [None, None]
    t_wrep_dma = [C.t_attn1, C.t_attn1]
    P.wait("gpsimd", C.t_conv[0])
    for kc in range(8):
        P.dma("gpsimd", C.w1sb(kc)[:], C.w1bf[0, kc])
    t_w1 = P.qtok("gpsimd")
    for e in range(8):
        for tt in range(4):
            lo = 3 + tt * 512
            wr = C.w_rep(tt % 2)
            P.wait("tensor", C.t_router)
            P.wait("tensor", t_wrep_used[tt % 2])
            P.wait("tensor", getattr(C, "t_wrep_copy", None))
            P.op("tensor", lambda T, e=e, tt=tt: T.matmul(
                C.pD[:, 0:512], C.sel8[:, e * 128:(e + 1) * 128],
                C.Wfmb[0:8, tt * 512:(tt + 1) * 512], start=True, stop=True))
            tmm = P.inc("tensor")
            P.wait("vector", tmm)
            P.wait("vector", t_wrep_dma[tt % 2])
            P.op("vector", lambda v, wr=wr: v.tensor_copy(wr, C.pD[:, 0:512]))
            t_wrep = P.inc("vector")
            C.t_wrep_copy = t_wrep
            t_h_evac = {}
            for hch in range(32):
                hb = hch % 4
                pa = (C.pA[:, 0:512], C.pA[:, 512:1024], C.pC[:, 0:512],
                      C.pE[:, 0:512])[hb]
                P.wait("tensor", t_w1)
                P.wait("tensor", C.t_xn1)
                P.wait("tensor", t_hps[hb])
                for kc in range(8):
                    P.op("tensor", lambda T, kc=kc, hch=hch, lo=lo, pa=pa: T.matmul(
                        pa, C.w1sb(kc)[:, hch * 128:(hch + 1) * 128],
                        C.xn1(kc, lo, 512), start=(kc == 0), stop=(kc == 7)))
                tmm = P.inc("tensor")
                P.wait("scalar", tmm)
                P.wait("scalar", t_h_used.get(hch))
                P.op("scalar", lambda s, e=e, hch=hch, pa=pa: s.activation(
                    C.h_sb(hch, 512), pa, AF.Gelu,
                    bias=C.b1sb[:, e * 32 + hch: e * 32 + hch + 1]))
                t_hps[hb] = P.inc("scalar")
                t_h_evac[hch] = t_hps[hb]
            if tt == 1 and e + 3 < 8:
                for j in range(16):
                    emit_w2_convert_step(C, (e + 3) * 16 + j)
            if tt == 3:
                t_w1_read = ("tensor", P.cnt["tensor"])
                if e + 1 < 8:
                    P.wait("gpsimd", t_w1_read)
                    P.wait("gpsimd", C.t_conv[e + 1])
                    for kc in range(8):
                        P.dma("gpsimd", C.w1sb(kc)[:], C.w1bf[e + 1, kc])
                    t_w1 = P.qtok("gpsimd")
            for dch in range(8):
                db = dch % 2
                P.wait("sync", t_w2_used[db])
                P.wait("sync", C.t_conv[e])
                P.dma("sync", C.w2buf[db][:], C.w2bf[e, dch])
                tw2 = P.qtok("sync")
                pb = C.pB[:, db * 512:db * 512 + 512]
                P.wait("tensor", tw2)
                P.wait("tensor", t_out_ev[db])
                for hch in range(32):
                    P.wait("tensor", t_h_evac[hch])
                    P.op("tensor", lambda T, hch=hch, db=db, pb=pb: T.matmul(
                        pb, C.w2buf[db][:, hch * 128:(hch + 1) * 128],
                        C.h_sb(hch, 512), start=(hch == 0), stop=(hch == 31)))
                tmm = P.inc("tensor")
                t_w2_used[db] = tmm
                P.wait("vector", tmm)
                P.wait("vector", t_wrep)
                P.wait("vector", t_acc_dma[db])
                P.op("vector", lambda v, db=db, pb=pb, wr=wr: v.tensor_tensor(
                    C.t_sb(db), pb, wr, ALU.mult))
                tev = P.inc("vector")
                t_out_ev[db] = tev
                P.wait("gpsimd", tev)
                P.dma("gpsimd",
                      C.stream_dram[:, dch * EXT + lo: dch * EXT + lo + 512],
                      C.t_sb(db), accum_op=ALU.add)
                t_acc_dma[db] = P.qtok("gpsimd")
            for hch in range(32):
                t_h_used[hch] = ("tensor", P.cnt["tensor"])
            t_wrep_used[tt % 2] = ("tensor", P.cnt["tensor"])
            t_wrep_dma[tt % 2] = ("vector", P.cnt["vector"])
    C.t_moe_done = P.qtok("gpsimd")


def kernel(**inputs):
    C = build(phases=("b0", "b1", "moe"))
    feeds = [load_feeds(C, inputs, c) for c in range(8)]
    res = run_bass_kernel_spmd(C.nc, feeds, core_ids=list(range(8)))
    out = np.zeros((4, 4096, 1024), np.float32)
    for c in range(8):
        b, s = c // 2, c % 2
        out[b, s * OWN:(s + 1) * OWN] = res.results[c]["y"].T
    return out


# version 18
# speedup vs baseline: 1.0560x; 1.0560x over previous
import numpy as np
import ml_dtypes
import concourse.bass as bass
import concourse.mybir as mybir
from concourse.bass_utils import run_bass_kernel_spmd
from contextlib import ExitStack

F32 = mybir.dt.float32
BF16 = mybir.dt.bfloat16
AF = mybir.ActivationFunctionType
ALU = mybir.AluOpType
AX = mybir.AxisListType.X
BF = ml_dtypes.bfloat16

B_, SQ, D = 4, 4096, 1024
OWN, HALO = 2048, 3
EXT = OWN + 2 * HALO
NTT = 17
WINDOWS = [(0, 512), (512, 512), (1024, 512), (1536, 512), (2048, 6)]
OWNW = [(3, 512), (515, 512), (1027, 512), (1539, 512)]
ENGINES = ("sync", "tensor", "vector", "scalar", "gpsimd")


def twidth(t):
    return 6 if t == 16 else 128


QPOOL = {"sync": 6, "gpsimd": 6, "scalar": 4, "vector": 2, "tensor": 2}


class Prog:
    def __init__(self, nc):
        self.nc = nc
        self.recs = {e: [] for e in ENGINES}
        self.cnt = {}
        self.sem_names = []
        self.qpool = {}
        self.qrr = {e: 0 for e in ENGINES}
        for e in ENGINES:
            self.cnt[e] = 0
            self.sem_names.append(e)
            self.qpool[e] = []
            for i in range(QPOOL[e]):
                n = f"q_{e}{i}"
                self.cnt[n] = 0
                self.sem_names.append(n)
                self.qpool[e].append(n)

    def op(self, eng, fn, tok=False):
        self.recs[eng].append(["op", fn, []])
        if tok:
            return self.inc(eng)
        return None

    def inc(self, eng, sem=None, n=1):
        name = sem or eng
        assert self.recs[eng] and self.recs[eng][-1][0] == "op"
        self.cnt[name] += n
        self.recs[eng][-1][2].append((name, n))
        return (name, self.cnt[name])

    def tok(self, eng):
        return (eng, self.cnt[eng])

    def wait(self, eng, tok):
        if tok is None:
            return
        if isinstance(tok, list):
            for t in tok:
                self.wait(eng, t)
            return
        if tok[1] <= 0:
            return
        self.recs[eng].append(["wait", tok[0], tok[1]])

    def dma(self, eng, out, in_, **kw):
        def fn(E, out=out, in_=in_, kw=kw):
            return E.dma_start(out=out, in_=in_, **kw)
        pool = self.qpool[eng]
        name = pool[self.qrr[eng] % len(pool)]
        self.qrr[eng] += 1
        if self.cnt[name] > 0:
            self.recs[eng].append(["wait", name, self.cnt[name]])
        self.recs[eng].append(["op", fn, []])
        return self.inc(eng, name, 16)

    def qtok(self, eng):
        return [(n, self.cnt[n]) for n in self.qpool[eng] if self.cnt[n] > 0]

    def wait_q_all(self, eng, queue_eng):
        for t in self.qtok(queue_eng):
            self.wait(eng, t)

    def emit(self, es):
        nc = self.nc
        sems = {}
        for name in self.sem_names:
            if self.cnt[name] > 0:
                sems[name] = es.enter_context(nc.semaphore(name))
        block = es.enter_context(nc.Block())
        for eng in ENGINES:
            recs = self.recs[eng]
            if not recs:
                continue

            def section(E, recs=recs, sems=sems):
                seen = {}
                for r in recs:
                    if r[0] == "wait":
                        if seen.get(r[1], -1) >= r[2]:
                            continue
                        seen[r[1]] = r[2]
                        E.wait_ge(sems[r[1]], r[2])
                    else:
                        ins = r[1](E)
                        for (name, n) in r[2]:
                            ins.then_inc(sems[name], n)
            getattr(block, eng)(section)



def host_constants():
    c = {}
    c["eye_f"] = np.eye(128, dtype=np.float32)
    c["eye_b"] = np.eye(128, dtype=BF)
    c["ones_lhs"] = np.ones((1, 128), dtype=BF)
    c["ones_rhs"] = np.ones((128, 1), dtype=BF)
    c["ones_row"] = np.ones((1, 512), dtype=BF)
    p1a = np.zeros((16, 128), dtype=np.float32)
    p1b = np.zeros((16, 128), dtype=np.float32)
    p2 = np.zeros((16, 128), dtype=np.float32)
    s = 2.0 ** -0.5
    for m in range(128):
        p1a[m // 16, m] = s
        p1b[(m + 128) // 16, m] = s
        p2[m % 16, m] = 1.0
    c["P1a"] = p1a.astype(BF)
    c["P1b"] = p1b.astype(BF)
    c["P2"] = p2.astype(BF)
    bA = np.zeros((128, 128), dtype=np.float32)
    for j in range(128):
        bA[j, max(0, j - 6):j + 1] = 1.0
    bB = np.zeros((6, 128), dtype=np.float32)
    for j in range(6):
        bB[j, 122 + j:128] = 1.0
    c["bandA"] = bA.astype(BF)
    c["bandB"] = bB.astype(BF)
    sel = np.zeros((4, 4 * 128), dtype=np.float32)
    for r in range(4):
        sel[r, r * 128:(r + 1) * 128] = 1.0
    c["sel4"] = sel.astype(BF)
    c["cvec"] = np.tile(np.array([[1e-5, 1e-8, 1.0, 0.0]], dtype=np.float32), (128, 1))
    sel8 = np.zeros((8, 8 * 128), dtype=np.float32)
    for r in range(8):
        sel8[r, r * 128:(r + 1) * 128] = 1.0
    c["sel8"] = sel8.astype(BF)
    return c


def host_weights(inp):
    w = {}
    f32 = np.float32
    A = np.ascontiguousarray
    w["wq_r"] = A(inp["L0_attn_w_q"].reshape(8, 128, 128).transpose(1, 0, 2))
    w["wkv_r"] = A(inp["L0_attn_w_kv"].reshape(8, 128, 256).transpose(1, 0, 2))
    w["wout0"] = A(inp["L0_attn_w_out"]).astype(f32)
    w["gamma_r"] = A(np.asarray(inp["L0_attn_gamma"]).reshape(8, 128).T)
    w["win_r"] = A(inp["L0_ffn_Win"].reshape(8, 128, 64).transpose(1, 0, 2))
    W1 = np.asarray(inp["L0_ffn_W1"])
    Bm = np.zeros((64, 1024), dtype=f32)
    for m in range(64):
        Bm[m, m * 16:(m + 1) * 16] = W1[m]
    w["Bmat"] = Bm
    W2 = np.asarray(inp["L0_ffn_W2"])
    Cm = np.zeros((1024, 64), dtype=f32)
    for m in range(64):
        Cm[m * 16:(m + 1) * 16, m] = W2[m]
    w["Cmat_r"] = A(Cm.reshape(8, 128, 64).transpose(1, 0, 2))
    w["woutf"] = A(inp["L0_ffn_Wout"]).astype(f32)
    w["bin_c"] = A(inp["L0_ffn_bin_"].reshape(64, 1))
    w["b1flat_r"] = A(inp["L0_ffn_b1"].reshape(1024).reshape(8, 128).T)
    w["b2f_c"] = A(inp["L0_ffn_b2"].reshape(64, 1))
    w["bout_r"] = A(inp["L0_ffn_bout"].reshape(8, 128).T)
    wqkv = np.asarray(inp["L1_attn_w_qkv"])
    w["wqk_r"] = A(wqkv[:, 0:2048].reshape(8, 128, 2048).transpose(1, 0, 2))
    w["wv_r"] = A(wqkv[:, 2048:3072].reshape(8, 128, 1024).transpose(1, 0, 2))
    w["proj"] = A(inp["L1_attn_proj"]).astype(f32)
    w["wout1_r"] = A(inp["L1_attn_w_out"].reshape(8, 128, 1024).transpose(1, 0, 2))
    w["bout1_r"] = A(inp["L1_attn_b_out"].reshape(8, 128).T)
    wc = np.concatenate([np.asarray(inp["L1_ffn_router_w"]).T,
                         np.asarray(inp["L1_ffn_lang_w"]).T], axis=1)
    w["wcomb_r"] = A(wc.reshape(8, 128, 12).transpose(1, 0, 2))
    w["rbias_c"] = A(np.concatenate([np.asarray(inp["L1_ffn_router_b"]),
                                     np.asarray(inp["L1_ffn_lang_b"])]).reshape(12, 1))
    w["w1"] = np.asarray(inp["L1_ffn_w1"])
    w["w2"] = np.asarray(inp["L1_ffn_w2"])
    w["b1_r"] = A(np.asarray(inp["L1_ffn_b1"]).reshape(8, 32, 128).transpose(0, 2, 1))
    w["b2m"] = np.asarray(inp["L1_ffn_b2"])
    return w


WDEFS = [
    ("wq_r", [128, 8, 128]), ("wkv_r", [128, 8, 256]), ("wout0", [128, 1024]),
    ("gamma_r", [128, 8]), ("win_r", [128, 8, 64]), ("Bmat", [64, 1024]),
    ("Cmat_r", [128, 8, 64]), ("woutf", [64, 1024]), ("bin_c", [64, 1]),
    ("b1flat_r", [128, 8]), ("b2f_c", [64, 1]), ("bout_r", [128, 8]),
    ("wqk_r", [128, 8, 2048]), ("wv_r", [128, 8, 1024]), ("proj", [128, 64]),
    ("wout1_r", [128, 8, 1024]), ("bout1_r", [128, 8]), ("wcomb_r", [128, 8, 12]),
    ("rbias_c", [12, 1]), ("w1", [8, 1024, 4096]), ("w2", [8, 4096, 1024]),
    ("b1_r", [8, 128, 32]), ("b2m", [8, 1024]),
]


def host_core_slices(x, core):
    b, s = core // 2, core % 2
    base = s * OWN
    idx = np.clip(np.arange(base - HALO, base + OWN + HALO), 0, SQ - 1)
    x_ext = np.ascontiguousarray(np.asarray(x[b])[idx])
    x_part = np.ascontiguousarray(np.asarray(x[b])[(1 - s) * OWN:(2 - s) * OWN])
    left_edge = 0.0 if s == 0 else 1.0
    right_edge = 0.0 if s == 1 else 1.0
    flags = np.tile(np.array([[left_edge, right_edge]], dtype=np.float32), (128, 1))
    return x_ext, x_part, flags



class Ctx:
    pass


def make_ctx(phases=("b0", "b1", "moe"), debug=()):
    nc = bass.Bass("TRN2", target_bir_lowering=False)
    C = Ctx()
    C.nc = nc
    C.P = Prog(nc)
    C.es = ExitStack()
    C.phases = phases
    C.debug = list(debug)

    C.t_conv = [None] * 8
    dram = lambda n, s, d, k: nc.dram_tensor(n, s, d, kind=k).ap()
    C.in_x = dram("x_ext", [EXT, D], F32, "ExternalInput")
    C.in_xp = dram("x_part", [OWN, D], F32, "ExternalInput")
    C.in_flags = dram("d_flags", [128, 2], F32, "ExternalInput")
    C.dc = {}
    for n, v in host_constants().items():
        dt = F32 if v.dtype == np.float32 else BF16
        C.dc[n] = dram("d_" + n, list(v.shape), dt, "ExternalInput")
    C.wd = {n: dram("d_" + n, s, F32, "ExternalInput") for n, s in WDEFS}
    C.out_y = dram("y", [D, OWN], F32, "ExternalOutput")
    C.stream_dram = dram("sdram", [128, 8 * EXT], F32, "Internal")
    C.w2bf = nc.dram_tensor("w2bf", [8, 8, 128, 4096], BF16, kind="Internal").ap()
    C.w1bf = nc.dram_tensor("w1bf", [8, 8, 128, 4096], BF16, kind="Internal").ap()
    C.dbg = {}
    for name, shape in debug:
        C.dbg[name] = dram("dbg_" + name, list(shape), F32, "ExternalOutput")

    sb = lambda n, s, d: C.es.enter_context(nc.sbuf_tensor(n, s, d))
    psa = lambda n, s, d: C.es.enter_context(nc.psum_tensor(n, s, d))

    C.pA = psa("pA", [128, 1024], F32)
    C.pB = psa("pB", [128, 1024], F32)
    C.pC = psa("pC", [128, 512], F32)
    C.pD = psa("pD", [128, 512], F32)
    C.pE = psa("pE", [128, 512], F32)
    C.pF = psa("pF", [128, 1024], BF16)
    C.kvslot = [("pC", 0, 0), ("pC", 64, 0), ("pD", 0, 0), ("pD", 64, 0),
                ("pA", 0, 512), ("pA", 64, 512), ("pB", 0, 512), ("pB", 64, 512)]

    C.R1 = sb("R1", [128, 16432], F32)
    C.R1b = C.R1.ap().bitcast(BF16)
    C.R2 = sb("R2", [128, 16432], BF16)
    C.R2f = C.R2.ap().bitcast(F32)
    C.R3 = sb("R3", [128, 16432], BF16)
    C.R3f = C.R3.ap().bitcast(F32)
    C.U3 = sb("U3", [128, 6144], BF16)
    C.U3f = C.U3.ap().bitcast(F32)
    C.U5 = sb("U5", [128, 7168], BF16)
    C.U5f = C.U5.ap().bitcast(F32)

    C.eye_f = sb("eye_f", [128, 128], F32)
    C.eye_b = sb("eye_b", [128, 128], BF16)
    C.ones_lhs = sb("ones_lhs", [1, 128], BF16)
    C.ones_rhs = sb("ones_rhs", [128, 1], BF16)
    C.ones_row = sb("ones_row", [1, 512], BF16)
    C.P1a = sb("P1a", [16, 128], BF16)
    C.P1b = sb("P1b", [16, 128], BF16)
    C.P2 = sb("P2", [16, 128], BF16)
    C.bandA = sb("bandA", [128, 128], BF16)
    C.bandB = sb("bandB", [6, 128], BF16)
    C.sel4 = sb("sel4", [4, 4 * 128], BF16)
    C.cbias = sb("cbias", [128, 4], F32)
    C.sel8 = sb("sel8", [8, 8 * 128], BF16)
    C.Bmat = sb("Bmat", [64, 1024], BF16)
    C.Cmat = sb("Cmat", [128, 8 * 64], BF16)
    C.woutf = sb("woutf", [64, 1024], BF16)
    C.gamma = sb("gamma", [128, 8], F32)
    C.bin_c = sb("bin_c", [64, 1], F32)
    C.b1flat = sb("b1flat", [128, 8], F32)
    C.b2f = sb("b2f", [64, 1], F32)
    C.bout = sb("bout", [128, 8], F32)
    C.proj = sb("proj", [128, 64], BF16)
    C.bout1 = sb("bout1", [128, 8], F32)
    C.wcomb = sb("wcomb", [128, 8 * 12], BF16)
    C.rbias = sb("rbias", [12, 1], F32)
    C.b1sb = sb("b1sb", [128, 8 * 32], F32)
    C.b2sb = sb("b2sb", [8, 1024], BF16)
    C.flags = sb("flags", [128, 2], F32)
    C.stat6 = sb("stat6", [128, 6 * NTT], F32)
    C.bn12 = sb("bn12", [128, 12], F32)
    C.stat6fm = sb("stat6fm", [4, EXT], BF16)
    C.lnA = sb("lnA", [1, EXT], BF16)
    C.lnC = sb("lnC", [1, EXT], BF16)
    C.kvA = sb("kvA", [128, 8 * 17], BF16)
    C.kvB = sb("kvB", [128, 8 * 17], BF16)
    C.kvC = sb("kvC", [17, 8 * 17], BF16)
    C.Wfmb = sb("Wfmb", [8, OWN], BF16)
    C.sc1 = sb("sc1", [128, 8], F32)
    C.sc2 = sb("sc2", [128, 16], F32)
    C.lp = sb("lp", [128, 5], F32)
    C.rt = sb("rt", [128, 12], F32)
    C.sr = sb("sr", [1, 3 * 512], F32)
    C.rl = sb("rl", [12, 128], F32)
    C.qinv = sb("qinv", [128, 8], F32)

    C.stream = lambda ci, lo, w: C.R1[:, ci * EXT + lo: ci * EXT + lo + w]
    C.xn1 = lambda ci, lo, w: C.R2[:, ci * EXT + lo: ci * EXT + lo + w]
    C.v2 = lambda ci, lo, w: C.R3[:, ci * EXT + lo: ci * EXT + lo + w]
    C.xtok = lambda i: C.R2f[:, i * 1024:(i + 1) * 1024]
    C.u_part = lambda i: C.R2[:, 4096 + i * 1024: 4096 + (i + 1) * 1024]
    C.up_fm = lambda i: C.R2[:, 6144 + i * 1024: 6144 + (i + 1) * 1024]
    C.kv_tok = lambda i: C.R2[:, 8192 + i * 256: 8192 + (i + 1) * 256]
    C.kexp_s = [C.R2[:, 8704:10888], C.R2[:, 4096:6280]]
    C.v_ext_s = [C.R2[:, 10888:11024], C.R2[:, 6280:6416]]
    C.kexp = C.kexp_s[0]
    C.v_ext = C.v_ext_s[0]
    C.wkv = C.R2[:, 11024:13072]
    C.wq = C.R2[:, 13072:14096]
    C.wout0 = C.R2[:, 14096:15120]
    C.win = C.R2[:, 15120:15632]
    C.v1 = lambda kc, w: C.U3[:, kc * 512: kc * 512 + w]
    C.s_sb = C.U3[0:64, 4096:4608]
    C.xbf = lambda kc, w: C.U3[:, kc * 512: kc * 512 + w]
    C.w2buf = [C.U3[:, 0:4096], C.U5[:, 2048:6144]]
    C.kv_acc = C.U5f[0:17, 1024:3208]
    C.qhead = lambda h: C.U5[0:17, h * 512:(h + 1) * 512]
    C.qf_sb = C.U5[:, 4096:4608]
    C.rsb = C.U5[:, 4608:5120]
    C.qeA = C.U5[:, 5120:5632]
    C.qeB = C.U5[:, 5632:6144]
    C.atok0 = C.U5f[:, 3072:3200]
    C.afm0 = C.U5[:, 6400:6912]
    C.stA = lambda i: C.U5[:, i * 128:(i + 1) * 128]
    C.stB = lambda i: C.U5[0:6, 512 + i * 128: 640 + i * 128]
    C.atok1 = C.U5[:, 1024:2048]
    C.afm1 = lambda kc, w: C.U5[:, 2048 + kc * 512: 2048 + kc * 512 + w]
    C.qkf = lambda i: C.U5[:, 6144 + i * 512: 6144 + (i + 1) * 512]
    C.t_sb1 = lambda i: C.U3f[:, 1024 + i * 512: 1536 + i * 512]
    C.t_sb = lambda i: C.U5f[:, i * 512:(i + 1) * 512]
    C.kphi = lambda kp, lo, w: C.R1b[:, kp * EXT + lo: kp * EXT + lo + w]
    C.qphi = lambda kp, lo, w: C.R1b[:, 8216 + kp * EXT + lo: 8216 + kp * EXT + lo + w]
    C.Vsb = lambda i: C.R1b[:, 16432 + i * 1032: 16432 + (i + 1) * 1032]
    C.wv = lambda kc: C.R1b[:, 19528 + kc * 1024: 19528 + (kc + 1) * 1024]
    C.wqk_rot = lambda i: C.R1b[:, 27720 + i * 1024: 27720 + (i + 1) * 1024]
    C.wout1_rot = lambda i: C.R1b[:, 27720 + i * 1024: 27720 + (i + 1) * 1024]
    C.w1sb = lambda kc: C.R1b[:, kc * 4096:(kc + 1) * 4096]
    C.h_sb = lambda hch, w: C.R3[:, hch * 512: hch * 512 + w]
    C.w_rep = lambda i: C.U3f[:, 2048 + i * 512: 2048 + (i + 1) * 512]
    return C


def load_feeds(C, inp, core):
    feeds = {}
    x_ext, x_part, flags = host_core_slices(inp["L0_attn_x"], core)
    feeds["x_ext"] = x_ext
    feeds["x_part"] = x_part
    feeds["d_flags"] = flags
    for n, v in host_constants().items():
        feeds["d_" + n] = v
    hw = host_weights(inp)
    for n, _ in WDEFS:
        feeds["d_" + n] = np.ascontiguousarray(hw[n], dtype=np.float32)
    return feeds



def emit_setup(C):
    P = C.P
    for name, dst in [("eye_f", C.eye_f), ("eye_b", C.eye_b), ("ones_lhs", C.ones_lhs),
                      ("ones_rhs", C.ones_rhs), ("ones_row", C.ones_row),
                      ("P1a", C.P1a), ("P1b", C.P1b), ("P2", C.P2),
                      ("bandA", C.bandA), ("bandB", C.bandB), ("sel4", C.sel4),
                      ("cvec", C.cbias), ("sel8", C.sel8)]:
        P.dma("sync", dst[:], C.dc[name])
    for dst, src in [
        (C.wq, C.wd["wq_r"].rearrange("p c m -> p (c m)")),
        (C.wkv, C.wd["wkv_r"].rearrange("p c m -> p (c m)")),
        (C.wout0, C.wd["wout0"]),
        (C.win, C.wd["win_r"].rearrange("p c m -> p (c m)")),
        (C.Bmat[:], C.wd["Bmat"]),
        (C.Cmat[:], C.wd["Cmat_r"].rearrange("p c m -> p (c m)")),
        (C.woutf[:], C.wd["woutf"]),
        (C.proj[:], C.wd["proj"]),
        (C.wcomb[:], C.wd["wcomb_r"].rearrange("p c m -> p (c m)")),
        (C.b2sb[:], C.wd["b2m"]),
    ]:
        P.dma("gpsimd", dst, src)
    for dst, src in [
        (C.gamma[:], C.wd["gamma_r"]), (C.bin_c[:], C.wd["bin_c"]),
        (C.b1flat[:], C.wd["b1flat_r"]), (C.b2f[:], C.wd["b2f_c"]),
        (C.bout[:], C.wd["bout_r"]), (C.bout1[:], C.wd["bout1_r"]),
        (C.rbias[:], C.wd["rbias_c"]), (C.flags[:], C.in_flags),
        (C.b1sb[:], C.wd["b1_r"].rearrange("e p c -> p e c")),
    ]:
        P.dma("sync", dst, src)
    C.t_su_sync = P.qtok("sync")
    C.t_su_gp = P.qtok("gpsimd")
    P.wait("vector", C.t_su_sync)
    P.wait("vector", C.t_su_gp)
    for kc in range(8):
        P.op("vector", lambda v, kc=kc: v.tensor_scalar(
            C.wq[:, kc * 128:(kc + 1) * 128], C.wq[:, kc * 128:(kc + 1) * 128],
            C.gamma[:, kc:kc + 1], None, ALU.mult))
        P.op("vector", lambda v, kc=kc: v.tensor_scalar(
            C.wkv[:, kc * 256:(kc + 1) * 256], C.wkv[:, kc * 256:(kc + 1) * 256],
            C.gamma[:, kc:kc + 1], None, ALU.mult))
    C.t_gamma = P.inc("vector")


def emit_stats_tile(C, x, t, col0, w, gate_pool, rows4=True):
    P = C.P
    st = C.stat6[:, 6 * t: 6 * t + 6]
    P.op("vector", lambda v: v.bn_stats(C.bn12[:, 0:6], x[:, 0:512]))
    P.op("vector", lambda v: v.bn_stats(C.bn12[:, 6:12], x[:, 512:1024]))
    t_sq = P.inc("vector")
    P.wait("vector", t_sq)
    P.op("vector", lambda v: v.bn_aggr(st[:, 4:6], C.bn12[:]))
    tv = P.inc("vector")
    P.wait("vector", tv)
    P.op("vector", lambda v: v.tensor_scalar(st[:, 5:6], st[:, 5:6], 1e-30, None, ALU.max))
    tvar = P.inc("vector")
    P.wait("scalar", tvar)
    P.op("scalar", lambda s: s.activation(C.sc2[:, 1:2], st[:, 5:6], AF.Ln, bias=C.cbias[:, 0:1]))
    P.op("scalar", lambda s: s.activation(C.sc2[:, 2:3], st[:, 5:6], AF.Ln))
    ts = P.inc("scalar")
    P.wait("scalar", ts)
    P.op("scalar", lambda s: s.activation(st[:, 0:1], C.sc2[:, 1:2], AF.Exp, scale=-0.5))
    P.op("scalar", lambda s: s.activation(st[:, 2:3], C.sc2[:, 2:3], AF.Exp, scale=-0.5))
    ts = P.inc("scalar")
    P.wait("vector", ts)
    P.op("vector", lambda v: v.tensor_scalar(C.sc2[:, 3:4], st[:, 4:5], -1.0, None, ALU.mult))
    tv = P.inc("vector")
    P.wait("vector", tv)
    P.op("vector", lambda v: v.tensor_scalar(st[:, 1:2], C.sc2[:, 3:4], st[:, 0:1], None, ALU.mult))
    P.op("vector", lambda v: v.tensor_scalar(st[:, 3:4], C.sc2[:, 3:4], st[:, 2:3], None, ALU.mult))
    tv = P.inc("vector")
    if not rows4:
        return t_sq, tv
    P.wait("tensor", tv)
    P.wait("tensor", getattr(C, "t_stats_ev", None))
    P.op("tensor", lambda T: T.transpose(C.pE[0:4, 0:128], st[:, 0:4], C.eye_f[:]))
    tt = P.inc("tensor")
    P.wait("vector", tt)
    P.op("vector", lambda v: v.tensor_copy(C.stat6fm[:, col0:col0 + w], C.pE[0:4, 0:w]))
    te = P.inc("vector")
    C.t_stats_ev = te
    return t_sq, te


def emit_load_x(C):
    P = C.P
    P.wait("vector", C.t_su_sync)
    P.wait("tensor", C.t_su_sync)
    buf_free = [None, None]
    gate_pool = None
    t_pe_prev = [None, None]
    for t in range(NTT):
        buf = t % 2
        w = twidth(t)
        xt = C.xtok(buf)
        if t == 16:
            P.wait("vector", buf_free[buf])
            P.op("vector", lambda v, xt=xt: v.memset(xt[:], 0.0))
            tz = P.inc("vector")
            buf_free[buf] = tz
        P.wait("sync", buf_free[buf])
        tl = P.dma("sync", xt[0:w, :], C.in_x[t * 128: t * 128 + w, :])
        P.wait("vector", tl)
        P.wait("gpsimd", tl)
        pz = (C.pA, C.pB)[t % 2]
        P.wait("tensor", tl)
        P.wait("tensor", t_pe_prev[t % 2])
        for kc in range(8):
            P.op("tensor", lambda T, kc=kc, xt=xt, pz=pz: T.transpose(
                pz[:, kc * 128:(kc + 1) * 128], xt[:, kc * 128:(kc + 1) * 128],
                C.eye_f[:]))
        tt = P.inc("tensor")
        gate_pool, t_stats = emit_stats_tile(C, xt, t, t * 128, w, gate_pool)
        for z in range(2):
            if 2 * t + z < 34:
                emit_w2_convert_step(C, 2 * t + z)
        P.wait("scalar", tt)
        evtok = None
        for kc in range(8):
            P.op("scalar", lambda sc, kc=kc, t=t, w=w, pz=pz: sc.copy(
                C.stream(kc, t * 128, w), pz[:, kc * 128: kc * 128 + w]))
            evtok = P.inc("scalar")
        buf_free[buf] = evtok
        t_pe_prev[t % 2] = evtok
    C.t_xload = [P.tok("scalar"), P.tok("vector")]
    C.t_stats_done = C.t_xload


def emit_vphase(C):
    P = C.P


def emit_v_window(C, lo, w, which, gate_mm=None, gate_rep=None):
    P = C.P
    r0 = 2 * which
    selA = C.sel4[:, r0 * 128:(r0 + 1) * 128]
    selC = C.sel4[:, (r0 + 1) * 128:(r0 + 2) * 128]
    P.wait("tensor", gate_mm)
    P.op("tensor", lambda T: T.matmul(C.pB[:, 0:w], selA, C.stat6fm[:, lo:lo + w],
                                      start=True, stop=True))
    P.op("tensor", lambda T: T.matmul(C.pB[:, 512:512 + w], selC, C.stat6fm[:, lo:lo + w],
                                      start=True, stop=True))
    tmm = P.inc("tensor")
    P.wait("vector", tmm)
    P.wait("vector", gate_rep)
    P.op("vector", lambda v: v.tensor_copy(C.rsb[:, 0:w], C.pB[:, 0:w]))
    P.op("vector", lambda v: v.tensor_copy(C.qf_sb[:, 0:w], C.pB[:, 512:512 + w]))
    trep = P.inc("vector")
    toks = []
    for kc in range(8):
        dst = C.v1(kc, w) if which == 0 else C.v2(kc, lo, w)
        P.wait("vector", trep)
        P.op("vector", lambda v, kc=kc, dst=dst, lo=lo, w=w: v.tensor_tensor(
            dst, C.stream(kc, lo, w), C.rsb[:, 0:w], ALU.mult))
        tv = P.inc("vector")
        P.wait("gpsimd", tv)
        P.op("gpsimd", lambda g, kc=kc, dst=dst, w=w: g.tensor_tensor(
            dst, dst, C.qf_sb[:, 0:w], ALU.add))
        toks.append(P.inc("gpsimd"))
    return tmm, trep, toks[-1]


def emit_ffn0_window(C, wi, lo, w, t_v1):
    P = C.P
    h0 = C.U5[:, 0:512]
    h1 = C.U5[:, 512:1024]
    g_sb = C.U5[0:64, 1024:1536]
    P.wait("tensor", t_v1)
    P.wait("tensor", getattr(C, "t_ffn_out_prev", None))
    for kc in range(8):
        P.op("tensor", lambda T, kc=kc: T.matmul(
            C.pA[0:64, 0:w], C.win[0:128, kc * 64:(kc + 1) * 64], C.v1(kc, w),
            start=(kc == 0), stop=(kc == 7)))
    tsm = P.inc("tensor")
    P.wait("scalar", tsm)
    P.op("scalar", lambda s: s.activation(C.s_sb[0:64, 0:w], C.pA[0:64, 0:w],
                                          AF.Identity, bias=C.bin_c[0:64, 0:1]))
    t_s = P.inc("scalar")
    t_h_ev = [None, None]
    P.wait("tensor", t_s)
    for mch in range(8):
        hb = (h0, h1)[mch % 2]
        pb = C.pB[:, (mch % 2) * 512:(mch % 2) * 512 + w]
        P.wait("tensor", t_h_ev[mch % 2])
        P.op("tensor", lambda T, mch=mch, pb=pb: T.matmul(
            pb, C.Bmat[0:64, mch * 128:(mch + 1) * 128], C.s_sb[0:64, 0:w],
            start=True, stop=True))
        tmm = P.inc("tensor")
        P.wait("scalar", tmm)
        P.op("scalar", lambda s, mch=mch, hb=hb, pb=pb: s.activation(
            hb[:, 0:w], pb, AF.Relu, bias=C.b1flat[:, mch:mch + 1]))
        the = P.inc("scalar")
        t_h_ev[mch % 2] = the
        P.wait("tensor", the)
        P.op("tensor", lambda T, mch=mch, hb=hb: T.matmul(
            C.pE[0:64, 0:w], C.Cmat[:, mch * 64:(mch + 1) * 64], hb[:, 0:w],
            start=(mch == 0), stop=(mch == 7)))
        tg = P.inc("tensor")
        t_h_ev[mch % 2] = tg
    P.wait("scalar", tg)
    P.op("scalar", lambda s: s.activation(g_sb[0:64, 0:w], C.pE[0:64, 0:w],
                                          AF.Identity, bias=C.b2f[0:64, 0:1]))
    t_gs = P.inc("scalar")
    P.wait("tensor", t_gs)
    t_o_ev = [None, None]
    last = None
    for dch in range(8):
        po = C.pA[:, (dch % 2) * 512:(dch % 2) * 512 + w]
        P.wait("tensor", t_o_ev[dch % 2])
        P.op("tensor", lambda T, dch=dch, po=po: T.matmul(
            po, C.woutf[0:64, dch * 128:(dch + 1) * 128], g_sb[0:64, 0:w],
            start=True, stop=True))
        tmm = P.inc("tensor")
        P.wait("vector", tmm)
        P.op("vector", lambda v, dch=dch, po=po, lo=lo, w=w: v.scalar_tensor_tensor(
            C.stream(dch, lo, w), po, C.bout[:, dch:dch + 1], C.stream(dch, lo, w),
            ALU.add, ALU.add))
        tev = P.inc("vector")
        t_o_ev[dch % 2] = tev
        last = tev
    C.t_ffn_out_prev = last
    return last


def emit_kside_tile(C, fm_lhs, i, n_tiles_done, first, last, gate, slot=0):
    P = C.P
    kvt = C.kv_tok(i % 2)
    pe = (C.pE[:, 0:256], C.pE[:, 256:512])[i % 2]
    if not hasattr(C, "t_kvtok_ev2"):
        C.t_kvtok_ev2 = [None, None]
    P.wait("tensor", C.t_kvtok_ev2[i % 2])
    for kc in range(8):
        P.op("tensor", lambda T, kc=kc, pe=pe: T.matmul(
            pe, fm_lhs(kc), C.wkv[:, kc * 256:(kc + 1) * 256],
            start=(kc == 0), stop=(kc == 7)))
    tkv = P.inc("tensor")
    P.wait("scalar", tkv)
    P.op("scalar", lambda sc, kvt=kvt, pe=pe: sc.copy(kvt[:], pe))
    tev = P.inc("scalar")
    C.t_kvtok_ev2[i % 2] = tev
    kx = C.kexp_s[slot]
    vx = C.v_ext_s[slot]
    for eng in ("vector", "gpsimd"):
        P.wait(eng, gate)
        P.wait(eng, tev)
    def tt_build(e, h, kvt=kvt, kx=kx):
        ko = 273 * h
        e.tensor_tensor(
            kx[:, ko:ko + 256].rearrange("p (i j) -> p i j", j=16),
            kvt[:, 16 * h:16 * h + 16].unsqueeze(2).broadcast_to([128, 16, 16]),
            kvt[:, 16 * h:16 * h + 16].unsqueeze(1).broadcast_to([128, 16, 16]),
            ALU.mult)
    for h in range(4):
        P.op("vector", lambda v, h=h: tt_build(v, h))
    P.op("vector", lambda v, kvt=kvt, kx=kx: v.tensor_copy(
        kx[:].rearrange("p (h j) -> p h j", j=273)[:, :, 256:272],
        kvt[:, 0:128].rearrange("p (h j) -> p h j", j=16)))
    tbv = P.inc("vector")
    for h in range(4, 8):
        P.op("gpsimd", lambda g, h=h: tt_build(g, h))
    P.op("gpsimd", lambda g, kvt=kvt, vx=vx: g.tensor_copy(
        vx[:].rearrange("p (h j) -> p h j", j=17)[:, :, 0:16],
        kvt[:, 128:256].rearrange("p (h j) -> p h j", j=16)))
    tbg = P.inc("gpsimd")
    P.wait("tensor", tbv)
    P.wait("tensor", tbg)
    if first:
        P.wait("tensor", C.t_ks_vgate)
        P.wait("tensor", C.t_ks_sgate)
    for h in range(8):
        bn, bs, co = C.kvslot[h]
        bank = getattr(C, bn)
        P.op("tensor", lambda T, h=h, bank=bank, bs=bs, co=co, kx=kx, vx=vx:
             T.matmul(bank[bs:bs + 17, co:co + 273],
                      vx[:, 17 * h:17 * h + 17], kx[:, 273 * h:273 * h + 273],
                      start=first, stop=last, skip_group_check=True))
    return P.inc("tensor")


def emit_partner(C):
    P = C.P
    P.wait("vector", C.t_v2_done)
    C.t_ks_sgate = ("scalar", P.cnt["scalar"])
    P.op("vector", lambda v: v.memset(C.kv_acc[:], 0.0))
    P.op("vector", lambda v: v.memset(
        C.kexp_s[0][:].rearrange("p (h j) -> p h j", j=273)[:, :, 272:273], 1.0))
    P.op("vector", lambda v: v.memset(
        C.v_ext_s[0][:].rearrange("p (h j) -> p h j", j=17)[:, :, 16:17], 1.0))
    t0 = P.inc("vector")
    C.t_ks_vgate = t0
    C.t_kv_zero = t0
    buf_free = [C.t_xload, C.t_xload]
    C.t_up_r = [None, None]
    C.t_uf_r = [None, None]
    C.t_upfm_ev = None
    gate_pool = ("tensor", P.cnt["tensor"])
    gate_kexp = t0
    for i in range(16):
        buf = i % 2
        xp = C.xtok(buf)
        P.wait("sync", buf_free[buf])
        tl = P.dma("sync", xp[:], C.in_xp[i * 128:(i + 1) * 128, :])
        P.wait("vector", tl)
        P.wait("gpsimd", tl)
        gate_pool, trc = emit_stats_tile(C, xp, 0, 0, 128, gate_pool, rows4=False)
        if 34 + i < 48:
            emit_w2_convert_step(C, 34 + i)
        up = C.u_part(buf)
        P.wait("scalar", trc)
        P.wait("scalar", C.t_up_r[buf])
        P.op("scalar", lambda s, xp=xp, up=up: s.activation(
            up[:], xp[:], AF.Identity, bias=C.stat6[:, 3:4], scale=C.stat6[:, 2:3]))
        tup = P.inc("scalar")
        P.wait("tensor", tup)
        P.wait("tensor", C.t_upfm_ev)
        for kc in range(8):
            P.op("tensor", lambda T, kc=kc, up=up: T.transpose(
                C.pF[:, kc * 128:(kc + 1) * 128], up[:, kc * 128:(kc + 1) * 128],
                C.eye_b[:]))
        ttr = P.inc("tensor")
        C.t_up_r[buf] = ttr
        P.wait("vector", ttr)
        P.wait("vector", C.t_uf_r[buf])
        uf = C.up_fm(buf)
        P.op("vector", lambda v, uf=uf: v.tensor_copy(uf[:], C.pF[:]))
        tuf = P.inc("vector")
        C.t_upfm_ev = tuf
        buf_free[buf] = tuf
        P.wait("tensor", tuf)
        gate_kexp = emit_kside_tile(
            C, lambda kc, uf=uf: uf[:, kc * 128:(kc + 1) * 128],
            i, i, i == 0, i == 15, gate_kexp)
        C.t_uf_r[buf] = ("tensor", P.cnt["tensor"])
    C.gate_kexp = gate_kexp


def emit_own_kside(C):
    P = C.P
    P.wait("gpsimd", C.gate_kexp)
    P.op("gpsimd", lambda g: g.memset(
        C.kexp_s[1][:].rearrange("p (h j) -> p h j", j=273)[:, :, 272:273], 1.0))
    P.op("gpsimd", lambda g: g.memset(
        C.v_ext_s[1][:].rearrange("p (h j) -> p h j", j=17)[:, :, 16:17], 1.0))
    P.wait("vector", P.inc("gpsimd"))
    gates = [C.gate_kexp, C.gate_kexp]
    for i in range(16):
        lo = 3 + i * 128
        gates[i % 2] = emit_kside_tile(
            C, lambda kc, lo=lo: C.v2(kc, lo, 128), i, 16 + i,
            False, i == 15, gates[i % 2], slot=i % 2)
    C.gate_kexp = gates[15 % 2]
    P.wait("vector", gates[1])
    for h in range(8):
        bn, bs, co = C.kvslot[h]
        bank = getattr(C, bn)
        P.op("vector", lambda v, h=h, bank=bank, bs=bs, co=co: v.tensor_copy(
            C.kv_acc[0:17, 273 * h:273 * h + 273], bank[bs:bs + 17, co:co + 273]))
    C.t_kvacc_ev = P.inc("vector")


def emit_kv_finalize(C):
    P = C.P
    P.wait("tensor", C.t_kvacc_ev)
    for h in range(8):
        base = 273 * h
        for j, dst in ((0, C.kvA), (1, C.kvB)):
            P.op("tensor", lambda T, h=h, base=base, j=j: T.transpose(
                C.pE[:, 0:17] if j == 0 else C.pE[:, 17:34],
                C.kv_acc[0:17, base + j * 128: base + (j + 1) * 128],
                C.eye_f[0:17, 0:17]))
        P.op("tensor", lambda T, h=h, base=base: T.transpose(
            C.pE[0:17, 34:51], C.kv_acc[0:17, base + 256: base + 273],
            C.eye_f[0:17, 0:17]))
        tt = P.inc("tensor")
        P.wait("vector", tt)
        P.op("vector", lambda v, h=h: v.tensor_copy(
            C.kvA[:, 17 * h:17 * h + 17], C.pE[:, 0:17]))
        P.op("vector", lambda v, h=h: v.tensor_copy(
            C.kvB[:, 17 * h:17 * h + 17], C.pE[:, 17:34]))
        P.op("vector", lambda v, h=h: v.tensor_copy(
            C.kvC[0:17, 17 * h:17 * h + 17], C.pE[0:17, 34:51]))
        tev = P.inc("vector")
        P.wait("tensor", tev)
    C.t_kv_ready = P.tok("vector")


def emit_qphase(C):
    P = C.P
    P.wait("sync", C.t_kv_ready)
    for h in range(8):
        P.dma("sync", C.qhead(h)[16:17, 0:512], C.dc["ones_row"])
    t_ones = P.qtok("sync")
    t_wout_ev = [None, None]
    t_qf_dma = None
    t_pcpd_free = None
    for wi, (lo, w) in enumerate(WINDOWS):
        ntc = (w + 127) // 128
        P.wait("tensor", t_wout_ev[0])
        P.wait("tensor", t_wout_ev[1])
        for kc in range(8):
            P.op("tensor", lambda T, kc=kc, lo=lo, w=w: T.matmul(
                C.pA[:, 0:w], C.wq[:, kc * 128:(kc + 1) * 128], C.v2(kc, lo, w),
                start=(kc == 0), stop=(kc == 7)))
        tqm = P.inc("tensor")
        P.wait("scalar", tqm)
        P.wait("scalar", t_qf_dma)
        P.op("scalar", lambda s, w=w: s.activation(C.qf_sb[:, 0:w], C.pA[:, 0:w],
                                                   AF.Identity, scale=0.25))
        tqf = P.inc("scalar")
        P.wait("scalar", t_ones)
        P.wait("scalar", tqf)
        P.wait("scalar", getattr(C, "t_qhead_free", None))
        for h in range(8):
            P.dma("scalar", C.qhead(h)[0:16, 0:w], C.qf_sb[16 * h:16 * h + 16, 0:w])
        t_qf_dma = P.qtok("scalar")
        t_evA = None
        t_evB = None
        t_out_done = []
        for h in range(8):
            P.wait("tensor", t_qf_dma)
            P.wait("tensor", t_evB)
            P.wait("tensor", t_evA)
            P.op("tensor", lambda T, h=h, w=w: T.matmul(
                C.pB[:, 0:w], C.P1a[:], C.qhead(h)[0:16, 0:w], start=True, stop=True))
            P.op("tensor", lambda T, h=h, w=w: T.matmul(
                C.pB[:, 512:512 + w], C.P2[:], C.qhead(h)[0:16, 0:w], start=True, stop=True))
            tra = P.inc("tensor")
            P.wait("vector", tra)
            P.op("vector", lambda v, w=w: v.tensor_copy(C.rsb[:, 0:w], C.pB[:, 0:w]))
            tcp = P.inc("vector")
            P.wait("vector", tcp)
            P.op("vector", lambda v, w=w: v.tensor_tensor(
                C.qeA[:, 0:w], C.rsb[:, 0:w], C.pB[:, 512:512 + w], ALU.mult))
            t_evA = P.inc("vector")
            P.wait("tensor", t_evA)
            P.wait("tensor", t_evB)
            P.op("tensor", lambda T, h=h, w=w: T.matmul(
                C.pB[:, 0:w], C.P1b[:], C.qhead(h)[0:16, 0:w], start=True, stop=True))
            P.op("tensor", lambda T, h=h, w=w: T.matmul(
                C.pB[:, 512:512 + w], C.P2[:], C.qhead(h)[0:16, 0:w], start=True, stop=True))
            trb = P.inc("tensor")
            P.wait("vector", trb)
            P.op("vector", lambda v, w=w: v.tensor_copy(C.rsb[:, 0:w], C.pB[:, 0:w]))
            tcp = P.inc("vector")
            P.wait("vector", tcp)
            P.op("vector", lambda v, w=w: v.tensor_tensor(
                C.qeB[:, 0:w], C.rsb[:, 0:w], C.pB[:, 512:512 + w], ALU.mult))
            t_evB = P.inc("vector")
            P.wait("tensor", t_evB)
            if h == 0:
                P.wait("tensor", t_pcpd_free)
            for tc in range(ntc):
                tw = min(128, w - tc * 128)
                pk = (C.pC, C.pC, C.pD, C.pD)[tc]
                off = (0, 136, 0, 136)[tc]
                P.op("tensor", lambda T, h=h, tc=tc, tw=tw, pk=pk, off=off: T.matmul(
                    pk[0:tw, off + 17 * h: off + 17 * h + 17],
                    C.qeA[:, tc * 128: tc * 128 + tw], C.kvA[:, 17 * h:17 * h + 17],
                    start=True, stop=False))
                P.op("tensor", lambda T, h=h, tc=tc, tw=tw, pk=pk, off=off: T.matmul(
                    pk[0:tw, off + 17 * h: off + 17 * h + 17],
                    C.qeB[:, tc * 128: tc * 128 + tw], C.kvB[:, 17 * h:17 * h + 17],
                    start=False, stop=False))
                P.op("tensor", lambda T, h=h, tc=tc, tw=tw, pk=pk, off=off, lo=lo:
                     T.matmul(pk[0:tw, off + 17 * h: off + 17 * h + 17],
                              C.qhead(h)[0:17, tc * 128: tc * 128 + tw],
                              C.kvC[0:17, 17 * h:17 * h + 17],
                              start=False, stop=True))
            tmo = P.inc("tensor")
            t_out_done.append(tmo)
            C.t_qhead_free = tmo
        t_tr = []
        t_at_free = None
        t_af_ev = [tqf, tqf]
        for tc in range(ntc):
            tw = min(128, w - tc * 128)
            pk = (C.pC, C.pC, C.pD, C.pD)[tc]
            off = (0, 136, 0, 136)[tc]
            pv = pk[0:tw, off:off + 136].rearrange("p (h e) -> p h e", e=17)
            P.wait("vector", t_out_done[-1])
            P.wait("vector", t_at_free)
            P.op("vector", lambda v, pv=pv, tw=tw: v.tensor_scalar(
                C.sc2[0:tw, 0:8].unsqueeze(2), pv[:, :, 16:17], 1e-5, None, ALU.max))
            tv = P.inc("vector")
            P.wait("vector", tv)
            P.op("vector", lambda v, tw=tw: v.reciprocal(
                C.qinv[0:tw, 0:8], C.sc2[0:tw, 0:8]))
            tv = P.inc("vector")
            P.wait("vector", tv)
            P.op("vector", lambda v, pv=pv, tw=tw: v.tensor_tensor(
                C.atok0[0:tw, 0:128].rearrange("p (h e) -> p h e", e=16),
                pv[:, :, 0:16],
                C.qinv[0:tw, 0:8].unsqueeze(2).broadcast_to([tw, 8, 16]), ALU.mult))
            tat = P.inc("vector")
            P.wait("tensor", tat)
            P.wait("tensor", t_af_ev[tc % 2])
            P.op("tensor", lambda T, tc=tc, tw=tw: T.transpose(
                C.pA[:, (tc % 2) * 512:(tc % 2) * 512 + tw], C.atok0[0:tw, 0:128],
                C.eye_f[0:tw, 0:tw]))
            tt = P.inc("tensor")
            t_at_free = tt
            P.wait("vector", tt)
            P.op("vector", lambda v, tc=tc, tw=tw: v.tensor_copy(
                C.afm0[:, tc * 128: tc * 128 + tw],
                C.pA[:, (tc % 2) * 512:(tc % 2) * 512 + tw]))
            t_tr.append(P.inc("vector"))
            t_af_ev[tc % 2] = t_tr[-1]
        t_pcpd_free = ("vector", C.P.cnt["vector"])
        P.wait("tensor", t_tr[-1])
        for dch in range(8):
            po = C.pA[:, (dch % 2) * 512:(dch % 2) * 512 + w]
            P.wait("tensor", t_wout_ev[dch % 2])
            P.op("tensor", lambda T, dch=dch, po=po, w=w: T.matmul(
                po, C.wout0[:, dch * 128:(dch + 1) * 128], C.afm0[:, 0:w],
                start=True, stop=True))
            tmm = P.inc("tensor")
            P.wait("vector", tmm)
            P.op("vector", lambda v, dch=dch, po=po, lo=lo, w=w: v.scalar_tensor_tensor(
                C.stream(dch, lo, w), po, 0.0, C.stream(dch, lo, w),
                ALU.add, ALU.add))
            t_wout_ev[dch % 2] = P.inc("vector")
    C.t_block0 = P.tok("vector")


def emit_final_store(C, src="dram"):
    P = C.P
    if C.dbg:
        P.wait_q_all("vector", "gpsimd")
        P.wait_q_all("vector", "sync")
    if src == "dram":
        P.wait("sync", C.t_moe_done)
    else:
        P.wait("sync", C.t_block0)
        C.stream_dram = C.R1
    for kc in range(8):
        P.dma("sync", C.out_y[kc * 128:(kc + 1) * 128, :],
              (C.stream_dram[:, kc * EXT + 3: kc * EXT + 3 + OWN] if src == "dram"
               else C.R1[:, kc * EXT + 3: kc * EXT + 3 + OWN]))
    P.wait_q_all("sync", "sync")
    P.wait_q_all("sync", "gpsimd")
    P.wait_q_all("sync", "scalar")


def emit_debug_stores(C):
    P = C.P
    if "stream" in C.dbg:
        P.wait("sync", C.t_block0)
        P.dma("sync", C.dbg["stream"], C.R1[:, 0:8 * EXT])
    if "statfm" in C.dbg:
        P.wait("gpsimd", C.t_stats_done)
        P.dma("gpsimd", C.dbg["statfm"], C.stat6fm[:])
    if "v2" in C.dbg:
        P.wait("gpsimd", C.t_v2_done)
        P.dma("gpsimd", C.dbg["v2"], C.R3[:, 0:8 * EXT])
    if "xn1" in C.dbg:
        P.wait("gpsimd", C.t_xn1)
        P.dma("gpsimd", C.dbg["xn1"], C.R2[:, 0:8 * EXT])
    if "phi" in C.dbg:
        P.wait("gpsimd", C.t_phi)
        P.dma("gpsimd", C.dbg["phi"], C.R1b[:, 0:8 * EXT])
    if "afm" in C.dbg:
        P.wait("gpsimd", C.t_attn1)
        P.dma("gpsimd", C.dbg["afm"], C.U5[:, 2048:6144])
    if "atok" in C.dbg:
        P.wait("gpsimd", C.t_attn1)
        P.dma("gpsimd", C.dbg["atok"], C.atok1)
    if "sdram" in C.dbg:
        P.wait("gpsimd", C.t_moe_done)
        P.dma("gpsimd", C.dbg["sdram"], C.stream_dram[:, 0:8 * EXT])
    if "wfm" in C.dbg:
        P.wait("gpsimd", C.t_router)
        P.dma("gpsimd", C.dbg["wfm"], C.Wfmb[:])


def emit_block0(C):
    emit_load_x(C)
    P = C.P
    prev_trep = C.t_xload
    prev_pool = None
    t_prev_s = None
    P.wait("tensor", C.t_xload)
    for wi, (lo, w) in enumerate(WINDOWS):
        tmm, trep, tpool = emit_v_window(C, lo, w, 0, gate_mm=prev_trep,
                                         gate_rep=prev_pool)
        tmm2, trep2, tpool2 = emit_v_window(C, lo, w, 1, gate_mm=trep,
                                            gate_rep=tpool)
        prev_trep = trep2
        prev_pool = tpool2
        P.wait("tensor", trep2)
        t_prev_s = emit_ffn0_window(C, wi, lo, w, tpool)
    C.t_v2_done = prev_pool
    C.t_ffn0 = t_prev_s
    C.P.wait("vector", C.t_ffn0)
    emit_partner(C)
    emit_own_kside(C)
    emit_kv_finalize(C)
    emit_qphase(C)


def build(phases=("b0", "b1", "moe"), debug=()):
    C = make_ctx(phases, debug)
    emit_setup(C)
    if "load" in phases:
        emit_load_x(C)
        C.t_block0 = C.t_xload
    if "loadd" in phases:
        emit_load_x(C)
        C.t_block0 = C.t_xload
        C.P.wait("sync", C.t_xload)
        C.t_offload = C.P.dma("sync", C.stream_dram[:, 0:8 * EXT], C.R1[:, 0:8 * EXT])
        C.t_moe_done = C.t_offload
    if "b0" in phases:
        emit_block0(C)
    if "b1" in phases:
        emit_block1(C, with_b2=("moe" in phases))
    if "moe" in phases:
        emit_moe(C)
    emit_debug_stores(C)
    emit_final_store(C, src="dram" if ("moe" in phases or "b1" in phases
                                       or "loadd" in phases) else "stream")
    C.P.emit(C.es)
    C.es.close()
    return C


def run_cores(C, inp, cores, trace=False):
    feeds = [load_feeds(C, inp, c) for c in cores]
    res = run_bass_kernel_spmd(C.nc, feeds, core_ids=list(range(len(cores))),
                               trace=trace)
    for r in res.results:
        r["y"] = np.ascontiguousarray(r["y"].T)
    return res


def emit_ln1(C):
    P = C.P
    sq = [C.U5[:, 0:512], C.U5[:, 512:1024]]
    arep_sb = C.U5[:, 1024:1536]
    crep_sb = C.U5[:, 1536:2048]
    t_prev_rep = None
    t_prev_xbf = C.t_block0
    t_prev_pool = None
    t_prev_mm = C.t_block0
    for wi, (lo, w) in enumerate(WINDOWS):
        t_sq_prev = [t_prev_pool, t_prev_pool]
        P.wait("vector", t_prev_xbf)
        P.wait("tensor", t_prev_mm)
        tkc = []
        for kc in range(8):
            P.op("vector", lambda v, kc=kc, lo=lo, w=w: v.tensor_copy(
                C.xbf(kc, w), C.stream(kc, lo, w)))
            txb = P.inc("vector")
            P.wait("gpsimd", txb)
            P.wait("gpsimd", t_sq_prev[kc % 2])
            P.op("gpsimd", lambda g, kc=kc, w=w: g.tensor_tensor(
                sq[kc % 2][:, 0:w], C.xbf(kc, w), C.xbf(kc, w), ALU.mult))
            tsq = P.inc("gpsimd")
            P.wait("tensor", txb)
            P.op("tensor", lambda T, kc=kc, w=w: T.matmul(
                C.pE[0:1, 0:w], C.ones_rhs[:], C.xbf(kc, w),
                start=(kc == 0), stop=(kc == 7)))
            P.wait("tensor", tsq)
            P.op("tensor", lambda T, kc=kc, w=w: T.matmul(
                C.pC[0:1, 0:w], C.ones_rhs[:], sq[kc % 2][:, 0:w],
                start=(kc == 0), stop=(kc == 7)))
            t_sq_prev[kc % 2] = P.inc("tensor")
        t_sums = P.tok("tensor")
        t_prev_pool = P.tok("gpsimd")
        P.wait("vector", t_sums)
        P.op("vector", lambda v, w=w: v.tensor_scalar(
            C.sr[0:1, 0:w], C.pE[0:1, 0:w], 1.0 / D, None, ALU.mult))
        P.op("vector", lambda v, w=w: v.tensor_scalar(
            C.sr[0:1, 512:512 + w], C.pC[0:1, 0:w], 1.0 / D, None, ALU.mult))
        tv = P.inc("vector")
        P.wait("vector", tv)
        P.op("vector", lambda v, w=w: v.tensor_tensor(
            C.sr[0:1, 1024:1024 + w], C.sr[0:1, 0:w], C.sr[0:1, 0:w], ALU.mult))
        tv = P.inc("vector")
        P.wait("vector", tv)
        P.op("vector", lambda v, w=w: v.tensor_tensor(
            C.sr[0:1, 512:512 + w], C.sr[0:1, 512:512 + w],
            C.sr[0:1, 1024:1024 + w], ALU.subtract))
        tv = P.inc("vector")
        P.wait("vector", tv)
        P.op("vector", lambda v, w=w: v.tensor_scalar(
            C.sr[0:1, 512:512 + w], C.sr[0:1, 512:512 + w], 0.0, None, ALU.max))
        tvar = P.inc("vector")
        P.wait("scalar", tvar)
        P.op("scalar", lambda s, w=w: s.activation(
            C.sr[0:1, 1024:1024 + w], C.sr[0:1, 512:512 + w], AF.Ln,
            bias=C.cbias[0:1, 0:1]))
        ts = P.inc("scalar")
        P.wait("scalar", ts)
        P.op("scalar", lambda s, w=w: s.activation(
            C.sr[0:1, 512:512 + w], C.sr[0:1, 1024:1024 + w], AF.Exp, scale=-0.5))
        ta = P.inc("scalar")
        P.wait("vector", ta)
        P.op("vector", lambda v, lo=lo, w=w: v.tensor_copy(
            C.lnA[0:1, lo:lo + w], C.sr[0:1, 512:512 + w]))
        P.op("vector", lambda v, w=w: v.tensor_scalar(
            C.sr[0:1, 0:w], C.sr[0:1, 0:w], -1.0, None, ALU.mult))
        tv = P.inc("vector")
        P.wait("vector", tv)
        P.op("vector", lambda v, lo=lo, w=w: v.tensor_tensor(
            C.sr[0:1, 0:w], C.sr[0:1, 0:w], C.sr[0:1, 512:512 + w], ALU.mult))
        tv = P.inc("vector")
        P.wait("vector", tv)
        P.op("vector", lambda v, lo=lo, w=w: v.tensor_copy(
            C.lnC[0:1, lo:lo + w], C.sr[0:1, 0:w]))
        tc_ = P.inc("vector")
        P.wait("tensor", tc_)
        P.wait("tensor", t_prev_rep)
        P.op("tensor", lambda T, lo=lo, w=w: T.matmul(
            C.pB[:, 0:w], C.ones_lhs[:], C.lnA[0:1, lo:lo + w], start=True, stop=True))
        P.op("tensor", lambda T, lo=lo, w=w: T.matmul(
            C.pB[:, 512:512 + w], C.ones_lhs[:], C.lnC[0:1, lo:lo + w],
            start=True, stop=True))
        tmm = P.inc("tensor")
        P.wait("vector", tmm)
        P.wait("vector", t_prev_pool)
        P.op("vector", lambda v, w=w: v.tensor_copy(arep_sb[:, 0:w], C.pB[:, 0:w]))
        P.op("vector", lambda v, w=w: v.tensor_copy(crep_sb[:, 0:w], C.pB[:, 512:512 + w]))
        t_prev_rep = P.inc("vector")
        P.wait("vector", t_prev_rep)
        toks = []
        for kc in range(8):
            P.op("vector", lambda v, kc=kc, lo=lo, w=w: v.tensor_tensor(
                C.xn1(kc, lo, w), C.xbf(kc, w), arep_sb[:, 0:w], ALU.mult))
            tv = P.inc("vector")
            P.wait("gpsimd", tv)
            P.op("gpsimd", lambda g, kc=kc, lo=lo, w=w: g.tensor_tensor(
                C.xn1(kc, lo, w), C.xn1(kc, lo, w), crep_sb[:, 0:w], ALU.add))
            toks.append(P.inc("gpsimd"))
        t_prev_xbf = toks[-1]
        t_prev_pool = toks[-1]
        t_prev_mm = P.tok("vector")
    C.t_xn1 = t_prev_xbf
    P.wait("sync", P.tok("vector"))
    C.t_offload = P.dma("sync", C.stream_dram[:, 0:8 * EXT], C.R1[:, 0:8 * EXT])


def emit_router_tc(C, t):
    P = C.P
    rsc = C.U3f[:, 512:536]
    E_ = rsc[:, 0:8]
    sel = rsc[:, 8:16]
    Wv = rsc[:, 16:24]
    t_rl_ev = C.t_rl_ev
    t_tr_prev = C.t_tr_prev
    if True:
        lo = 3 + t * 128
        P.wait("tensor", C.t_xn1)
        P.wait("tensor", t_rl_ev)
        P.wait("tensor", t_tr_prev)
        for kc in range(8):
            P.op("tensor", lambda T, kc=kc, lo=lo: T.matmul(
                C.pD[0:12, 0:128], C.wcomb[:, kc * 12:(kc + 1) * 12],
                C.xn1(kc, lo, 128), start=(kc == 0), stop=(kc == 7)))
        tmm = P.inc("tensor")
        P.wait("scalar", tmm)
        P.op("scalar", lambda s: s.activation(C.rl[0:12, 0:128], C.pD[0:12, 0:128],
                                              AF.Identity, bias=C.rbias[0:12, 0:1]))
        trl = P.inc("scalar")
        t_rl_ev = trl
        P.wait("tensor", trl)
        P.wait("tensor", t_tr_prev)
        P.op("tensor", lambda T: T.transpose(C.pD[0:128, 128:140], C.rl[0:12, 0:128],
                                             C.eye_f[0:12, 0:12]))
        ttr = P.inc("tensor")
        P.wait("vector", ttr)
        P.op("vector", lambda v: v.tensor_copy(C.rt[:, 0:12], C.pD[:, 128:140]))
        tv = P.inc("vector")
        t_tr_prev = tv
        P.wait("vector", tv)
        P.op("vector", lambda v: v.reduce_max(C.sc2[:, 8:9], C.rt[:, 8:12], axis=AX))
        tv = P.inc("vector")
        P.wait("vector", tv)
        P.op("vector", lambda v: v.tensor_scalar(C.sc2[:, 9:10], C.sc2[:, 8:9], -1.0, None, ALU.mult))
        tv = P.inc("vector")
        P.wait("scalar", tv)
        P.op("scalar", lambda s: s.activation(C.lp[:, 0:4], C.rt[:, 8:12], AF.Exp,
                                              bias=C.sc2[:, 9:10]))
        ts = P.inc("scalar")
        P.wait("vector", ts)
        P.op("vector", lambda v: v.reduce_sum(C.sc2[:, 10:11], C.lp[:, 0:4], axis=AX))
        tv = P.inc("vector")
        P.wait("vector", tv)
        P.op("vector", lambda v: v.reciprocal(C.sc2[:, 11:12], C.sc2[:, 10:11]))
        tv = P.inc("vector")
        P.wait("vector", tv)
        P.op("vector", lambda v: v.tensor_scalar(C.lp[:, 0:4], C.lp[:, 0:4],
                                                 C.sc2[:, 11:12], None, ALU.mult))
        tv = P.inc("vector")
        P.wait("vector", tv)
        P.op("vector", lambda v: v.tensor_tensor(
            C.sc1[:, 0:8].rearrange("p (a b) -> p a b", b=4),
            C.lp[:, 0:4].unsqueeze(1).broadcast_to([128, 2, 4]),
            C.lp[:, 1:5].unsqueeze(1).broadcast_to([128, 2, 4]), ALU.add))
        tv = P.inc("vector")
        P.wait("vector", tv)
        P.op("vector", lambda v: v.scalar_tensor_tensor(
            C.rt[:, 0:8], C.sc1[:, 0:8], 0.1, C.rt[:, 0:8], ALU.mult, ALU.add))
        tv = P.inc("vector")
        P.wait("vector", tv)
        P.op("vector", lambda v: v.reduce_max(C.sc2[:, 8:9], C.rt[:, 0:8], axis=AX))
        tv = P.inc("vector")
        P.wait("vector", tv)
        P.op("vector", lambda v: v.tensor_scalar(C.sc2[:, 9:10], C.sc2[:, 8:9], -1.0, None, ALU.mult))
        tv = P.inc("vector")
        P.wait("scalar", tv)
        P.op("scalar", lambda s: s.activation(E_, C.rt[:, 0:8], AF.Exp,
                                              bias=C.sc2[:, 9:10]))
        ts = P.inc("scalar")
        P.wait("vector", ts)
        P.op("vector", lambda v: v.tensor_scalar(sel, E_, 1.0, None, ALU.is_lt))
        tv = P.inc("vector")
        P.wait("vector", tv)
        P.op("vector", lambda v: v.tensor_tensor(Wv, E_, sel, ALU.mult))
        tv = P.inc("vector")
        P.wait("vector", tv)
        P.op("vector", lambda v: v.reduce_max(C.sc2[:, 8:9], Wv, axis=AX))
        tv = P.inc("vector")
        P.wait("vector", tv)
        P.op("vector", lambda v: v.tensor_scalar(C.sc2[:, 9:10], C.sc2[:, 8:9], 1.0, None, ALU.add))
        P.op("vector", lambda v: v.tensor_scalar(sel, E_, C.sc2[:, 8:9], None, ALU.is_ge))
        tv = P.inc("vector")
        P.wait("vector", tv)
        P.op("vector", lambda v: v.reciprocal(C.sc2[:, 10:11], C.sc2[:, 9:10]))
        tv = P.inc("vector")
        P.wait("vector", tv)
        P.op("vector", lambda v: v.tensor_tensor(Wv, E_, sel, ALU.mult))
        tv = P.inc("vector")
        P.wait("vector", tv)
        P.op("vector", lambda v: v.tensor_scalar(Wv, Wv, C.sc2[:, 10:11], None, ALU.mult))
        tv = P.inc("vector")
        P.wait("tensor", tv)
        P.op("tensor", lambda T: T.transpose(C.pD[0:8, 140:268],
                                             Wv, C.eye_f[:]))
        ttr = P.inc("tensor")
        P.wait("vector", ttr)
        P.op("vector", lambda v, t=t: v.tensor_copy(C.Wfmb[0:8, t * 128:(t + 1) * 128],
                                                    C.pD[0:8, 140:268]))
        tv = P.inc("vector")
        t_tr_prev = tv
    C.t_rl_ev = t_rl_ev
    C.t_tr_prev = t_tr_prev
    C.t_router = P.tok("vector")


def emit_qkv_phi_v(C):
    P = C.P
    php = C.U5[:, 0:512]
    tm1 = C.U5[:, 512:1024]
    tm2 = C.U5[:, 1024:1536]
    P.wait("gpsimd", C.t_offload)
    P.wait("vector", C.t_offload)
    t_w_used = [None, None]
    t_w_read = [None, None]
    t_phi_prev = None
    t_qh_used = [None, None]
    for pair in range(8):
        for half in range(2):
            mc = (pair % 4) * 2 + half + (8 if pair >= 4 else 0)
            P.wait("gpsimd", t_w_read[half])
            tl = P.dma("gpsimd", C.wqk_rot(half)[:],
                       C.wd["wqk_r"][:, :, mc * 128:(mc + 1) * 128])
            t_w_used[half] = tl
        for wi, (lo, w) in enumerate(WINDOWS):
            for half in range(2):
                pq = (C.pE, C.pC)[half]
                P.wait("tensor", t_w_used[half])
                P.wait("tensor", C.t_xn1)
                P.wait("tensor", t_qh_used[half])
                for kc in range(8):
                    P.op("tensor", lambda T, kc=kc, half=half, lo=lo, w=w, pq=pq:
                         T.matmul(pq[:, 0:w],
                                  C.wqk_rot(half)[:, kc * 128:(kc + 1) * 128],
                                  C.xn1(kc, lo, w), start=(kc == 0), stop=(kc == 7)))
                tmm = P.inc("tensor")
                P.wait("scalar", tmm)
                P.op("scalar", lambda sc, half=half, w=w, pq=pq: sc.copy(
                    C.qkf(half)[:, 0:w], pq[:, 0:w]))
                tev = P.inc("scalar")
                P.wait("tensor", tev)
                P.wait("tensor", t_phi_prev)
                ph = (C.pB[:, 0:512], C.pB[:, 512:1024])[half]
                P.op("tensor", lambda T, half=half, w=w, ph=ph: T.matmul(
                    ph[0:64, 0:w], C.proj[:], C.qkf(half)[:, 0:w],
                    start=True, stop=True))
                t_qh_used[half] = P.inc("tensor")
            tph = P.tok("tensor")
            P.wait("vector", tph)
            P.op("vector", lambda v, w=w: v.tensor_copy(php[0:64, 0:w], C.pB[0:64, 0:w]))
            P.op("vector", lambda v, w=w: v.tensor_copy(php[64:128, 0:w],
                                                        C.pB[0:64, 512:512 + w]))
            tev = P.inc("vector")
            t_phi_prev = tev
            P.wait("vector", tev)
            P.op("vector", lambda v, w=w: v.tensor_scalar(tm1[:, 0:w], php[:, 0:w],
                                                          0.0, None, ALU.min))
            tv = P.inc("vector")
            P.wait("scalar", tv)
            P.op("scalar", lambda s, w=w: s.activation(tm2[:, 0:w], tm1[:, 0:w], AF.Exp))
            ts = P.inc("scalar")
            P.wait("vector", ts)
            kp = pair % 4
            dst = (C.qphi, C.kphi)[pair // 4]
            P.op("vector", lambda v, kp=kp, dst=dst, lo=lo, w=w: v.scalar_tensor_tensor(
                dst(kp, lo, w), php[:, 0:w], 0.0, tm2[:, 0:w], ALU.max, ALU.add))
            tv = P.inc("vector")
            t_phi_prev = tv
        t_w_read[0] = ("tensor", P.cnt["tensor"])
        t_w_read[1] = t_w_read[0]
        emit_router_tc(C, 2 * pair)
        emit_router_tc(C, 2 * pair + 1)
    C.t_qkv_done = ("tensor", P.cnt["tensor"])
    if True:
        pass
    P.wait("vector", P.tok("vector"))
    for kp in range(4):
        P.op("vector", lambda v, kp=kp: v.tensor_scalar(
            C.kphi(kp, 0, 3), C.kphi(kp, 0, 3), C.flags[:, 0:1], None, ALU.mult))
        P.op("vector", lambda v, kp=kp: v.tensor_scalar(
            C.kphi(kp, 2051, 3), C.kphi(kp, 2051, 3), C.flags[:, 1:2], None, ALU.mult))
    C.t_phi = P.inc("vector")
    C.t_v_ready = {}
    C.t_vslot_used = [C.t_offload, C.t_offload, C.t_offload]
    for sl in range(3):
        P.wait("vector", C.t_offload)
        P.op("vector", lambda v, sl=sl: v.memset(
            C.Vsb(sl).rearrange("p (h e) -> p h e", e=129)[:, :, 128:129], 1.0))
    C.t_vones = P.inc("vector")


def emit_v_tile(C, t):
    P = C.P
    tw = twidth(t)
    sl = t % 3
    P.wait("tensor", C.t_xn1)
    tev = None
    for half in range(2):
        P.wait("tensor", C.t_vslot_used[sl])
        P.wait("tensor", getattr(C, "t_pd_ev", None))
        P.wait("tensor", getattr(C, "t_router", None))
        for kc in range(8):
            P.op("tensor", lambda T, kc=kc, half=half, t=t, tw=tw: T.matmul(
                C.pD[0:tw, 0:512], C.xn1(kc, t * 128, tw),
                C.wv(kc)[:, half * 512:half * 512 + 512],
                start=(kc == 0), stop=(kc == 7)))
        tmm = P.inc("tensor")
        P.wait("vector", tmm)
        P.wait("vector", C.t_vones)
        P.op("vector", lambda v, t=t, tw=tw, sl=sl, half=half: v.tensor_copy(
            C.Vsb(sl)[0:tw, half * 516:half * 516 + 516]
            .rearrange("p (h e) -> p h e", e=129)[:, :, 0:128],
            C.pD[0:tw, 0:512].rearrange("p (h e) -> p h e", e=128)))
        tev = P.inc("vector")
        C.t_pd_ev = tev
    C.t_v_ready[t] = tev
    return tev


def emit_block1_attn(C, with_b2=True):
    P = C.P
    P.wait("gpsimd", C.t_offload)
    for kc in range(8):
        P.dma("gpsimd", C.wv(kc)[:], C.wd["wv_r"][:, kc:kc + 1, :])
    C.t_wv = P.qtok("gpsimd")
    P.wait("tensor", C.t_wv)
    emit_v_tile(C, 0)
    emit_v_tile(C, 1)
    t_st_prev = [None, None]
    t_st_rd = [None, None]
    t_atok_prev = None
    t_bank_ev = None
    t_pf_prev = None
    t_afm_used = None
    t_wout_used = [None, None, None, None]
    t_wo_dma = [None, None]
    for qt in range(16):
        if qt + 1 <= 16:
            emit_v_tile(C, qt + 1)
        for hb4 in range(2):
            P.wait("tensor", C.t_phi)
            P.wait("tensor", t_st_prev[hb4 % 2])
            if hb4 == 0:
                for _b in range(4):
                    P.wait("tensor", t_wout_used[_b])
            for hi in range(4):
                h = hb4 * 4 + hi
                kp, hh = h // 2, h % 2
                P.op("tensor", lambda T, kp=kp, hh=hh, qt=qt, hi=hi: T.matmul(
                    C.pE[:, hi * 128:(hi + 1) * 128],
                    C.kphi(kp, qt * 128, 128)[64 * hh:64 * hh + 64, :],
                    C.qphi(kp, qt * 128 + 3, 128)[64 * hh:64 * hh + 64, :],
                    start=True, stop=True))
                P.op("tensor", lambda T, kp=kp, hh=hh, qt=qt, hi=hi: T.matmul(
                    C.pC[0:6, hi * 128:(hi + 1) * 128],
                    C.kphi(kp, qt * 128 + 128, 6)[64 * hh:64 * hh + 64, :],
                    C.qphi(kp, qt * 128 + 3, 128)[64 * hh:64 * hh + 64, :],
                    start=True, stop=True))
            tst = P.inc("tensor")
            P.wait("vector", tst)
            P.wait("vector", t_st_rd[hb4 % 2])
            P.op("vector", lambda v: v.tensor_tensor(
                C.U5[:, 0:512].rearrange("p (a b) -> p a b", b=128),
                C.pE[:, 0:512].rearrange("p (a b) -> p a b", b=128),
                C.bandA[:].unsqueeze(1).broadcast_to([128, 4, 128]), ALU.mult))
            P.op("vector", lambda v: v.tensor_tensor(
                C.U5[0:6, 512:1024].rearrange("p (a b) -> p a b", b=128),
                C.pC[0:6, 0:512].rearrange("p (a b) -> p a b", b=128),
                C.bandB[:].unsqueeze(1).broadcast_to([6, 4, 128]), ALU.mult))
            t_st_prev[hb4 % 2] = P.inc("vector")
            P.wait("tensor", t_st_prev[hb4 % 2])
            if hb4 == 0:
                P.wait("tensor", t_bank_ev)
            P.wait("tensor", C.t_v_ready[qt])
            P.wait("tensor", C.t_v_ready[qt + 1])
            for hi in range(4):
                h = hb4 * 4 + hi
                bank = (C.pA, C.pA, C.pB, C.pB)[h // 2]
                off = (0, 512)[(h // 2) % 2] + (h % 2) * 129
                P.op("tensor", lambda T, h=h, qt=qt, bank=bank, off=off, hi=hi:
                     T.matmul(bank[:, off:off + 129], C.stA(hi),
                              C.Vsb(qt % 3)[:, h * 129:(h + 1) * 129],
                              start=True, stop=False))
                P.op("tensor", lambda T, h=h, qt=qt, bank=bank, off=off, hi=hi:
                     T.matmul(bank[:, off:off + 129], C.stB(hi),
                              C.Vsb((qt + 1) % 3)[0:6, h * 129:(h + 1) * 129],
                              start=False, stop=True))
            tat = P.inc("tensor")
            t_st_rd[hb4 % 2] = tat
        C.t_vslot_used[qt % 3] = tat
        C.t_vslot_used[(qt + 1) % 3] = tat
        t_ev = []
        for g in range(4):
            bank = (C.pA, C.pA, C.pB, C.pB)[g]
            off = (0, 512)[g % 2]
            pv = bank[:, off:off + 258].rearrange("p (h e) -> p h e", e=129)
            P.wait("vector", tat)
            P.wait("vector", t_atok_prev)
            P.op("vector", lambda v, pv=pv, g=g: v.tensor_scalar(
                C.sc2[:, 2 * g:2 * g + 2].unsqueeze(2), pv[:, :, 128:129],
                1e-8, None, ALU.add))
            tv = P.inc("vector")
            P.wait("vector", tv)
            P.op("vector", lambda v, g=g: v.reciprocal(
                C.qinv[:, 2 * g:2 * g + 2], C.sc2[:, 2 * g:2 * g + 2]))
            tv = P.inc("vector")
            P.wait("vector", tv)
            P.op("vector", lambda v, pv=pv, g=g: v.tensor_tensor(
                C.atok1[:, g * 256:(g + 1) * 256].rearrange("p (h e) -> p h e", e=128),
                pv[:, :, 0:128],
                C.qinv[:, 2 * g:2 * g + 2].unsqueeze(2).broadcast_to([128, 2, 128]),
                ALU.mult))
            t_ev.append(P.inc("vector"))
        t_bank_ev = t_ev[-1]
        t_atok_prev = None
        P.wait("tensor", t_ev[-1])
        P.wait("tensor", t_pf_prev)
        for kc in range(8):
            P.op("tensor", lambda T, kc=kc: T.transpose(
                C.pF[:, kc * 128:(kc + 1) * 128],
                C.atok1[:, kc * 128:(kc + 1) * 128], C.eye_b[:]))
        ttr = P.inc("tensor")
        t_atok_prev = ttr
        P.wait("vector", ttr)
        P.wait("vector", t_afm_used)
        for kc in range(8):
            P.op("vector", lambda v, kc=kc, qt=qt: v.tensor_copy(
                C.afm1(kc, 512)[:, (qt % 4) * 128:(qt % 4) * 128 + 128],
                C.pF[:, kc * 128:(kc + 1) * 128]))
        t_pf_prev = P.inc("vector")
        if qt % 4 == 3:
            g4 = qt // 4
            lo = 3 + g4 * 512
            P.wait("tensor", t_pf_prev)
            for dch in range(8):
                buf = dch % 4
                P.wait("gpsimd", C.t_qkv_done)
                P.wait("gpsimd", t_wout_used[buf])
                tl = P.dma("gpsimd", C.wout1_rot(buf)[:],
                           C.wd["wout1_r"][:, :, dch * 128:(dch + 1) * 128])
                P.wait("tensor", tl)
                P.wait("tensor", t_wout_used[buf])
                po = (C.pE, C.pC)[dch % 2]
                for kc in range(8):
                    P.op("tensor", lambda T, kc=kc, buf=buf, po=po, wb=with_b2:
                         T.matmul(po[:, 0:512],
                                  C.wout1_rot(buf)[:, kc * 128:(kc + 1) * 128],
                                  C.afm1(kc, 512), start=(kc == 0),
                                  stop=(kc == 7 and not wb)))
                if with_b2:
                    P.wait("tensor", C.t_router)
                    P.op("tensor", lambda T, dch=dch, g4=g4, po=po: T.matmul(
                        po[:, 0:512], C.b2sb[0:8, dch * 128:(dch + 1) * 128],
                        C.Wfmb[0:8, g4 * 512:(g4 + 1) * 512],
                        start=False, stop=True))
                tmm = P.inc("tensor")
                P.wait("vector", tmm)
                P.wait("vector", t_wo_dma[dch % 2])
                P.op("vector", lambda v, dch=dch, po=po: v.tensor_scalar(
                    C.t_sb1(dch % 2), po[:, 0:512], C.bout1[:, dch:dch + 1],
                    None, ALU.add))
                tev = P.inc("vector")
                t_wout_used[buf] = tev
                P.wait("gpsimd", tev)
                tD = P.dma("gpsimd",
                           C.stream_dram[:, dch * EXT + lo: dch * EXT + lo + 512],
                           C.t_sb1(dch % 2), accum_op=ALU.add)
                t_wo_dma[dch % 2] = tD
            t_afm_used = P.tok("tensor")
    C.t_attn1 = P.qtok("gpsimd")


def emit_block1(C, with_b2=True):
    emit_ln1(C)
    C.t_rl_ev = None
    C.t_tr_prev = None
    C.P.op("vector", lambda v: v.memset(C.lp[:, 4:5], 0.0))
    C.P.inc("vector")
    emit_qkv_phi_v(C)
    emit_block1_attn(C, with_b2=with_b2)
    C.t_moe_done = C.t_attn1


def emit_w2_convert_step(C, i):
    P = C.P
    e, j = i // 16, i % 16
    if j < 8:
        P.dma("gpsimd", C.w2bf[e, j],
              C.wd["w2"][e, :, j * 128:(j + 1) * 128]
              .rearrange("(c p) m -> p c m", p=128))
    else:
        P.dma("gpsimd", C.w1bf[e, j - 8],
              C.wd["w1"][e, (j - 8) * 128:(j - 7) * 128, :])
    C.t_conv[e] = P.qtok("gpsimd")


def emit_moe(C):
    P = C.P
    t_blk1_pe = ("tensor", P.cnt["tensor"])
    t_blk1_ve = ("vector", P.cnt["vector"])
    P.wait("gpsimd", t_blk1_pe)
    P.wait("gpsimd", t_blk1_ve)
    P.wait("gpsimd", C.t_attn1)
    P.wait("sync", t_blk1_pe)
    P.wait("sync", t_blk1_ve)
    P.wait("sync", C.t_attn1)

    t_w1_read = None
    t_w2_used = [None, None]
    t_h_used = {}
    t_hps = [None, None, None, None]
    t_out_ev = [None, None]
    t_acc_dma = [None, None]
    t_wrep_used = [None, None]
    t_wrep_dma = [C.t_attn1, C.t_attn1]
    P.wait("gpsimd", C.t_conv[0])
    for kc in range(8):
        P.dma("gpsimd", C.w1sb(kc)[:], C.w1bf[0, kc])
    t_w1 = P.qtok("gpsimd")
    for e in range(8):
        for tt in range(4):
            lo = 3 + tt * 512
            wr = C.w_rep(tt % 2)
            P.wait("tensor", C.t_router)
            P.wait("tensor", t_wrep_used[tt % 2])
            P.wait("tensor", getattr(C, "t_wrep_copy", None))
            P.op("tensor", lambda T, e=e, tt=tt: T.matmul(
                C.pD[:, 0:512], C.sel8[:, e * 128:(e + 1) * 128],
                C.Wfmb[0:8, tt * 512:(tt + 1) * 512], start=True, stop=True))
            tmm = P.inc("tensor")
            P.wait("vector", tmm)
            P.wait("vector", t_wrep_dma[tt % 2])
            P.op("vector", lambda v, wr=wr: v.tensor_copy(wr, C.pD[:, 0:512]))
            t_wrep = P.inc("vector")
            C.t_wrep_copy = t_wrep
            t_h_evac = {}
            for hch in range(32):
                hb = hch % 4
                pa = (C.pA[:, 0:512], C.pA[:, 512:1024], C.pC[:, 0:512],
                      C.pE[:, 0:512])[hb]
                P.wait("tensor", t_w1)
                P.wait("tensor", C.t_xn1)
                P.wait("tensor", t_hps[hb])
                for kc in range(8):
                    P.op("tensor", lambda T, kc=kc, hch=hch, lo=lo, pa=pa: T.matmul(
                        pa, C.w1sb(kc)[:, hch * 128:(hch + 1) * 128],
                        C.xn1(kc, lo, 512), start=(kc == 0), stop=(kc == 7)))
                tmm = P.inc("tensor")
                P.wait("scalar", tmm)
                P.wait("scalar", t_h_used.get(hch))
                P.op("scalar", lambda s, e=e, hch=hch, pa=pa: s.activation(
                    C.h_sb(hch, 512), pa, AF.Gelu,
                    bias=C.b1sb[:, e * 32 + hch: e * 32 + hch + 1]))
                t_hps[hb] = P.inc("scalar")
                t_h_evac[hch] = t_hps[hb]
            if tt == 1 and e + 3 < 8:
                for j in range(16):
                    emit_w2_convert_step(C, (e + 3) * 16 + j)
            if tt == 3:
                t_w1_read = ("tensor", P.cnt["tensor"])
                if e + 1 < 8:
                    P.wait("gpsimd", t_w1_read)
                    P.wait("gpsimd", C.t_conv[e + 1])
                    for kc in range(8):
                        P.dma("gpsimd", C.w1sb(kc)[:], C.w1bf[e + 1, kc])
                    t_w1 = P.qtok("gpsimd")
            for dch in range(8):
                db = dch % 2
                P.wait("sync", t_w2_used[db])
                P.wait("sync", C.t_conv[e])
                P.dma("sync", C.w2buf[db][:], C.w2bf[e, dch])
                tw2 = P.qtok("sync")
                pb = C.pB[:, db * 512:db * 512 + 512]
                P.wait("tensor", tw2)
                P.wait("tensor", t_out_ev[db])
                for hch in range(32):
                    P.wait("tensor", t_h_evac[hch])
                    P.op("tensor", lambda T, hch=hch, db=db, pb=pb: T.matmul(
                        pb, C.w2buf[db][:, hch * 128:(hch + 1) * 128],
                        C.h_sb(hch, 512), start=(hch == 0), stop=(hch == 31)))
                tmm = P.inc("tensor")
                t_w2_used[db] = tmm
                P.wait("vector", tmm)
                P.wait("vector", t_wrep)
                P.wait("vector", t_acc_dma[db])
                P.op("vector", lambda v, db=db, pb=pb, wr=wr: v.tensor_tensor(
                    C.t_sb(db), pb, wr, ALU.mult))
                tev = P.inc("vector")
                t_out_ev[db] = tev
                P.wait("gpsimd", tev)
                P.dma("gpsimd",
                      C.stream_dram[:, dch * EXT + lo: dch * EXT + lo + 512],
                      C.t_sb(db), accum_op=ALU.add)
                t_acc_dma[db] = P.qtok("gpsimd")
            for hch in range(32):
                t_h_used[hch] = ("tensor", P.cnt["tensor"])
            t_wrep_used[tt % 2] = ("tensor", P.cnt["tensor"])
            t_wrep_dma[tt % 2] = ("vector", P.cnt["vector"])
    C.t_moe_done = P.qtok("gpsimd")


def kernel(**inputs):
    C = build(phases=("b0", "b1", "moe"))
    feeds = [load_feeds(C, inputs, c) for c in range(8)]
    res = run_bass_kernel_spmd(C.nc, feeds, core_ids=list(range(8)))
    out = np.zeros((4, 4096, 1024), np.float32)
    for c in range(8):
        b, s = c // 2, c % 2
        out[b, s * OWN:(s + 1) * OWN] = res.results[c]["y"].T
    return out


# version 19
# speedup vs baseline: 1.1046x; 1.0461x over previous
import numpy as np
import ml_dtypes
import concourse.bass as bass
import concourse.mybir as mybir
from concourse.bass_utils import run_bass_kernel_spmd
from contextlib import ExitStack

F32 = mybir.dt.float32
BF16 = mybir.dt.bfloat16
AF = mybir.ActivationFunctionType
ALU = mybir.AluOpType
AX = mybir.AxisListType.X
BF = ml_dtypes.bfloat16

B_, SQ, D = 4, 4096, 1024
OWN, HALO = 2048, 3
EXT = OWN + 2 * HALO
NTT = 17
WINDOWS = [(0, 512), (512, 512), (1024, 512), (1536, 512), (2048, 6)]
OWNW = [(3, 512), (515, 512), (1027, 512), (1539, 512)]
ENGINES = ("sync", "tensor", "vector", "scalar", "gpsimd")


def twidth(t):
    return 6 if t == 16 else 128


QPOOL = {"sync": 6, "gpsimd": 6, "scalar": 4, "vector": 2, "tensor": 2}


class Prog:
    def __init__(self, nc):
        self.nc = nc
        self.recs = {e: [] for e in ENGINES}
        self.cnt = {}
        self.sem_names = []
        self.qpool = {}
        self.qrr = {e: 0 for e in ENGINES}
        for e in ENGINES:
            self.cnt[e] = 0
            self.sem_names.append(e)
            self.qpool[e] = []
            for i in range(QPOOL[e]):
                n = f"q_{e}{i}"
                self.cnt[n] = 0
                self.sem_names.append(n)
                self.qpool[e].append(n)

    def op(self, eng, fn, tok=False):
        self.recs[eng].append(["op", fn, []])
        if tok:
            return self.inc(eng)
        return None

    def inc(self, eng, sem=None, n=1):
        name = sem or eng
        assert self.recs[eng] and self.recs[eng][-1][0] == "op"
        self.cnt[name] += n
        self.recs[eng][-1][2].append((name, n))
        return (name, self.cnt[name])

    def tok(self, eng):
        return (eng, self.cnt[eng])

    def wait(self, eng, tok):
        if tok is None:
            return
        if isinstance(tok, list):
            for t in tok:
                self.wait(eng, t)
            return
        if tok[1] <= 0:
            return
        self.recs[eng].append(["wait", tok[0], tok[1]])

    def dma(self, eng, out, in_, **kw):
        def fn(E, out=out, in_=in_, kw=kw):
            return E.dma_start(out=out, in_=in_, **kw)
        pool = self.qpool[eng]
        name = pool[self.qrr[eng] % len(pool)]
        self.qrr[eng] += 1
        if self.cnt[name] > 0:
            self.recs[eng].append(["wait", name, self.cnt[name]])
        self.recs[eng].append(["op", fn, []])
        return self.inc(eng, name, 16)

    def qtok(self, eng):
        return [(n, self.cnt[n]) for n in self.qpool[eng] if self.cnt[n] > 0]

    def wait_q_all(self, eng, queue_eng):
        for t in self.qtok(queue_eng):
            self.wait(eng, t)

    def emit(self, es):
        nc = self.nc
        sems = {}
        for name in self.sem_names:
            if self.cnt[name] > 0:
                sems[name] = es.enter_context(nc.semaphore(name))
        block = es.enter_context(nc.Block())
        for eng in ENGINES:
            recs = self.recs[eng]
            if not recs:
                continue

            def section(E, recs=recs, sems=sems):
                seen = {}
                for r in recs:
                    if r[0] == "wait":
                        if seen.get(r[1], -1) >= r[2]:
                            continue
                        seen[r[1]] = r[2]
                        E.wait_ge(sems[r[1]], r[2])
                    else:
                        ins = r[1](E)
                        for (name, n) in r[2]:
                            ins.then_inc(sems[name], n)
            getattr(block, eng)(section)



def host_constants():
    c = {}
    c["eye_f"] = np.eye(128, dtype=np.float32)
    c["eye_b"] = np.eye(128, dtype=BF)
    c["ones_lhs"] = np.ones((1, 128), dtype=BF)
    c["ones_rhs"] = np.ones((128, 1), dtype=BF)
    c["ones_row"] = np.ones((1, 512), dtype=BF)
    p1a = np.zeros((16, 128), dtype=np.float32)
    p1b = np.zeros((16, 128), dtype=np.float32)
    p2 = np.zeros((16, 128), dtype=np.float32)
    s = 2.0 ** -0.5
    for m in range(128):
        p1a[m // 16, m] = s
        p1b[(m + 128) // 16, m] = s
        p2[m % 16, m] = 1.0
    c["P1a"] = p1a.astype(BF)
    c["P1b"] = p1b.astype(BF)
    c["P2"] = p2.astype(BF)
    bA = np.zeros((128, 128), dtype=np.float32)
    for j in range(128):
        bA[j, max(0, j - 6):j + 1] = 1.0
    bB = np.zeros((6, 128), dtype=np.float32)
    for j in range(6):
        bB[j, 122 + j:128] = 1.0
    c["bandA"] = bA.astype(BF)
    c["bandB"] = bB.astype(BF)
    sel = np.zeros((4, 4 * 128), dtype=np.float32)
    for r in range(4):
        sel[r, r * 128:(r + 1) * 128] = 1.0
    c["sel4"] = sel.astype(BF)
    c["cvec"] = np.tile(np.array([[1e-5, 1e-8, 1.0, 0.0]], dtype=np.float32), (128, 1))
    sel8 = np.zeros((8, 8 * 128), dtype=np.float32)
    for r in range(8):
        sel8[r, r * 128:(r + 1) * 128] = 1.0
    c["sel8"] = sel8.astype(BF)
    return c


def host_weights(inp):
    w = {}
    f32 = np.float32
    A = np.ascontiguousarray
    w["wq_r"] = A(inp["L0_attn_w_q"].reshape(8, 128, 128).transpose(1, 0, 2))
    w["wkv_r"] = A(inp["L0_attn_w_kv"].reshape(8, 128, 256).transpose(1, 0, 2))
    w["wout0"] = A(inp["L0_attn_w_out"]).astype(f32)
    w["gamma_r"] = A(np.asarray(inp["L0_attn_gamma"]).reshape(8, 128).T)
    w["win_r"] = A(inp["L0_ffn_Win"].reshape(8, 128, 64).transpose(1, 0, 2))
    W1 = np.asarray(inp["L0_ffn_W1"])
    Bm = np.zeros((64, 1024), dtype=f32)
    for m in range(64):
        Bm[m, m * 16:(m + 1) * 16] = W1[m]
    w["Bmat"] = Bm
    W2 = np.asarray(inp["L0_ffn_W2"])
    Cm = np.zeros((1024, 64), dtype=f32)
    for m in range(64):
        Cm[m * 16:(m + 1) * 16, m] = W2[m]
    w["Cmat_r"] = A(Cm.reshape(8, 128, 64).transpose(1, 0, 2))
    w["woutf"] = A(inp["L0_ffn_Wout"]).astype(f32)
    w["bin_c"] = A(inp["L0_ffn_bin_"].reshape(64, 1))
    w["b1flat_r"] = A(inp["L0_ffn_b1"].reshape(1024).reshape(8, 128).T)
    w["b2f_c"] = A(inp["L0_ffn_b2"].reshape(64, 1))
    w["bout_r"] = A(inp["L0_ffn_bout"].reshape(8, 128).T)
    wqkv = np.asarray(inp["L1_attn_w_qkv"])
    w["wqk_r"] = A(wqkv[:, 0:2048].reshape(8, 128, 2048).transpose(1, 0, 2))
    w["wv_r"] = A(wqkv[:, 2048:3072].reshape(8, 128, 1024).transpose(1, 0, 2))
    w["proj"] = A(inp["L1_attn_proj"]).astype(f32)
    w["wout1_r"] = A(inp["L1_attn_w_out"].reshape(8, 128, 1024).transpose(1, 0, 2))
    w["bout1_r"] = A(inp["L1_attn_b_out"].reshape(8, 128).T)
    wc = np.concatenate([np.asarray(inp["L1_ffn_router_w"]).T,
                         np.asarray(inp["L1_ffn_lang_w"]).T], axis=1)
    w["wcomb_r"] = A(wc.reshape(8, 128, 12).transpose(1, 0, 2))
    w["rbias_c"] = A(np.concatenate([np.asarray(inp["L1_ffn_router_b"]),
                                     np.asarray(inp["L1_ffn_lang_b"])]).reshape(12, 1))
    w["w1"] = np.asarray(inp["L1_ffn_w1"])
    w["w2"] = np.asarray(inp["L1_ffn_w2"])
    w["b1_r"] = A(np.asarray(inp["L1_ffn_b1"]).reshape(8, 32, 128).transpose(0, 2, 1))
    w["b2m"] = np.asarray(inp["L1_ffn_b2"])
    return w


WDEFS = [
    ("wq_r", [128, 8, 128]), ("wkv_r", [128, 8, 256]), ("wout0", [128, 1024]),
    ("gamma_r", [128, 8]), ("win_r", [128, 8, 64]), ("Bmat", [64, 1024]),
    ("Cmat_r", [128, 8, 64]), ("woutf", [64, 1024]), ("bin_c", [64, 1]),
    ("b1flat_r", [128, 8]), ("b2f_c", [64, 1]), ("bout_r", [128, 8]),
    ("wqk_r", [128, 8, 2048]), ("wv_r", [128, 8, 1024]), ("proj", [128, 64]),
    ("wout1_r", [128, 8, 1024]), ("bout1_r", [128, 8]), ("wcomb_r", [128, 8, 12]),
    ("rbias_c", [12, 1]), ("w1", [8, 1024, 4096]), ("w2", [8, 4096, 1024]),
    ("b1_r", [8, 128, 32]), ("b2m", [8, 1024]),
]


def host_core_slices(x, core):
    b, s = core // 2, core % 2
    base = s * OWN
    idx = np.clip(np.arange(base - HALO, base + OWN + HALO), 0, SQ - 1)
    x_ext = np.ascontiguousarray(np.asarray(x[b])[idx])
    x_part = np.ascontiguousarray(np.asarray(x[b])[(1 - s) * OWN:(2 - s) * OWN])
    left_edge = 0.0 if s == 0 else 1.0
    right_edge = 0.0 if s == 1 else 1.0
    flags = np.tile(np.array([[left_edge, right_edge]], dtype=np.float32), (128, 1))
    return x_ext, x_part, flags



class Ctx:
    pass


def make_ctx(phases=("b0", "b1", "moe"), debug=()):
    nc = bass.Bass("TRN2", target_bir_lowering=False)
    C = Ctx()
    C.nc = nc
    C.P = Prog(nc)
    C.es = ExitStack()
    C.phases = phases
    C.debug = list(debug)

    C.t_conv = [None] * 8
    dram = lambda n, s, d, k: nc.dram_tensor(n, s, d, kind=k).ap()
    C.in_x = dram("x_ext", [EXT, D], F32, "ExternalInput")
    C.in_xp = dram("x_part", [OWN, D], F32, "ExternalInput")
    C.in_flags = dram("d_flags", [128, 2], F32, "ExternalInput")
    C.dc = {}
    for n, v in host_constants().items():
        dt = F32 if v.dtype == np.float32 else BF16
        C.dc[n] = dram("d_" + n, list(v.shape), dt, "ExternalInput")
    C.wd = {n: dram("d_" + n, s, F32, "ExternalInput") for n, s in WDEFS}
    C.out_y = dram("y", [D, OWN], F32, "ExternalOutput")
    C.stream_dram = dram("sdram", [128, 8 * EXT], F32, "Internal")
    C.w2bf = nc.dram_tensor("w2bf", [8, 8, 128, 4096], BF16, kind="Internal").ap()
    C.w1bf = nc.dram_tensor("w1bf", [8, 8, 128, 4096], BF16, kind="Internal").ap()
    C.dbg = {}
    for name, shape in debug:
        C.dbg[name] = dram("dbg_" + name, list(shape), F32, "ExternalOutput")

    sb = lambda n, s, d: C.es.enter_context(nc.sbuf_tensor(n, s, d))
    psa = lambda n, s, d: C.es.enter_context(nc.psum_tensor(n, s, d))

    C.pA = psa("pA", [128, 1024], F32)
    C.pB = psa("pB", [128, 1024], F32)
    C.pC = psa("pC", [128, 512], F32)
    C.pD = psa("pD", [128, 512], F32)
    C.pE = psa("pE", [128, 512], F32)
    C.pF = psa("pF", [128, 1024], BF16)
    C.kvslot = [("pC", 0, 0), ("pC", 64, 0), ("pD", 0, 0), ("pD", 64, 0),
                ("pA", 0, 512), ("pA", 64, 512), ("pB", 0, 512), ("pB", 64, 512)]

    C.R1 = sb("R1", [128, 16432], F32)
    C.R1b = C.R1.ap().bitcast(BF16)
    C.R2 = sb("R2", [128, 16432], BF16)
    C.R2f = C.R2.ap().bitcast(F32)
    C.R3 = sb("R3", [128, 16432], BF16)
    C.R3f = C.R3.ap().bitcast(F32)
    C.U3 = sb("U3", [128, 6144], BF16)
    C.U3f = C.U3.ap().bitcast(F32)
    C.U5 = sb("U5", [128, 7168], BF16)
    C.U5f = C.U5.ap().bitcast(F32)

    C.eye_f = sb("eye_f", [128, 128], F32)
    C.eye_b = sb("eye_b", [128, 128], BF16)
    C.ones_lhs = sb("ones_lhs", [1, 128], BF16)
    C.ones_rhs = sb("ones_rhs", [128, 1], BF16)
    C.ones_row = sb("ones_row", [1, 512], BF16)
    C.P1a = sb("P1a", [16, 128], BF16)
    C.P1b = sb("P1b", [16, 128], BF16)
    C.P2 = sb("P2", [16, 128], BF16)
    C.bandA = sb("bandA", [128, 128], BF16)
    C.bandB = sb("bandB", [6, 128], BF16)
    C.sel4 = sb("sel4", [4, 4 * 128], BF16)
    C.cbias = sb("cbias", [128, 4], F32)
    C.sel8 = sb("sel8", [8, 8 * 128], BF16)
    C.Bmat = sb("Bmat", [64, 1024], BF16)
    C.Cmat = sb("Cmat", [128, 8 * 64], BF16)
    C.woutf = sb("woutf", [64, 1024], BF16)
    C.gamma = sb("gamma", [128, 8], F32)
    C.bin_c = sb("bin_c", [64, 1], F32)
    C.b1flat = sb("b1flat", [128, 8], F32)
    C.b2f = sb("b2f", [64, 1], F32)
    C.bout = sb("bout", [128, 8], F32)
    C.proj = sb("proj", [128, 64], BF16)
    C.bout1 = sb("bout1", [128, 8], F32)
    C.wcomb = sb("wcomb", [128, 8 * 12], BF16)
    C.rbias = sb("rbias", [12, 1], F32)
    C.b1sb = sb("b1sb", [128, 8 * 32], F32)
    C.b2sb = sb("b2sb", [8, 1024], BF16)
    C.flags = sb("flags", [128, 2], F32)
    C.stat6 = sb("stat6", [128, 6 * NTT], F32)
    C.bn12 = sb("bn12", [128, 12], F32)
    C.stat6fm = sb("stat6fm", [4, EXT], BF16)
    C.lnA = sb("lnA", [1, EXT], BF16)
    C.lnC = sb("lnC", [1, EXT], BF16)
    C.kvA = sb("kvA", [128, 8 * 17], BF16)
    C.kvB = sb("kvB", [128, 8 * 17], BF16)
    C.kvC = sb("kvC", [17, 8 * 17], BF16)
    C.Wfmb = sb("Wfmb", [8, OWN], BF16)
    C.sc1 = sb("sc1", [128, 8], F32)
    C.sc2 = sb("sc2", [128, 16], F32)
    C.lp = sb("lp", [128, 5], F32)
    C.rt = sb("rt", [128, 12], F32)
    C.sr = sb("sr", [1, 3 * 512], F32)
    C.rl = sb("rl", [12, 128], F32)
    C.qinv = sb("qinv", [128, 8], F32)

    C.stream = lambda ci, lo, w: C.R1[:, ci * EXT + lo: ci * EXT + lo + w]
    C.xn1 = lambda ci, lo, w: C.R2[:, ci * EXT + lo: ci * EXT + lo + w]
    C.v2 = lambda ci, lo, w: C.R3[:, ci * EXT + lo: ci * EXT + lo + w]
    C.xtok = lambda i: C.R2f[:, i * 1024:(i + 1) * 1024]
    C.u_part = lambda i: C.R2[:, 4096 + i * 1024: 4096 + (i + 1) * 1024]
    C.up_fm = lambda i: C.R2[:, 6144 + i * 1024: 6144 + (i + 1) * 1024]
    C.kv_tok = lambda i: C.R2[:, 8192 + i * 256: 8192 + (i + 1) * 256]
    C.kexp_s = [C.R2[:, 8704:10888], C.R2[:, 4096:6280]]
    C.v_ext_s = [C.R2[:, 10888:11024], C.R2[:, 6280:6416]]
    C.kexp = C.kexp_s[0]
    C.v_ext = C.v_ext_s[0]
    C.wkv = C.R2[:, 11024:13072]
    C.wq = C.R2[:, 13072:14096]
    C.wout0 = C.R2[:, 14096:15120]
    C.win = C.R2[:, 15120:15632]
    C.v1 = lambda kc, w: C.U3[:, kc * 512: kc * 512 + w]
    C.s_sb = C.U3[0:64, 4096:4608]
    C.xbf = lambda kc, w: C.U3[:, kc * 512: kc * 512 + w]
    C.w2buf = [C.U3[:, 0:4096], C.U5[:, 2048:6144]]
    C.kv_acc = C.U5f[0:17, 1024:3208]
    C.qhead = lambda h: C.U5[0:17, h * 512:(h + 1) * 512]
    C.qf_sb = C.U5[:, 4096:4608]
    C.rsb = C.U5[:, 4608:5120]
    C.qeA = C.U5[:, 5120:5632]
    C.qeB = C.U5[:, 5632:6144]
    C.atok0 = C.U5f[:, 3072:3200]
    C.afm0 = C.U5[:, 6400:6912]
    C.stA = lambda i: C.U5[:, i * 128:(i + 1) * 128]
    C.stB = lambda i: C.U5[0:6, 512 + i * 128: 640 + i * 128]
    C.atok1 = C.U5[:, 1024:2048]
    C.afm1 = lambda kc, w: C.U5[:, 2048 + kc * 512: 2048 + kc * 512 + w]
    C.qkf = lambda i: C.U5[:, 6144 + i * 512: 6144 + (i + 1) * 512]
    C.t_sb1 = lambda i: C.U3f[:, 1024 + i * 512: 1536 + i * 512]
    C.t_sb = lambda i: C.U5f[:, i * 512:(i + 1) * 512]
    C.kphi = lambda kp, lo, w: C.R1b[:, kp * EXT + lo: kp * EXT + lo + w]
    C.qphi = lambda kp, lo, w: C.R1b[:, 8216 + kp * EXT + lo: 8216 + kp * EXT + lo + w]
    C.Vsb = lambda i: C.R1b[:, 16432 + i * 1032: 16432 + (i + 1) * 1032]
    C.wv = lambda kc: C.R1b[:, 19528 + kc * 1024: 19528 + (kc + 1) * 1024]
    C.wqk_rot = lambda i: C.R1b[:, 27720 + i * 1024: 27720 + (i + 1) * 1024]
    C.wout1_rot = lambda i: C.R1b[:, 27720 + i * 1024: 27720 + (i + 1) * 1024]
    C.w1sb = lambda kc: C.R1b[:, kc * 4096:(kc + 1) * 4096]
    C.h_sb = lambda hch, w: C.R3[:, hch * 512: hch * 512 + w]
    C.w_rep = lambda i: C.U3f[:, 2048 + i * 512: 2048 + (i + 1) * 512]
    return C


def load_feeds(C, inp, core):
    feeds = {}
    x_ext, x_part, flags = host_core_slices(inp["L0_attn_x"], core)
    feeds["x_ext"] = x_ext
    feeds["x_part"] = x_part
    feeds["d_flags"] = flags
    for n, v in host_constants().items():
        feeds["d_" + n] = v
    hw = host_weights(inp)
    for n, _ in WDEFS:
        feeds["d_" + n] = np.ascontiguousarray(hw[n], dtype=np.float32)
    return feeds



def emit_setup(C):
    P = C.P
    for name, dst in [("eye_f", C.eye_f), ("eye_b", C.eye_b), ("ones_lhs", C.ones_lhs),
                      ("ones_rhs", C.ones_rhs), ("ones_row", C.ones_row),
                      ("P1a", C.P1a), ("P1b", C.P1b), ("P2", C.P2),
                      ("bandA", C.bandA), ("bandB", C.bandB), ("sel4", C.sel4),
                      ("cvec", C.cbias), ("sel8", C.sel8)]:
        P.dma("sync", dst[:], C.dc[name])
    for dst, src in [
        (C.wq, C.wd["wq_r"].rearrange("p c m -> p (c m)")),
        (C.wkv, C.wd["wkv_r"].rearrange("p c m -> p (c m)")),
        (C.wout0, C.wd["wout0"]),
        (C.win, C.wd["win_r"].rearrange("p c m -> p (c m)")),
        (C.Bmat[:], C.wd["Bmat"]),
        (C.Cmat[:], C.wd["Cmat_r"].rearrange("p c m -> p (c m)")),
        (C.woutf[:], C.wd["woutf"]),
        (C.proj[:], C.wd["proj"]),
        (C.wcomb[:], C.wd["wcomb_r"].rearrange("p c m -> p (c m)")),
        (C.b2sb[:], C.wd["b2m"]),
    ]:
        P.dma("gpsimd", dst, src)
    for dst, src in [
        (C.gamma[:], C.wd["gamma_r"]), (C.bin_c[:], C.wd["bin_c"]),
        (C.b1flat[:], C.wd["b1flat_r"]), (C.b2f[:], C.wd["b2f_c"]),
        (C.bout[:], C.wd["bout_r"]), (C.bout1[:], C.wd["bout1_r"]),
        (C.rbias[:], C.wd["rbias_c"]), (C.flags[:], C.in_flags),
        (C.b1sb[:], C.wd["b1_r"].rearrange("e p c -> p e c")),
    ]:
        P.dma("sync", dst, src)
    C.t_su_sync = P.qtok("sync")
    C.t_su_gp = P.qtok("gpsimd")
    P.wait("vector", C.t_su_sync)
    P.wait("vector", C.t_su_gp)
    for kc in range(8):
        P.op("vector", lambda v, kc=kc: v.tensor_scalar(
            C.wq[:, kc * 128:(kc + 1) * 128], C.wq[:, kc * 128:(kc + 1) * 128],
            C.gamma[:, kc:kc + 1], None, ALU.mult))
        P.op("vector", lambda v, kc=kc: v.tensor_scalar(
            C.wkv[:, kc * 256:(kc + 1) * 256], C.wkv[:, kc * 256:(kc + 1) * 256],
            C.gamma[:, kc:kc + 1], None, ALU.mult))
    C.t_gamma = P.inc("vector")


def emit_stats_tile(C, x, t, col0, w, gate_pool, rows4=True):
    P = C.P
    st = C.stat6[:, 6 * t: 6 * t + 6]
    P.op("vector", lambda v: v.bn_stats(C.bn12[:, 0:6], x[:, 0:512]))
    P.op("vector", lambda v: v.bn_stats(C.bn12[:, 6:12], x[:, 512:1024]))
    t_sq = P.inc("vector")
    P.wait("vector", t_sq)
    P.op("vector", lambda v: v.bn_aggr(st[:, 4:6], C.bn12[:]))
    tv = P.inc("vector")
    P.wait("vector", tv)
    P.op("vector", lambda v: v.tensor_scalar(st[:, 5:6], st[:, 5:6], 1e-30, None, ALU.max))
    tvar = P.inc("vector")
    P.wait("scalar", tvar)
    P.op("scalar", lambda s: s.activation(C.sc2[:, 1:2], st[:, 5:6], AF.Ln, bias=C.cbias[:, 0:1]))
    P.op("scalar", lambda s: s.activation(C.sc2[:, 2:3], st[:, 5:6], AF.Ln))
    ts = P.inc("scalar")
    P.wait("scalar", ts)
    P.op("scalar", lambda s: s.activation(st[:, 0:1], C.sc2[:, 1:2], AF.Exp, scale=-0.5))
    P.op("scalar", lambda s: s.activation(st[:, 2:3], C.sc2[:, 2:3], AF.Exp, scale=-0.5))
    ts = P.inc("scalar")
    P.wait("vector", ts)
    P.op("vector", lambda v: v.tensor_scalar(C.sc2[:, 3:4], st[:, 4:5], -1.0, None, ALU.mult))
    tv = P.inc("vector")
    P.wait("vector", tv)
    P.op("vector", lambda v: v.tensor_scalar(st[:, 1:2], C.sc2[:, 3:4], st[:, 0:1], None, ALU.mult))
    P.op("vector", lambda v: v.tensor_scalar(st[:, 3:4], C.sc2[:, 3:4], st[:, 2:3], None, ALU.mult))
    tv = P.inc("vector")
    if not rows4:
        return t_sq, tv
    P.wait("tensor", tv)
    P.wait("tensor", getattr(C, "t_stats_ev", None))
    P.op("tensor", lambda T: T.transpose(C.pE[0:4, 0:128], st[:, 0:4], C.eye_f[:]))
    tt = P.inc("tensor")
    P.wait("vector", tt)
    P.op("vector", lambda v: v.tensor_copy(C.stat6fm[:, col0:col0 + w], C.pE[0:4, 0:w]))
    te = P.inc("vector")
    C.t_stats_ev = te
    return t_sq, te


def emit_load_x(C):
    P = C.P
    P.wait("vector", C.t_su_sync)
    P.wait("tensor", C.t_su_sync)
    buf_free = [None, None]
    gate_pool = None
    t_pe_prev = [None, None]
    for t in range(NTT):
        buf = t % 2
        w = twidth(t)
        xt = C.xtok(buf)
        if t == 16:
            P.wait("vector", buf_free[buf])
            P.op("vector", lambda v, xt=xt: v.memset(xt[:], 0.0))
            tz = P.inc("vector")
            buf_free[buf] = tz
        P.wait("sync", buf_free[buf])
        tl = P.dma("sync", xt[0:w, :], C.in_x[t * 128: t * 128 + w, :])
        P.wait("vector", tl)
        P.wait("gpsimd", tl)
        pz = (C.pA, C.pB)[t % 2]
        P.wait("tensor", tl)
        P.wait("tensor", t_pe_prev[t % 2])
        for kc in range(8):
            P.op("tensor", lambda T, kc=kc, xt=xt, pz=pz: T.transpose(
                pz[:, kc * 128:(kc + 1) * 128], xt[:, kc * 128:(kc + 1) * 128],
                C.eye_f[:]))
        tt = P.inc("tensor")
        gate_pool, t_stats = emit_stats_tile(C, xt, t, t * 128, w, gate_pool)
        for z in range(2):
            if 2 * t + z < 34:
                emit_w2_convert_step(C, 2 * t + z)
        P.wait("scalar", tt)
        evtok = None
        for kc in range(8):
            P.op("scalar", lambda sc, kc=kc, t=t, w=w, pz=pz: sc.copy(
                C.stream(kc, t * 128, w), pz[:, kc * 128: kc * 128 + w]))
            evtok = P.inc("scalar")
        buf_free[buf] = evtok
        t_pe_prev[t % 2] = evtok
    C.t_xload = [P.tok("scalar"), P.tok("vector")]
    C.t_stats_done = C.t_xload


def emit_vphase(C):
    P = C.P


def emit_v_window(C, lo, w, which, gate_mm=None, gate_rep=None):
    P = C.P
    r0 = 2 * which
    selA = C.sel4[:, r0 * 128:(r0 + 1) * 128]
    selC = C.sel4[:, (r0 + 1) * 128:(r0 + 2) * 128]
    P.wait("tensor", gate_mm)
    P.op("tensor", lambda T: T.matmul(C.pB[:, 0:w], selA, C.stat6fm[:, lo:lo + w],
                                      start=True, stop=True))
    P.op("tensor", lambda T: T.matmul(C.pB[:, 512:512 + w], selC, C.stat6fm[:, lo:lo + w],
                                      start=True, stop=True))
    tmm = P.inc("tensor")
    P.wait("vector", tmm)
    P.wait("vector", gate_rep)
    P.op("vector", lambda v: v.tensor_copy(C.rsb[:, 0:w], C.pB[:, 0:w]))
    P.op("vector", lambda v: v.tensor_copy(C.qf_sb[:, 0:w], C.pB[:, 512:512 + w]))
    trep = P.inc("vector")
    toks = []
    for kc in range(8):
        dst = C.v1(kc, w) if which == 0 else C.v2(kc, lo, w)
        P.wait("vector", trep)
        P.op("vector", lambda v, kc=kc, dst=dst, lo=lo, w=w: v.tensor_tensor(
            dst, C.stream(kc, lo, w), C.rsb[:, 0:w], ALU.mult))
        tv = P.inc("vector")
        P.wait("gpsimd", tv)
        P.op("gpsimd", lambda g, kc=kc, dst=dst, w=w: g.tensor_tensor(
            dst, dst, C.qf_sb[:, 0:w], ALU.add))
        toks.append(P.inc("gpsimd"))
    return tmm, trep, toks[-1]


def emit_ffn0_window(C, wi, lo, w, t_v1):
    P = C.P
    h0 = C.U5[:, 0:512]
    h1 = C.U5[:, 512:1024]
    g_sb = C.U5[0:64, 1024:1536]
    P.wait("tensor", t_v1)
    P.wait("tensor", getattr(C, "t_ffn_out_prev", None))
    for kc in range(8):
        P.op("tensor", lambda T, kc=kc: T.matmul(
            C.pA[0:64, 0:w], C.win[0:128, kc * 64:(kc + 1) * 64], C.v1(kc, w),
            start=(kc == 0), stop=(kc == 7)))
    tsm = P.inc("tensor")
    P.wait("scalar", tsm)
    P.op("scalar", lambda s: s.activation(C.s_sb[0:64, 0:w], C.pA[0:64, 0:w],
                                          AF.Identity, bias=C.bin_c[0:64, 0:1]))
    t_s = P.inc("scalar")
    t_h_ev = [None, None]
    P.wait("tensor", t_s)
    for mch in range(8):
        hb = (h0, h1)[mch % 2]
        pb = C.pB[:, (mch % 2) * 512:(mch % 2) * 512 + w]
        P.wait("tensor", t_h_ev[mch % 2])
        P.op("tensor", lambda T, mch=mch, pb=pb: T.matmul(
            pb, C.Bmat[0:64, mch * 128:(mch + 1) * 128], C.s_sb[0:64, 0:w],
            start=True, stop=True))
        tmm = P.inc("tensor")
        P.wait("scalar", tmm)
        P.op("scalar", lambda s, mch=mch, hb=hb, pb=pb: s.activation(
            hb[:, 0:w], pb, AF.Relu, bias=C.b1flat[:, mch:mch + 1]))
        the = P.inc("scalar")
        t_h_ev[mch % 2] = the
        P.wait("tensor", the)
        P.op("tensor", lambda T, mch=mch, hb=hb: T.matmul(
            C.pE[0:64, 0:w], C.Cmat[:, mch * 64:(mch + 1) * 64], hb[:, 0:w],
            start=(mch == 0), stop=(mch == 7)))
        tg = P.inc("tensor")
        t_h_ev[mch % 2] = tg
    P.wait("scalar", tg)
    P.op("scalar", lambda s: s.activation(g_sb[0:64, 0:w], C.pE[0:64, 0:w],
                                          AF.Identity, bias=C.b2f[0:64, 0:1]))
    t_gs = P.inc("scalar")
    P.wait("tensor", t_gs)
    t_o_ev = [None, None]
    last = None
    for dch in range(8):
        po = C.pA[:, (dch % 2) * 512:(dch % 2) * 512 + w]
        P.wait("tensor", t_o_ev[dch % 2])
        P.op("tensor", lambda T, dch=dch, po=po: T.matmul(
            po, C.woutf[0:64, dch * 128:(dch + 1) * 128], g_sb[0:64, 0:w],
            start=True, stop=True))
        tmm = P.inc("tensor")
        P.wait("vector", tmm)
        P.op("vector", lambda v, dch=dch, po=po, lo=lo, w=w: v.scalar_tensor_tensor(
            C.stream(dch, lo, w), po, C.bout[:, dch:dch + 1], C.stream(dch, lo, w),
            ALU.add, ALU.add))
        tev = P.inc("vector")
        t_o_ev[dch % 2] = tev
        last = tev
    C.t_ffn_out_prev = last
    return last


def emit_kside_tile(C, fm_lhs, i, n_tiles_done, first, last, gate, slot=0):
    P = C.P
    kvt = C.kv_tok(i % 2)
    pe = (C.pE[:, 0:256], C.pE[:, 256:512])[i % 2]
    if not hasattr(C, "t_kvtok_ev2"):
        C.t_kvtok_ev2 = [None, None]
    P.wait("tensor", C.t_kvtok_ev2[i % 2])
    for kc in range(8):
        P.op("tensor", lambda T, kc=kc, pe=pe: T.matmul(
            pe, fm_lhs(kc), C.wkv[:, kc * 256:(kc + 1) * 256],
            start=(kc == 0), stop=(kc == 7)))
    tkv = P.inc("tensor")
    P.wait("scalar", tkv)
    P.op("scalar", lambda sc, kvt=kvt, pe=pe: sc.copy(kvt[:], pe))
    tev = P.inc("scalar")
    C.t_kvtok_ev2[i % 2] = tev
    kx = C.kexp_s[slot]
    vx = C.v_ext_s[slot]
    for eng in ("vector", "gpsimd"):
        P.wait(eng, gate)
        P.wait(eng, tev)
    def tt_build(e, h, kvt=kvt, kx=kx):
        ko = 273 * h
        e.tensor_tensor(
            kx[:, ko:ko + 256].rearrange("p (i j) -> p i j", j=16),
            kvt[:, 16 * h:16 * h + 16].unsqueeze(2).broadcast_to([128, 16, 16]),
            kvt[:, 16 * h:16 * h + 16].unsqueeze(1).broadcast_to([128, 16, 16]),
            ALU.mult)
    for h in range(4):
        P.op("vector", lambda v, h=h: tt_build(v, h))
    P.op("vector", lambda v, kvt=kvt, kx=kx: v.tensor_copy(
        kx[:].rearrange("p (h j) -> p h j", j=273)[:, :, 256:272],
        kvt[:, 0:128].rearrange("p (h j) -> p h j", j=16)))
    tbv = P.inc("vector")
    for h in range(4, 8):
        P.op("gpsimd", lambda g, h=h: tt_build(g, h))
    P.op("gpsimd", lambda g, kvt=kvt, vx=vx: g.tensor_copy(
        vx[:].rearrange("p (h j) -> p h j", j=17)[:, :, 0:16],
        kvt[:, 128:256].rearrange("p (h j) -> p h j", j=16)))
    tbg = P.inc("gpsimd")
    P.wait("tensor", tbv)
    P.wait("tensor", tbg)
    if first:
        P.wait("tensor", C.t_ks_vgate)
        P.wait("tensor", C.t_ks_sgate)
    for h in range(8):
        bn, bs, co = C.kvslot[h]
        bank = getattr(C, bn)
        P.op("tensor", lambda T, h=h, bank=bank, bs=bs, co=co, kx=kx, vx=vx:
             T.matmul(bank[bs:bs + 17, co:co + 273],
                      vx[:, 17 * h:17 * h + 17], kx[:, 273 * h:273 * h + 273],
                      start=first, stop=last, skip_group_check=True))
    return P.inc("tensor")


def emit_partner(C):
    P = C.P
    P.wait("vector", C.t_v2_done)
    C.t_ks_sgate = ("scalar", P.cnt["scalar"])
    P.op("vector", lambda v: v.memset(C.kv_acc[:], 0.0))
    P.op("vector", lambda v: v.memset(
        C.kexp_s[0][:].rearrange("p (h j) -> p h j", j=273)[:, :, 272:273], 1.0))
    P.op("vector", lambda v: v.memset(
        C.v_ext_s[0][:].rearrange("p (h j) -> p h j", j=17)[:, :, 16:17], 1.0))
    t0 = P.inc("vector")
    C.t_ks_vgate = t0
    C.t_kv_zero = t0
    buf_free = [C.t_xload, C.t_xload]
    C.t_up_r = [None, None]
    C.t_uf_r = [None, None]
    C.t_upfm_ev = None
    gate_pool = ("tensor", P.cnt["tensor"])
    gate_kexp = t0
    for i in range(16):
        buf = i % 2
        xp = C.xtok(buf)
        P.wait("sync", buf_free[buf])
        tl = P.dma("sync", xp[:], C.in_xp[i * 128:(i + 1) * 128, :])
        P.wait("vector", tl)
        P.wait("gpsimd", tl)
        gate_pool, trc = emit_stats_tile(C, xp, 0, 0, 128, gate_pool, rows4=False)
        if 34 + i < 48:
            emit_w2_convert_step(C, 34 + i)
        up = C.u_part(buf)
        P.wait("scalar", trc)
        P.wait("scalar", C.t_up_r[buf])
        P.op("scalar", lambda s, xp=xp, up=up: s.activation(
            up[:], xp[:], AF.Identity, bias=C.stat6[:, 3:4], scale=C.stat6[:, 2:3]))
        tup = P.inc("scalar")
        P.wait("tensor", tup)
        P.wait("tensor", C.t_upfm_ev)
        for kc in range(8):
            P.op("tensor", lambda T, kc=kc, up=up: T.transpose(
                C.pF[:, kc * 128:(kc + 1) * 128], up[:, kc * 128:(kc + 1) * 128],
                C.eye_b[:]))
        ttr = P.inc("tensor")
        C.t_up_r[buf] = ttr
        P.wait("vector", ttr)
        P.wait("vector", C.t_uf_r[buf])
        uf = C.up_fm(buf)
        P.op("vector", lambda v, uf=uf: v.tensor_copy(uf[:], C.pF[:]))
        tuf = P.inc("vector")
        C.t_upfm_ev = tuf
        buf_free[buf] = tuf
        P.wait("tensor", tuf)
        gate_kexp = emit_kside_tile(
            C, lambda kc, uf=uf: uf[:, kc * 128:(kc + 1) * 128],
            i, i, i == 0, i == 15, gate_kexp)
        C.t_uf_r[buf] = ("tensor", P.cnt["tensor"])
    C.gate_kexp = gate_kexp


def emit_own_kside(C):
    P = C.P
    P.wait("gpsimd", C.gate_kexp)
    P.op("gpsimd", lambda g: g.memset(
        C.kexp_s[1][:].rearrange("p (h j) -> p h j", j=273)[:, :, 272:273], 1.0))
    P.op("gpsimd", lambda g: g.memset(
        C.v_ext_s[1][:].rearrange("p (h j) -> p h j", j=17)[:, :, 16:17], 1.0))
    P.wait("vector", P.inc("gpsimd"))
    gates = [C.gate_kexp, C.gate_kexp]
    for i in range(16):
        lo = 3 + i * 128
        gates[i % 2] = emit_kside_tile(
            C, lambda kc, lo=lo: C.v2(kc, lo, 128), i, 16 + i,
            False, i == 15, gates[i % 2], slot=i % 2)
    C.gate_kexp = gates[15 % 2]
    P.wait("vector", gates[1])
    for h in range(8):
        bn, bs, co = C.kvslot[h]
        bank = getattr(C, bn)
        P.op("vector", lambda v, h=h, bank=bank, bs=bs, co=co: v.tensor_copy(
            C.kv_acc[0:17, 273 * h:273 * h + 273], bank[bs:bs + 17, co:co + 273]))
    C.t_kvacc_ev = P.inc("vector")


def emit_kv_finalize(C):
    P = C.P
    P.wait("tensor", C.t_kvacc_ev)
    for h in range(8):
        base = 273 * h
        for j, dst in ((0, C.kvA), (1, C.kvB)):
            P.op("tensor", lambda T, h=h, base=base, j=j: T.transpose(
                C.pE[:, 0:17] if j == 0 else C.pE[:, 17:34],
                C.kv_acc[0:17, base + j * 128: base + (j + 1) * 128],
                C.eye_f[0:17, 0:17]))
        P.op("tensor", lambda T, h=h, base=base: T.transpose(
            C.pE[0:17, 34:51], C.kv_acc[0:17, base + 256: base + 273],
            C.eye_f[0:17, 0:17]))
        tt = P.inc("tensor")
        P.wait("vector", tt)
        P.op("vector", lambda v, h=h: v.tensor_copy(
            C.kvA[:, 17 * h:17 * h + 17], C.pE[:, 0:17]))
        P.op("vector", lambda v, h=h: v.tensor_copy(
            C.kvB[:, 17 * h:17 * h + 17], C.pE[:, 17:34]))
        P.op("vector", lambda v, h=h: v.tensor_copy(
            C.kvC[0:17, 17 * h:17 * h + 17], C.pE[0:17, 34:51]))
        tev = P.inc("vector")
        P.wait("tensor", tev)
    C.t_kv_ready = P.tok("vector")


def emit_qphase(C):
    P = C.P
    P.wait("sync", C.t_kv_ready)
    for h in range(8):
        P.dma("sync", C.qhead(h)[16:17, 0:512], C.dc["ones_row"])
    t_ones = P.qtok("sync")
    t_wout_ev = [None, None]
    t_qf_dma = None
    t_pcpd_free = None
    for wi, (lo, w) in enumerate(WINDOWS):
        ntc = (w + 127) // 128
        P.wait("tensor", t_wout_ev[0])
        P.wait("tensor", t_wout_ev[1])
        for kc in range(8):
            P.op("tensor", lambda T, kc=kc, lo=lo, w=w: T.matmul(
                C.pA[:, 0:w], C.wq[:, kc * 128:(kc + 1) * 128], C.v2(kc, lo, w),
                start=(kc == 0), stop=(kc == 7)))
        tqm = P.inc("tensor")
        P.wait("scalar", tqm)
        P.wait("scalar", t_qf_dma)
        P.op("scalar", lambda s, w=w: s.activation(C.qf_sb[:, 0:w], C.pA[:, 0:w],
                                                   AF.Identity, scale=0.25))
        tqf = P.inc("scalar")
        P.wait("scalar", t_ones)
        P.wait("scalar", tqf)
        P.wait("scalar", getattr(C, "t_qhead_free", None))
        for h in range(8):
            P.dma("scalar", C.qhead(h)[0:16, 0:w], C.qf_sb[16 * h:16 * h + 16, 0:w])
        t_qf_dma = P.qtok("scalar")
        t_evA = None
        t_evB = None
        t_out_done = []
        for h in range(8):
            P.wait("tensor", t_qf_dma)
            P.wait("tensor", t_evB)
            P.wait("tensor", t_evA)
            P.op("tensor", lambda T, h=h, w=w: T.matmul(
                C.pB[:, 0:w], C.P1a[:], C.qhead(h)[0:16, 0:w], start=True, stop=True))
            P.op("tensor", lambda T, h=h, w=w: T.matmul(
                C.pB[:, 512:512 + w], C.P2[:], C.qhead(h)[0:16, 0:w], start=True, stop=True))
            tra = P.inc("tensor")
            P.wait("vector", tra)
            P.op("vector", lambda v, w=w: v.tensor_copy(C.rsb[:, 0:w], C.pB[:, 0:w]))
            tcp = P.inc("vector")
            P.wait("vector", tcp)
            P.op("vector", lambda v, w=w: v.tensor_tensor(
                C.qeA[:, 0:w], C.rsb[:, 0:w], C.pB[:, 512:512 + w], ALU.mult))
            t_evA = P.inc("vector")
            P.wait("tensor", t_evA)
            P.wait("tensor", t_evB)
            P.op("tensor", lambda T, h=h, w=w: T.matmul(
                C.pB[:, 0:w], C.P1b[:], C.qhead(h)[0:16, 0:w], start=True, stop=True))
            P.op("tensor", lambda T, h=h, w=w: T.matmul(
                C.pB[:, 512:512 + w], C.P2[:], C.qhead(h)[0:16, 0:w], start=True, stop=True))
            trb = P.inc("tensor")
            P.wait("vector", trb)
            P.op("vector", lambda v, w=w: v.tensor_copy(C.rsb[:, 0:w], C.pB[:, 0:w]))
            tcp = P.inc("vector")
            P.wait("vector", tcp)
            P.op("vector", lambda v, w=w: v.tensor_tensor(
                C.qeB[:, 0:w], C.rsb[:, 0:w], C.pB[:, 512:512 + w], ALU.mult))
            t_evB = P.inc("vector")
            P.wait("tensor", t_evB)
            if h == 0:
                P.wait("tensor", t_pcpd_free)
            for tc in range(ntc):
                tw = min(128, w - tc * 128)
                pk = (C.pC, C.pC, C.pD, C.pD)[tc]
                off = (0, 136, 0, 136)[tc]
                P.op("tensor", lambda T, h=h, tc=tc, tw=tw, pk=pk, off=off: T.matmul(
                    pk[0:tw, off + 17 * h: off + 17 * h + 17],
                    C.qeA[:, tc * 128: tc * 128 + tw], C.kvA[:, 17 * h:17 * h + 17],
                    start=True, stop=False))
                P.op("tensor", lambda T, h=h, tc=tc, tw=tw, pk=pk, off=off: T.matmul(
                    pk[0:tw, off + 17 * h: off + 17 * h + 17],
                    C.qeB[:, tc * 128: tc * 128 + tw], C.kvB[:, 17 * h:17 * h + 17],
                    start=False, stop=False))
                P.op("tensor", lambda T, h=h, tc=tc, tw=tw, pk=pk, off=off, lo=lo:
                     T.matmul(pk[0:tw, off + 17 * h: off + 17 * h + 17],
                              C.qhead(h)[0:17, tc * 128: tc * 128 + tw],
                              C.kvC[0:17, 17 * h:17 * h + 17],
                              start=False, stop=True))
            tmo = P.inc("tensor")
            t_out_done.append(tmo)
            C.t_qhead_free = tmo
        t_tr = []
        t_at_free = None
        t_af_ev = [tqf, tqf]
        for tc in range(ntc):
            tw = min(128, w - tc * 128)
            pk = (C.pC, C.pC, C.pD, C.pD)[tc]
            off = (0, 136, 0, 136)[tc]
            pv = pk[0:tw, off:off + 136].rearrange("p (h e) -> p h e", e=17)
            P.wait("vector", t_out_done[-1])
            P.wait("vector", t_at_free)
            P.op("vector", lambda v, pv=pv, tw=tw: v.tensor_scalar(
                C.sc2[0:tw, 0:8].unsqueeze(2), pv[:, :, 16:17], 1e-5, None, ALU.max))
            tv = P.inc("vector")
            P.wait("vector", tv)
            P.op("vector", lambda v, tw=tw: v.reciprocal(
                C.qinv[0:tw, 0:8], C.sc2[0:tw, 0:8]))
            tv = P.inc("vector")
            P.wait("vector", tv)
            P.op("vector", lambda v, pv=pv, tw=tw: v.tensor_tensor(
                C.atok0[0:tw, 0:128].rearrange("p (h e) -> p h e", e=16),
                pv[:, :, 0:16],
                C.qinv[0:tw, 0:8].unsqueeze(2).broadcast_to([tw, 8, 16]), ALU.mult))
            tat = P.inc("vector")
            P.wait("tensor", tat)
            P.wait("tensor", t_af_ev[tc % 2])
            P.op("tensor", lambda T, tc=tc, tw=tw: T.transpose(
                C.pA[:, (tc % 2) * 512:(tc % 2) * 512 + tw], C.atok0[0:tw, 0:128],
                C.eye_f[0:tw, 0:tw]))
            tt = P.inc("tensor")
            t_at_free = tt
            P.wait("vector", tt)
            P.op("vector", lambda v, tc=tc, tw=tw: v.tensor_copy(
                C.afm0[:, tc * 128: tc * 128 + tw],
                C.pA[:, (tc % 2) * 512:(tc % 2) * 512 + tw]))
            t_tr.append(P.inc("vector"))
            t_af_ev[tc % 2] = t_tr[-1]
        t_pcpd_free = ("vector", C.P.cnt["vector"])
        P.wait("tensor", t_tr[-1])
        for dch in range(8):
            po = C.pA[:, (dch % 2) * 512:(dch % 2) * 512 + w]
            P.wait("tensor", t_wout_ev[dch % 2])
            P.op("tensor", lambda T, dch=dch, po=po, w=w: T.matmul(
                po, C.wout0[:, dch * 128:(dch + 1) * 128], C.afm0[:, 0:w],
                start=True, stop=True))
            tmm = P.inc("tensor")
            P.wait("vector", tmm)
            P.op("vector", lambda v, dch=dch, po=po, lo=lo, w=w: v.scalar_tensor_tensor(
                C.stream(dch, lo, w), po, 0.0, C.stream(dch, lo, w),
                ALU.add, ALU.add))
            t_wout_ev[dch % 2] = P.inc("vector")
    C.t_block0 = P.tok("vector")


def emit_final_store(C, src="dram"):
    P = C.P
    if C.dbg:
        P.wait_q_all("vector", "gpsimd")
        P.wait_q_all("vector", "sync")
    if src == "dram":
        P.wait("sync", C.t_moe_done)
    else:
        P.wait("sync", C.t_block0)
        C.stream_dram = C.R1
    for kc in range(8):
        P.dma("sync", C.out_y[kc * 128:(kc + 1) * 128, :],
              (C.stream_dram[:, kc * EXT + 3: kc * EXT + 3 + OWN] if src == "dram"
               else C.R1[:, kc * EXT + 3: kc * EXT + 3 + OWN]))
    P.wait_q_all("sync", "sync")
    P.wait_q_all("sync", "gpsimd")
    P.wait_q_all("sync", "scalar")


def emit_debug_stores(C):
    P = C.P
    if "stream" in C.dbg:
        P.wait("sync", C.t_block0)
        P.dma("sync", C.dbg["stream"], C.R1[:, 0:8 * EXT])
    if "statfm" in C.dbg:
        P.wait("gpsimd", C.t_stats_done)
        P.dma("gpsimd", C.dbg["statfm"], C.stat6fm[:])
    if "v2" in C.dbg:
        P.wait("gpsimd", C.t_v2_done)
        P.dma("gpsimd", C.dbg["v2"], C.R3[:, 0:8 * EXT])
    if "xn1" in C.dbg:
        P.wait("gpsimd", C.t_xn1)
        P.dma("gpsimd", C.dbg["xn1"], C.R2[:, 0:8 * EXT])
    if "phi" in C.dbg:
        P.wait("gpsimd", C.t_phi)
        P.dma("gpsimd", C.dbg["phi"], C.R1b[:, 0:8 * EXT])
    if "afm" in C.dbg:
        P.wait("gpsimd", C.t_attn1)
        P.dma("gpsimd", C.dbg["afm"], C.U5[:, 2048:6144])
    if "atok" in C.dbg:
        P.wait("gpsimd", C.t_attn1)
        P.dma("gpsimd", C.dbg["atok"], C.atok1)
    if "sdram" in C.dbg:
        P.wait("gpsimd", C.t_moe_done)
        P.dma("gpsimd", C.dbg["sdram"], C.stream_dram[:, 0:8 * EXT])
    if "wfm" in C.dbg:
        P.wait("gpsimd", C.t_router)
        P.dma("gpsimd", C.dbg["wfm"], C.Wfmb[:])


def emit_block0(C):
    emit_load_x(C)
    P = C.P
    prev_trep = C.t_xload
    prev_pool = None
    t_prev_s = None
    P.wait("tensor", C.t_xload)
    for wi, (lo, w) in enumerate(WINDOWS):
        tmm, trep, tpool = emit_v_window(C, lo, w, 0, gate_mm=prev_trep,
                                         gate_rep=prev_pool)
        tmm2, trep2, tpool2 = emit_v_window(C, lo, w, 1, gate_mm=trep,
                                            gate_rep=tpool)
        prev_trep = trep2
        prev_pool = tpool2
        P.wait("tensor", trep2)
        t_prev_s = emit_ffn0_window(C, wi, lo, w, tpool)
    C.t_v2_done = prev_pool
    C.t_ffn0 = t_prev_s
    C.P.wait("vector", C.t_ffn0)
    emit_partner(C)
    emit_own_kside(C)
    emit_kv_finalize(C)
    emit_qphase(C)


def build(phases=("b0", "b1", "moe"), debug=()):
    C = make_ctx(phases, debug)
    emit_setup(C)
    if "load" in phases:
        emit_load_x(C)
        C.t_block0 = C.t_xload
    if "loadd" in phases:
        emit_load_x(C)
        C.t_block0 = C.t_xload
        C.P.wait("sync", C.t_xload)
        C.t_offload = C.P.dma("sync", C.stream_dram[:, 0:8 * EXT], C.R1[:, 0:8 * EXT])
        C.t_moe_done = C.t_offload
    if "b0" in phases:
        emit_block0(C)
    if "b1" in phases:
        emit_block1(C, with_b2=("moe" in phases))
    if "moe" in phases:
        emit_moe(C)
    emit_debug_stores(C)
    emit_final_store(C, src="dram" if ("moe" in phases or "b1" in phases
                                       or "loadd" in phases) else "stream")
    C.P.emit(C.es)
    C.es.close()
    return C


def run_cores(C, inp, cores, trace=False):
    feeds = [load_feeds(C, inp, c) for c in cores]
    res = run_bass_kernel_spmd(C.nc, feeds, core_ids=list(range(len(cores))),
                               trace=trace)
    for r in res.results:
        r["y"] = np.ascontiguousarray(r["y"].T)
    return res


def emit_ln1(C):
    P = C.P
    sq = [C.U5[:, 0:512], C.U5[:, 512:1024]]
    arep_sb = C.U5[:, 1024:1536]
    crep_sb = C.U5[:, 1536:2048]
    t_prev_rep = None
    t_prev_xbf = C.t_block0
    t_prev_pool = None
    t_prev_mm = C.t_block0
    for wi, (lo, w) in enumerate(WINDOWS):
        t_sq_prev = [t_prev_pool, t_prev_pool]
        P.wait("vector", t_prev_xbf)
        P.wait("tensor", t_prev_mm)
        tkc = []
        for kc in range(8):
            P.op("vector", lambda v, kc=kc, lo=lo, w=w: v.tensor_copy(
                C.xbf(kc, w), C.stream(kc, lo, w)))
            txb = P.inc("vector")
            P.wait("gpsimd", txb)
            P.wait("gpsimd", t_sq_prev[kc % 2])
            P.op("gpsimd", lambda g, kc=kc, w=w: g.tensor_tensor(
                sq[kc % 2][:, 0:w], C.xbf(kc, w), C.xbf(kc, w), ALU.mult))
            tsq = P.inc("gpsimd")
            P.wait("tensor", txb)
            P.op("tensor", lambda T, kc=kc, w=w: T.matmul(
                C.pE[0:1, 0:w], C.ones_rhs[:], C.xbf(kc, w),
                start=(kc == 0), stop=(kc == 7)))
            P.wait("tensor", tsq)
            P.op("tensor", lambda T, kc=kc, w=w: T.matmul(
                C.pC[0:1, 0:w], C.ones_rhs[:], sq[kc % 2][:, 0:w],
                start=(kc == 0), stop=(kc == 7)))
            t_sq_prev[kc % 2] = P.inc("tensor")
        t_sums = P.tok("tensor")
        t_prev_pool = P.tok("gpsimd")
        P.wait("vector", t_sums)
        P.op("vector", lambda v, w=w: v.tensor_scalar(
            C.sr[0:1, 0:w], C.pE[0:1, 0:w], 1.0 / D, None, ALU.mult))
        P.op("vector", lambda v, w=w: v.tensor_scalar(
            C.sr[0:1, 512:512 + w], C.pC[0:1, 0:w], 1.0 / D, None, ALU.mult))
        tv = P.inc("vector")
        P.wait("vector", tv)
        P.op("vector", lambda v, w=w: v.tensor_tensor(
            C.sr[0:1, 1024:1024 + w], C.sr[0:1, 0:w], C.sr[0:1, 0:w], ALU.mult))
        tv = P.inc("vector")
        P.wait("vector", tv)
        P.op("vector", lambda v, w=w: v.tensor_tensor(
            C.sr[0:1, 512:512 + w], C.sr[0:1, 512:512 + w],
            C.sr[0:1, 1024:1024 + w], ALU.subtract))
        tv = P.inc("vector")
        P.wait("vector", tv)
        P.op("vector", lambda v, w=w: v.tensor_scalar(
            C.sr[0:1, 512:512 + w], C.sr[0:1, 512:512 + w], 0.0, None, ALU.max))
        tvar = P.inc("vector")
        P.wait("scalar", tvar)
        P.op("scalar", lambda s, w=w: s.activation(
            C.sr[0:1, 1024:1024 + w], C.sr[0:1, 512:512 + w], AF.Ln,
            bias=C.cbias[0:1, 0:1]))
        ts = P.inc("scalar")
        P.wait("scalar", ts)
        P.op("scalar", lambda s, w=w: s.activation(
            C.sr[0:1, 512:512 + w], C.sr[0:1, 1024:1024 + w], AF.Exp, scale=-0.5))
        ta = P.inc("scalar")
        P.wait("vector", ta)
        P.op("vector", lambda v, lo=lo, w=w: v.tensor_copy(
            C.lnA[0:1, lo:lo + w], C.sr[0:1, 512:512 + w]))
        P.op("vector", lambda v, w=w: v.tensor_scalar(
            C.sr[0:1, 0:w], C.sr[0:1, 0:w], -1.0, None, ALU.mult))
        tv = P.inc("vector")
        P.wait("vector", tv)
        P.op("vector", lambda v, lo=lo, w=w: v.tensor_tensor(
            C.sr[0:1, 0:w], C.sr[0:1, 0:w], C.sr[0:1, 512:512 + w], ALU.mult))
        tv = P.inc("vector")
        P.wait("vector", tv)
        P.op("vector", lambda v, lo=lo, w=w: v.tensor_copy(
            C.lnC[0:1, lo:lo + w], C.sr[0:1, 0:w]))
        tc_ = P.inc("vector")
        P.wait("tensor", tc_)
        P.wait("tensor", t_prev_rep)
        P.op("tensor", lambda T, lo=lo, w=w: T.matmul(
            C.pB[:, 0:w], C.ones_lhs[:], C.lnA[0:1, lo:lo + w], start=True, stop=True))
        P.op("tensor", lambda T, lo=lo, w=w: T.matmul(
            C.pB[:, 512:512 + w], C.ones_lhs[:], C.lnC[0:1, lo:lo + w],
            start=True, stop=True))
        tmm = P.inc("tensor")
        P.wait("vector", tmm)
        P.wait("vector", t_prev_pool)
        P.op("vector", lambda v, w=w: v.tensor_copy(arep_sb[:, 0:w], C.pB[:, 0:w]))
        P.op("vector", lambda v, w=w: v.tensor_copy(crep_sb[:, 0:w], C.pB[:, 512:512 + w]))
        t_prev_rep = P.inc("vector")
        P.wait("vector", t_prev_rep)
        toks = []
        for kc in range(8):
            P.op("vector", lambda v, kc=kc, lo=lo, w=w: v.tensor_tensor(
                C.xn1(kc, lo, w), C.xbf(kc, w), arep_sb[:, 0:w], ALU.mult))
            tv = P.inc("vector")
            P.wait("gpsimd", tv)
            P.op("gpsimd", lambda g, kc=kc, lo=lo, w=w: g.tensor_tensor(
                C.xn1(kc, lo, w), C.xn1(kc, lo, w), crep_sb[:, 0:w], ALU.add))
            toks.append(P.inc("gpsimd"))
        t_prev_xbf = toks[-1]
        t_prev_pool = toks[-1]
        t_prev_mm = P.tok("vector")
    C.t_xn1 = t_prev_xbf
    P.wait("sync", P.tok("vector"))
    C.t_offload = P.dma("sync", C.stream_dram[:, 0:8 * EXT], C.R1[:, 0:8 * EXT])


def emit_router_tc(C, t):
    P = C.P
    rsc = C.U3f[:, 512:536]
    E_ = rsc[:, 0:8]
    sel = rsc[:, 8:16]
    Wv = rsc[:, 16:24]
    t_rl_ev = C.t_rl_ev
    t_tr_prev = C.t_tr_prev
    if True:
        lo = 3 + t * 128
        P.wait("tensor", C.t_xn1)
        P.wait("tensor", t_rl_ev)
        P.wait("tensor", t_tr_prev)
        for kc in range(8):
            P.op("tensor", lambda T, kc=kc, lo=lo: T.matmul(
                C.pD[0:12, 0:128], C.wcomb[:, kc * 12:(kc + 1) * 12],
                C.xn1(kc, lo, 128), start=(kc == 0), stop=(kc == 7)))
        tmm = P.inc("tensor")
        P.wait("scalar", tmm)
        P.op("scalar", lambda s: s.activation(C.rl[0:12, 0:128], C.pD[0:12, 0:128],
                                              AF.Identity, bias=C.rbias[0:12, 0:1]))
        trl = P.inc("scalar")
        t_rl_ev = trl
        P.wait("tensor", trl)
        P.wait("tensor", t_tr_prev)
        P.op("tensor", lambda T: T.transpose(C.pD[0:128, 128:140], C.rl[0:12, 0:128],
                                             C.eye_f[0:12, 0:12]))
        ttr = P.inc("tensor")
        P.wait("vector", ttr)
        P.op("vector", lambda v: v.tensor_copy(C.rt[:, 0:12], C.pD[:, 128:140]))
        tv = P.inc("vector")
        t_tr_prev = tv
        P.wait("vector", tv)
        P.op("vector", lambda v: v.reduce_max(C.sc2[:, 8:9], C.rt[:, 8:12], axis=AX))
        tv = P.inc("vector")
        P.wait("vector", tv)
        P.op("vector", lambda v: v.tensor_scalar(C.sc2[:, 9:10], C.sc2[:, 8:9], -1.0, None, ALU.mult))
        tv = P.inc("vector")
        P.wait("scalar", tv)
        P.op("scalar", lambda s: s.activation(C.lp[:, 0:4], C.rt[:, 8:12], AF.Exp,
                                              bias=C.sc2[:, 9:10]))
        ts = P.inc("scalar")
        P.wait("vector", ts)
        P.op("vector", lambda v: v.reduce_sum(C.sc2[:, 10:11], C.lp[:, 0:4], axis=AX))
        tv = P.inc("vector")
        P.wait("vector", tv)
        P.op("vector", lambda v: v.reciprocal(C.sc2[:, 11:12], C.sc2[:, 10:11]))
        tv = P.inc("vector")
        P.wait("vector", tv)
        P.op("vector", lambda v: v.tensor_scalar(C.lp[:, 0:4], C.lp[:, 0:4],
                                                 C.sc2[:, 11:12], None, ALU.mult))
        tv = P.inc("vector")
        P.wait("vector", tv)
        P.op("vector", lambda v: v.tensor_tensor(
            C.sc1[:, 0:8].rearrange("p (a b) -> p a b", b=4),
            C.lp[:, 0:4].unsqueeze(1).broadcast_to([128, 2, 4]),
            C.lp[:, 1:5].unsqueeze(1).broadcast_to([128, 2, 4]), ALU.add))
        tv = P.inc("vector")
        P.wait("vector", tv)
        P.op("vector", lambda v: v.scalar_tensor_tensor(
            C.rt[:, 0:8], C.sc1[:, 0:8], 0.1, C.rt[:, 0:8], ALU.mult, ALU.add))
        tv = P.inc("vector")
        P.wait("vector", tv)
        P.op("vector", lambda v: v.reduce_max(C.sc2[:, 8:9], C.rt[:, 0:8], axis=AX))
        tv = P.inc("vector")
        P.wait("vector", tv)
        P.op("vector", lambda v: v.tensor_scalar(C.sc2[:, 9:10], C.sc2[:, 8:9], -1.0, None, ALU.mult))
        tv = P.inc("vector")
        P.wait("scalar", tv)
        P.op("scalar", lambda s: s.activation(E_, C.rt[:, 0:8], AF.Exp,
                                              bias=C.sc2[:, 9:10]))
        ts = P.inc("scalar")
        P.wait("vector", ts)
        P.op("vector", lambda v: v.tensor_scalar(sel, E_, 1.0, None, ALU.is_lt))
        tv = P.inc("vector")
        P.wait("vector", tv)
        P.op("vector", lambda v: v.tensor_tensor(Wv, E_, sel, ALU.mult))
        tv = P.inc("vector")
        P.wait("vector", tv)
        P.op("vector", lambda v: v.reduce_max(C.sc2[:, 8:9], Wv, axis=AX))
        tv = P.inc("vector")
        P.wait("vector", tv)
        P.op("vector", lambda v: v.tensor_scalar(C.sc2[:, 9:10], C.sc2[:, 8:9], 1.0, None, ALU.add))
        P.op("vector", lambda v: v.tensor_scalar(sel, E_, C.sc2[:, 8:9], None, ALU.is_ge))
        tv = P.inc("vector")
        P.wait("vector", tv)
        P.op("vector", lambda v: v.reciprocal(C.sc2[:, 10:11], C.sc2[:, 9:10]))
        tv = P.inc("vector")
        P.wait("vector", tv)
        P.op("vector", lambda v: v.tensor_tensor(Wv, E_, sel, ALU.mult))
        tv = P.inc("vector")
        P.wait("vector", tv)
        P.op("vector", lambda v: v.tensor_scalar(Wv, Wv, C.sc2[:, 10:11], None, ALU.mult))
        tv = P.inc("vector")
        P.wait("tensor", tv)
        P.op("tensor", lambda T: T.transpose(C.pD[0:8, 140:268],
                                             Wv, C.eye_f[:]))
        ttr = P.inc("tensor")
        P.wait("vector", ttr)
        P.op("vector", lambda v, t=t: v.tensor_copy(C.Wfmb[0:8, t * 128:(t + 1) * 128],
                                                    C.pD[0:8, 140:268]))
        tv = P.inc("vector")
        t_tr_prev = tv
    C.t_rl_ev = t_rl_ev
    C.t_tr_prev = t_tr_prev
    C.t_router = P.tok("vector")


def emit_qkv_phi_v(C):
    P = C.P
    php = C.U5[:, 0:512]
    tm1 = C.U5[:, 512:1024]
    tm2 = C.U5[:, 1024:1536]
    P.wait("gpsimd", C.t_offload)
    P.wait("vector", C.t_offload)
    t_w_used = [None, None]
    t_w_read = [None, None]
    t_phi_prev = None
    t_qh_used = [None, None]
    for pair in range(8):
        for half in range(2):
            mc = (pair % 4) * 2 + half + (8 if pair >= 4 else 0)
            P.wait("gpsimd", t_w_read[half])
            tl = P.dma("gpsimd", C.wqk_rot(half)[:],
                       C.wd["wqk_r"][:, :, mc * 128:(mc + 1) * 128])
            t_w_used[half] = tl
        for wi, (lo, w) in enumerate(WINDOWS):
            for half in range(2):
                pq = (C.pE, C.pC)[half]
                P.wait("tensor", t_w_used[half])
                P.wait("tensor", C.t_xn1)
                P.wait("tensor", t_qh_used[half])
                for kc in range(8):
                    P.op("tensor", lambda T, kc=kc, half=half, lo=lo, w=w, pq=pq:
                         T.matmul(pq[:, 0:w],
                                  C.wqk_rot(half)[:, kc * 128:(kc + 1) * 128],
                                  C.xn1(kc, lo, w), start=(kc == 0), stop=(kc == 7)))
                tmm = P.inc("tensor")
                P.wait("scalar", tmm)
                P.op("scalar", lambda sc, half=half, w=w, pq=pq: sc.copy(
                    C.qkf(half)[:, 0:w], pq[:, 0:w]))
                tev = P.inc("scalar")
                P.wait("tensor", tev)
                P.wait("tensor", t_phi_prev)
                ph = (C.pB[:, 0:512], C.pB[:, 512:1024])[half]
                P.op("tensor", lambda T, half=half, w=w, ph=ph: T.matmul(
                    ph[0:64, 0:w], C.proj[:], C.qkf(half)[:, 0:w],
                    start=True, stop=True))
                t_qh_used[half] = P.inc("tensor")
            tph = P.tok("tensor")
            P.wait("vector", tph)
            P.op("vector", lambda v, w=w: v.tensor_copy(php[0:64, 0:w], C.pB[0:64, 0:w]))
            P.op("vector", lambda v, w=w: v.tensor_copy(php[64:128, 0:w],
                                                        C.pB[0:64, 512:512 + w]))
            tev = P.inc("vector")
            t_phi_prev = tev
            P.wait("vector", tev)
            P.op("vector", lambda v, w=w: v.tensor_scalar(tm1[:, 0:w], php[:, 0:w],
                                                          0.0, None, ALU.min))
            tv = P.inc("vector")
            P.wait("scalar", tv)
            P.op("scalar", lambda s, w=w: s.activation(tm2[:, 0:w], tm1[:, 0:w], AF.Exp))
            ts = P.inc("scalar")
            P.wait("vector", ts)
            kp = pair % 4
            dst = (C.qphi, C.kphi)[pair // 4]
            P.op("vector", lambda v, kp=kp, dst=dst, lo=lo, w=w: v.scalar_tensor_tensor(
                dst(kp, lo, w), php[:, 0:w], 0.0, tm2[:, 0:w], ALU.max, ALU.add))
            tv = P.inc("vector")
            t_phi_prev = tv
        t_w_read[0] = ("tensor", P.cnt["tensor"])
        t_w_read[1] = t_w_read[0]
        emit_router_tc(C, 2 * pair)
        emit_router_tc(C, 2 * pair + 1)
    C.t_qkv_done = ("tensor", P.cnt["tensor"])
    if True:
        pass
    P.wait("vector", P.tok("vector"))
    for kp in range(4):
        P.op("vector", lambda v, kp=kp: v.tensor_scalar(
            C.kphi(kp, 0, 3), C.kphi(kp, 0, 3), C.flags[:, 0:1], None, ALU.mult))
        P.op("vector", lambda v, kp=kp: v.tensor_scalar(
            C.kphi(kp, 2051, 3), C.kphi(kp, 2051, 3), C.flags[:, 1:2], None, ALU.mult))
    C.t_phi = P.inc("vector")
    C.t_v_ready = {}
    C.t_vslot_used = [C.t_offload, C.t_offload, C.t_offload]
    for sl in range(3):
        P.wait("vector", C.t_offload)
        P.op("vector", lambda v, sl=sl: v.memset(
            C.Vsb(sl).rearrange("p (h e) -> p h e", e=129)[:, :, 128:129], 1.0))
    C.t_vones = P.inc("vector")


def emit_v_tile(C, t):
    P = C.P
    tw = twidth(t)
    sl = t % 3
    P.wait("tensor", C.t_xn1)
    tev = None
    for half in range(2):
        P.wait("tensor", C.t_vslot_used[sl])
        P.wait("tensor", getattr(C, "t_pd_ev", None))
        P.wait("tensor", getattr(C, "t_router", None))
        for kc in range(8):
            P.op("tensor", lambda T, kc=kc, half=half, t=t, tw=tw: T.matmul(
                C.pD[0:tw, 0:512], C.xn1(kc, t * 128, tw),
                C.wv(kc)[:, half * 512:half * 512 + 512],
                start=(kc == 0), stop=(kc == 7)))
        tmm = P.inc("tensor")
        P.wait("vector", tmm)
        P.wait("vector", C.t_vones)
        P.op("vector", lambda v, t=t, tw=tw, sl=sl, half=half: v.tensor_copy(
            C.Vsb(sl)[0:tw, half * 516:half * 516 + 516]
            .rearrange("p (h e) -> p h e", e=129)[:, :, 0:128],
            C.pD[0:tw, 0:512].rearrange("p (h e) -> p h e", e=128)))
        tev = P.inc("vector")
        C.t_pd_ev = tev
    C.t_v_ready[t] = tev
    return tev


def emit_block1_attn(C, with_b2=True):
    P = C.P
    P.wait("gpsimd", C.t_offload)
    for kc in range(8):
        P.dma("gpsimd", C.wv(kc)[:], C.wd["wv_r"][:, kc:kc + 1, :])
    C.t_wv = P.qtok("gpsimd")
    C.wout1_full = lambda dch: C.R3[:, dch * 1024:(dch + 1) * 1024]
    P.wait("gpsimd", C.t_b0_end)
    for dch in range(8):
        P.dma("gpsimd", C.wout1_full(dch),
              C.wd["wout1_r"][:, :, dch * 128:(dch + 1) * 128])
    C.t_wo_full = P.qtok("gpsimd")
    P.wait("tensor", C.t_wv)
    emit_v_tile(C, 0)
    emit_v_tile(C, 1)
    t_st_prev = [None, None]
    t_st_rd = [None, None]
    t_atok_prev = None
    t_bank_ev = None
    t_pf_prev = None
    t_afm_used = None
    t_wout_used = [None, None, None, None]
    t_wo_dma = [None, None]
    for qt in range(16):
        if qt + 1 <= 16:
            emit_v_tile(C, qt + 1)
        for hb4 in range(2):
            P.wait("tensor", C.t_phi)
            P.wait("tensor", t_st_prev[hb4 % 2])
            if hb4 == 0:
                for _b in range(4):
                    P.wait("tensor", t_wout_used[_b])
            for hi in range(4):
                h = hb4 * 4 + hi
                kp, hh = h // 2, h % 2
                P.op("tensor", lambda T, kp=kp, hh=hh, qt=qt, hi=hi: T.matmul(
                    C.pE[:, hi * 128:(hi + 1) * 128],
                    C.kphi(kp, qt * 128, 128)[64 * hh:64 * hh + 64, :],
                    C.qphi(kp, qt * 128 + 3, 128)[64 * hh:64 * hh + 64, :],
                    start=True, stop=True))
                P.op("tensor", lambda T, kp=kp, hh=hh, qt=qt, hi=hi: T.matmul(
                    C.pC[0:6, hi * 128:(hi + 1) * 128],
                    C.kphi(kp, qt * 128 + 128, 6)[64 * hh:64 * hh + 64, :],
                    C.qphi(kp, qt * 128 + 3, 128)[64 * hh:64 * hh + 64, :],
                    start=True, stop=True))
            tst = P.inc("tensor")
            P.wait("vector", tst)
            P.wait("vector", t_st_rd[hb4 % 2])
            P.op("vector", lambda v: v.tensor_tensor(
                C.U5[:, 0:512].rearrange("p (a b) -> p a b", b=128),
                C.pE[:, 0:512].rearrange("p (a b) -> p a b", b=128),
                C.bandA[:].unsqueeze(1).broadcast_to([128, 4, 128]), ALU.mult))
            P.op("vector", lambda v: v.tensor_tensor(
                C.U5[0:6, 512:1024].rearrange("p (a b) -> p a b", b=128),
                C.pC[0:6, 0:512].rearrange("p (a b) -> p a b", b=128),
                C.bandB[:].unsqueeze(1).broadcast_to([6, 4, 128]), ALU.mult))
            t_st_prev[hb4 % 2] = P.inc("vector")
            P.wait("tensor", t_st_prev[hb4 % 2])
            if hb4 == 0:
                P.wait("tensor", t_bank_ev)
            P.wait("tensor", C.t_v_ready[qt])
            P.wait("tensor", C.t_v_ready[qt + 1])
            for hi in range(4):
                h = hb4 * 4 + hi
                bank = (C.pA, C.pA, C.pB, C.pB)[h // 2]
                off = (0, 512)[(h // 2) % 2] + (h % 2) * 129
                P.op("tensor", lambda T, h=h, qt=qt, bank=bank, off=off, hi=hi:
                     T.matmul(bank[:, off:off + 129], C.stA(hi),
                              C.Vsb(qt % 3)[:, h * 129:(h + 1) * 129],
                              start=True, stop=False))
                P.op("tensor", lambda T, h=h, qt=qt, bank=bank, off=off, hi=hi:
                     T.matmul(bank[:, off:off + 129], C.stB(hi),
                              C.Vsb((qt + 1) % 3)[0:6, h * 129:(h + 1) * 129],
                              start=False, stop=True))
            tat = P.inc("tensor")
            t_st_rd[hb4 % 2] = tat
        C.t_vslot_used[qt % 3] = tat
        C.t_vslot_used[(qt + 1) % 3] = tat
        t_ev = []
        for g in range(4):
            bank = (C.pA, C.pA, C.pB, C.pB)[g]
            off = (0, 512)[g % 2]
            pv = bank[:, off:off + 258].rearrange("p (h e) -> p h e", e=129)
            P.wait("vector", tat)
            P.wait("vector", t_atok_prev)
            P.op("vector", lambda v, pv=pv, g=g: v.tensor_scalar(
                C.sc2[:, 2 * g:2 * g + 2].unsqueeze(2), pv[:, :, 128:129],
                1e-8, None, ALU.add))
            tv = P.inc("vector")
            P.wait("vector", tv)
            P.op("vector", lambda v, g=g: v.reciprocal(
                C.qinv[:, 2 * g:2 * g + 2], C.sc2[:, 2 * g:2 * g + 2]))
            tv = P.inc("vector")
            P.wait("vector", tv)
            P.op("vector", lambda v, pv=pv, g=g: v.tensor_tensor(
                C.atok1[:, g * 256:(g + 1) * 256].rearrange("p (h e) -> p h e", e=128),
                pv[:, :, 0:128],
                C.qinv[:, 2 * g:2 * g + 2].unsqueeze(2).broadcast_to([128, 2, 128]),
                ALU.mult))
            t_ev.append(P.inc("vector"))
        t_bank_ev = t_ev[-1]
        t_atok_prev = None
        P.wait("tensor", t_ev[-1])
        P.wait("tensor", t_pf_prev)
        for kc in range(8):
            P.op("tensor", lambda T, kc=kc: T.transpose(
                C.pF[:, kc * 128:(kc + 1) * 128],
                C.atok1[:, kc * 128:(kc + 1) * 128], C.eye_b[:]))
        ttr = P.inc("tensor")
        t_atok_prev = ttr
        P.wait("vector", ttr)
        P.wait("vector", t_afm_used)
        for kc in range(8):
            P.op("vector", lambda v, kc=kc, qt=qt: v.tensor_copy(
                C.afm1(kc, 512)[:, (qt % 4) * 128:(qt % 4) * 128 + 128],
                C.pF[:, kc * 128:(kc + 1) * 128]))
        t_pf_prev = P.inc("vector")
        if qt % 4 == 3:
            g4 = qt // 4
            lo = 3 + g4 * 512
            P.wait("tensor", t_pf_prev)
            for dch in range(8):
                buf = dch % 4
                P.wait("tensor", C.t_wo_full)
                P.wait("tensor", t_wout_used[(dch + 2) % 4])
                po = (C.pE, C.pC)[dch % 2]
                for kc in range(8):
                    P.op("tensor", lambda T, kc=kc, dch=dch, po=po, wb=with_b2:
                         T.matmul(po[:, 0:512],
                                  C.wout1_full(dch)[:, kc * 128:(kc + 1) * 128],
                                  C.afm1(kc, 512), start=(kc == 0),
                                  stop=(kc == 7 and not wb)))
                if with_b2:
                    P.wait("tensor", C.t_router)
                    P.op("tensor", lambda T, dch=dch, g4=g4, po=po: T.matmul(
                        po[:, 0:512], C.b2sb[0:8, dch * 128:(dch + 1) * 128],
                        C.Wfmb[0:8, g4 * 512:(g4 + 1) * 512],
                        start=False, stop=True))
                tmm = P.inc("tensor")
                P.wait("vector", tmm)
                P.wait("vector", t_wo_dma[dch % 2])
                P.op("vector", lambda v, dch=dch, po=po: v.tensor_scalar(
                    C.t_sb1(dch % 2), po[:, 0:512], C.bout1[:, dch:dch + 1],
                    None, ALU.add))
                tev = P.inc("vector")
                t_wout_used[buf] = tev
                P.wait("gpsimd", tev)
                tD = P.dma("gpsimd",
                           C.stream_dram[:, dch * EXT + lo: dch * EXT + lo + 512],
                           C.t_sb1(dch % 2), accum_op=ALU.add)
                t_wo_dma[dch % 2] = tD
            t_afm_used = P.tok("tensor")
    C.t_attn1 = P.qtok("gpsimd")


def emit_block1(C, with_b2=True):
    C.t_b0_end = ("tensor", C.P.cnt["tensor"])
    emit_ln1(C)
    C.t_rl_ev = None
    C.t_tr_prev = None
    C.P.op("vector", lambda v: v.memset(C.lp[:, 4:5], 0.0))
    C.P.inc("vector")
    emit_qkv_phi_v(C)
    emit_block1_attn(C, with_b2=with_b2)
    C.t_moe_done = C.t_attn1


def emit_w2_convert_step(C, i):
    P = C.P
    e, j = i // 16, i % 16
    if j < 8:
        P.dma("gpsimd", C.w2bf[e, j],
              C.wd["w2"][e, :, j * 128:(j + 1) * 128]
              .rearrange("(c p) m -> p c m", p=128))
    else:
        P.dma("gpsimd", C.w1bf[e, j - 8],
              C.wd["w1"][e, (j - 8) * 128:(j - 7) * 128, :])
    C.t_conv[e] = P.qtok("gpsimd")


def emit_moe(C):
    P = C.P
    t_blk1_pe = ("tensor", P.cnt["tensor"])
    t_blk1_ve = ("vector", P.cnt["vector"])
    P.wait("gpsimd", t_blk1_pe)
    P.wait("gpsimd", t_blk1_ve)
    P.wait("gpsimd", C.t_attn1)
    P.wait("sync", t_blk1_pe)
    P.wait("sync", t_blk1_ve)
    P.wait("sync", C.t_attn1)

    t_w1_read = None
    t_w2_used = [None, None]
    t_h_used = {}
    t_hps = [None, None, None, None]
    t_out_ev = [None, None]
    t_acc_dma = [None, None]
    t_wrep_used = [None, None]
    t_wrep_dma = [C.t_attn1, C.t_attn1]
    P.wait("gpsimd", C.t_conv[0])
    for kc in range(8):
        P.dma("gpsimd", C.w1sb(kc)[:], C.w1bf[0, kc])
    t_w1 = P.qtok("gpsimd")
    for e in range(8):
        for tt in range(4):
            lo = 3 + tt * 512
            wr = C.w_rep(tt % 2)
            P.wait("tensor", C.t_router)
            P.wait("tensor", t_wrep_used[tt % 2])
            P.wait("tensor", getattr(C, "t_wrep_copy", None))
            P.op("tensor", lambda T, e=e, tt=tt: T.matmul(
                C.pD[:, 0:512], C.sel8[:, e * 128:(e + 1) * 128],
                C.Wfmb[0:8, tt * 512:(tt + 1) * 512], start=True, stop=True))
            tmm = P.inc("tensor")
            P.wait("vector", tmm)
            P.wait("vector", t_wrep_dma[tt % 2])
            P.op("vector", lambda v, wr=wr: v.tensor_copy(wr, C.pD[:, 0:512]))
            t_wrep = P.inc("vector")
            C.t_wrep_copy = t_wrep
            t_h_evac = {}
            for hch in range(32):
                hb = hch % 4
                pa = (C.pA[:, 0:512], C.pA[:, 512:1024], C.pC[:, 0:512],
                      C.pE[:, 0:512])[hb]
                P.wait("tensor", t_w1)
                P.wait("tensor", C.t_xn1)
                P.wait("tensor", t_hps[hb])
                for kc in range(8):
                    P.op("tensor", lambda T, kc=kc, hch=hch, lo=lo, pa=pa: T.matmul(
                        pa, C.w1sb(kc)[:, hch * 128:(hch + 1) * 128],
                        C.xn1(kc, lo, 512), start=(kc == 0), stop=(kc == 7)))
                tmm = P.inc("tensor")
                P.wait("scalar", tmm)
                P.wait("scalar", t_h_used.get(hch))
                P.op("scalar", lambda s, e=e, hch=hch, pa=pa: s.activation(
                    C.h_sb(hch, 512), pa, AF.Gelu,
                    bias=C.b1sb[:, e * 32 + hch: e * 32 + hch + 1]))
                t_hps[hb] = P.inc("scalar")
                t_h_evac[hch] = t_hps[hb]
            if tt == 1 and e + 3 < 8:
                for j in range(16):
                    emit_w2_convert_step(C, (e + 3) * 16 + j)
            if tt == 3:
                t_w1_read = ("tensor", P.cnt["tensor"])
                if e + 1 < 8:
                    P.wait("gpsimd", t_w1_read)
                    P.wait("gpsimd", C.t_conv[e + 1])
                    for kc in range(8):
                        P.dma("gpsimd", C.w1sb(kc)[:], C.w1bf[e + 1, kc])
                    t_w1 = P.qtok("gpsimd")
            for dch in range(8):
                db = dch % 2
                P.wait("sync", t_w2_used[db])
                P.wait("sync", C.t_conv[e])
                P.dma("sync", C.w2buf[db][:], C.w2bf[e, dch])
                tw2 = P.qtok("sync")
                pb = C.pB[:, db * 512:db * 512 + 512]
                P.wait("tensor", tw2)
                P.wait("tensor", t_out_ev[db])
                for hch in range(32):
                    P.wait("tensor", t_h_evac[hch])
                    P.op("tensor", lambda T, hch=hch, db=db, pb=pb: T.matmul(
                        pb, C.w2buf[db][:, hch * 128:(hch + 1) * 128],
                        C.h_sb(hch, 512), start=(hch == 0), stop=(hch == 31)))
                tmm = P.inc("tensor")
                t_w2_used[db] = tmm
                P.wait("vector", tmm)
                P.wait("vector", t_wrep)
                P.wait("vector", t_acc_dma[db])
                P.op("vector", lambda v, db=db, pb=pb, wr=wr: v.tensor_tensor(
                    C.t_sb(db), pb, wr, ALU.mult))
                tev = P.inc("vector")
                t_out_ev[db] = tev
                P.wait("gpsimd", tev)
                P.dma("gpsimd",
                      C.stream_dram[:, dch * EXT + lo: dch * EXT + lo + 512],
                      C.t_sb(db), accum_op=ALU.add)
                t_acc_dma[db] = P.qtok("gpsimd")
            for hch in range(32):
                t_h_used[hch] = ("tensor", P.cnt["tensor"])
            t_wrep_used[tt % 2] = ("tensor", P.cnt["tensor"])
            t_wrep_dma[tt % 2] = ("vector", P.cnt["vector"])
    C.t_moe_done = P.qtok("gpsimd")


def kernel(**inputs):
    C = build(phases=("b0", "b1", "moe"))
    feeds = [load_feeds(C, inputs, c) for c in range(8)]
    res = run_bass_kernel_spmd(C.nc, feeds, core_ids=list(range(8)))
    out = np.zeros((4, 4096, 1024), np.float32)
    for c in range(8):
        b, s = c // 2, c % 2
        out[b, s * OWN:(s + 1) * OWN] = res.results[c]["y"].T
    return out
